# version 18
# baseline (speedup 1.0000x reference)
import numpy as np
import ml_dtypes
from contextlib import ExitStack

import concourse.bass as bass
import concourse.mybir as mybir
from concourse.bass_utils import run_bass_kernel_spmd

F32 = mybir.dt.float32
BF16 = mybir.dt.bfloat16
I32 = mybir.dt.int32
AF = mybir.ActivationFunctionType
ALU = mybir.AluOpType

D = 1024
L = 4096
DIN = 2048
NH = 16
CDIM = 2176
DIP = 4240
G = 256
NG = L // G
Q = 128
CPG = G // Q
NEG = -1.0e30
EPS = 1e-5
BF = ml_dtypes.bfloat16


def build_consts():
    c = {}
    c["c_i128"] = np.eye(128, dtype=BF)
    c["c_i64"] = np.eye(64, dtype=BF)
    tt = np.arange(128)[None, :] - np.arange(128)[:, None]
    mfw = np.where(tt < 0, NEG, 0.0).astype(np.float32)
    mbw = np.where(tt > 0, NEG, 0.0).astype(np.float32)
    c["c_maskfw"] = np.repeat(mfw[:, None, :], 4, axis=1).astype(BF)
    c["c_maskbw"] = np.repeat(mbw[:, None, :], 4, axis=1).astype(BF)
    ind = np.zeros((48, NH, 128), dtype=BF)
    for base in (0, 16, 32):
        for h in range(NH):
            ind[base + h, h, :] = 1.0
    c["c_ind3"] = ind
    c["c_ones3"] = np.ones((35, 128), dtype=BF)
    c["c_ones128b"] = np.ones((128, 1), dtype=BF)
    c["c_ones1f"] = np.ones((1, 128), dtype=np.float32)
    return c


class P:
    def __init__(self, handles):
        self.h = handles
        self.cnt = {k: 0 for k in handles}
        self.eng_name = None
        self.eng = None

    def begin(self, name, eng):
        self.eng_name = name
        self.eng = eng
        for k in self.cnt:
            self.cnt[k] = 0

    def mark(self, sem):
        return self.cnt[sem]

    def op(self, eng_name, fn, wait=(), inc=None, dma=False):
        if self.eng_name == eng_name:
            for w in wait:
                if isinstance(w, str):
                    s, v = w, self.cnt[w]
                else:
                    s, v = w
                    if v is None:
                        v = self.cnt[s]
                self.eng.wait_ge(self.h[s], v)
            if dma and inc is not None:
                self.eng.wait_ge(self.h[inc], self.cnt[inc])
            instr = fn(self.eng)
            if inc is not None:
                instr.then_inc(self.h[inc], 16 if dma else 1)
        if inc is not None:
            self.cnt[inc] += 16 if dma else 1


SEM_NAMES = ["qs", "qc", "qg", "qw", "qu", "qx", "qy", "qd", "qo", "tv", "ts", "tp", "tg"]


def make_program(debug=0):
    state = {}

    def declare(nc):
        spk = "ExternalOutput" if debug else "Internal"
        dr = {}
        dr["u"] = nc.dram_tensor("u", [L, D], F32, kind="ExternalInput").ap()
        dr["ipw"] = nc.dram_tensor("ipw", [DIP, D], F32,
                                   kind="ExternalInput").ap()
        for nm, shp in [("cw", [CDIM, 4]), ("cb", [CDIM]),
                        ("cwb", [CDIM, 4]), ("cbb", [CDIM]),
                        ("dtb", [16]), ("dtbb", [16]),
                        ("alog", [16]), ("ablog", [16]),
                        ("dvec", [16]), ("dbvec", [16]),
                        ("normw", [DIN]), ("opw", [D, DIN]),
                        ("fwt", [D, D]), ("fb", [D])]:
            dr[nm] = nc.dram_tensor(nm, shp, F32, kind="ExternalInput").ap()
        for nm, shp, dt in [("c_i128", [128, 128], BF16),
                            ("c_i64", [64, 64], BF16),
                            ("c_maskfw", [128, 4, 128], BF16),
                            ("c_maskbw", [128, 4, 128], BF16),
                            ("c_ind3", [48, NH, 128], BF16),
                            ("c_ones3", [35, 128], BF16),
                            ("c_ones128b", [128, 1], BF16),
                            ("c_ones1f", [1, 128], F32)]:
            dr[nm] = nc.dram_tensor(nm, shp, dt, kind="ExternalInput").ap()
        dr["out"] = nc.dram_tensor("out", [L, D], F32, kind="ExternalOutput").ap()
        if debug:
            dr["dbg"] = nc.dram_tensor("dbg", [8, 128, 4096], BF16,
                                       kind="ExternalOutput").ap()
        dr["wbfA"] = nc.dram_tensor("wbfA", [2192, D], BF16, kind="Internal").ap()
        dr["wbfZ"] = nc.dram_tensor("wbfZ", [DIN, D], BF16, kind="Internal").ap()
        dr["wbfO"] = nc.dram_tensor("wbfO", [D, DIN], BF16, kind="Internal").ap()
        dr["wbfF"] = nc.dram_tensor("wbfF", [D, D], BF16, kind="Internal").ap()
        dr["sp_x"] = nc.dram_tensor("sp_x", [NG, 128, NH * G], BF16,
                                    kind=spk).ap()
        dr["sp_bc"] = nc.dram_tensor("sp_bc", [NG, 64, 2 * G], BF16,
                                     kind=spk).ap()
        dr["sp_dt"] = nc.dram_tensor("sp_dt", [NG, 16, G], F32,
                                     kind=spk).ap()
        dr["sp_y"] = nc.dram_tensor("sp_y", [NG, 128, NH * G], BF16,
                                    kind=spk).ap()
        dr["s_scr"] = nc.dram_tensor("s_scr", [16, 6, G], BF16,
                                     kind="Internal").ap()

        sbufs = {}

        def A(name, shape, dt):
            h = nc.sbuf_tensor(name, shape, dt).__enter__()
            sbufs[name] = h

        def PS(name, shape, dt):
            h = nc.psum_tensor(name, shape, dt).__enter__()
            sbufs[name] = h

        A("warena", [128, 40960], BF16)
        A("wstage", [128, 2, D], BF16)
        A("stgX", [128, NH, G], BF16)
        A("stgBC", [128, 2, G], BF16)
        A("haloX", [128, NH, 8], BF16)
        A("haloBC", [128, 2, 8], BF16)
        A("convX", [128, NH, G], BF16)
        A("convBC", [128, 2, G], BF16)
        A("u2", [128, D], F32)
        A("ur", [128, D], F32)
        A("um", [128, D], F32)
        A("mv", [128, 8], F32)
        A("bns", [128, 12], F32)
        A("xnT", [128, 8, G], BF16)
        A("xdT", [128, 8, G], BF16)
        A("xn2", [128, 2, D], BF16)
        A("xd2", [128, 2, D], BF16)
        A("Xtm", [128, NH * CPG, 128], BF16)
        A("Btm", [128, CPG, 64], BF16)
        A("dtraw", [16, G], F32)
        A("dt_t", [16, G], F32)
        A("sp1", [16, G], F32)
        A("sp2", [16, G], F32)
        A("a_hm", [16, G], F32)
        A("s_hm", [16, G], F32)
        A("lndt", [16, G], F32)
        A("ones16", [16, Q], F32)
        A("spl6", [16, 6, G], BF16)
        A("b3c", [48, G], BF16)
        A("sflat", [35, 16, 128], BF16)
        A("cbt", [128, 128], BF16)
        A("mtq", [128, 1024], BF16)
        A("gq", [64, 1024], BF16)
        A("gcq", [64, 1024], BF16)
        A("bwq", [128, 512], BF16)
        A("diagq", [64, 512], BF16)
        A("hT0", [64, NH * 128], BF16)
        A("ybw", [128, NH, G], BF16)
        A("siluz", [128, NH, G], BF16)
        A("gff", [128, 2, D], BF16)
        A("ostg", [128, 512], F32)
        A("rrow", [1, 2, G], BF16)
        A("rbc", [128, 2, G], BF16)
        A("qv8", [128, 512], F32)
        A("qau", [128, 512], BF16)
        A("qm1", [128, 512], BF16)
        A("qm2", [128, 512], BF16)
        A("qsc", [128, 16], F32)
        A("qsc2", [128, 16], F32)
        A("qrcp", [128, 16], F32)
        A("i128", [128, 128], BF16)
        A("i64", [64, 64], BF16)
        A("maskst", [128, 4, 128], BF16)
        A("ind3", [48, NH, 128], BF16)
        A("ones3", [35, 128], BF16)
        A("ones128b", [128, 1], BF16)
        A("ones1f", [1, 128], F32)
        A("negA", [16, 1], F32)
        A("dtbias", [16, 1], F32)
        A("ddiag", [128, NH * 128], BF16)
        A("dbc", [128, 16], F32)
        A("cwX", [128, NH, 4], F32)
        A("cbX", [128, NH], F32)
        A("cwB", [64, 4], F32)
        A("cwBC2", [64, 2, 4], F32)
        A("cbB", [64, 1], F32)
        A("cwC", [64, 4], F32)
        A("cbC", [64, 1], F32)
        A("vrow16", [1, 16], F32)
        A("nwtile", [128, 16], F32)
        A("epsb", [128, 1], F32)
        A("oneb", [128, 1], F32)
        sbufs["ystg"] = sbufs["convX"]
        sbufs["t16a"] = sbufs["sp1"]
        sbufs["t16b"] = sbufs["sp2"]
        sbufs["biasf"] = sbufs["lndt"]
        PS("ps", [128, 4096], F32)
        return dr, sbufs

    def prog(nc, p):
        shared = {}
        if "dr" not in state:
            state["dr"], state["sb"] = declare(nc)
            return
        dr = state["dr"]
        sb = state["sb"]
        ps = sb["ps"]

        PB_MM = [0, 512]
        PB_D = [1024, 1536]
        PB_Y = [2048, 2560]
        PB_S = 3072
        PB_X = 3584

        def mm(out_ap, lhsT, rhs, start, stop, wait=(), inc="tp"):
            p.op("tensor",
                 lambda e: e.matmul(out_ap, lhsT, rhs, start=start, stop=stop),
                 wait=wait, inc=inc)

        def ld(dst, src, wait=(), inc="qc", q="scalar"):
            p.op(q, lambda e: e.dma_start(out=dst, in_=src),
                 wait=wait, inc=inc, dma=True)

        def load_consts():
            ld(sb["i128"][:], dr["c_i128"], inc="qw")
            ld(sb["i64"][:], dr["c_i64"], inc="qw")
            ld(sb["ind3"][:], dr["c_ind3"], inc="qw")
            ld(sb["ones3"][:], dr["c_ones3"], inc="qw")
            ld(sb["ones128b"][:], dr["c_ones128b"], inc="qw")
            ld(sb["ones1f"][:], dr["c_ones1f"], inc="qw")

        def wT_build(src_rows_ap, nrows, wbf, dst_ap, xw=(), bounce=True,
                     bb=None):
            full = (nrows // 128) * 128
            if not bounce:
                sv = p.mark("qw")
                p.op("sync",
                     lambda e: e.dma_start(out=dst_ap, in_=wbf[0:nrows, :],
                                           transpose=True),
                     wait=[("qw", sv)] + list(xw), inc="qs", dma=True)
                return
            nt = nrows // 128
            rem = nrows - full
            bbt = 2 if bb is None else bb.shape[1]
            for c0 in range(0, nt, bbt):
                cn = min(bbt, nt - c0)
                buf = (sb["wstage"] if bb is None else bb)[:, 0:cn, :]
                wv0 = p.mark("qw")
                p.op("gpsimd",
                     lambda e, c0=c0, cn=cn, buf=buf: e.dma_start(
                         out=buf,
                         in_=src_rows_ap[c0 * 128:(c0 + cn) * 128, :]
                         .rearrange("(t p) c -> p t c", p=128)),
                     wait=[("qw", wv0)], inc="qg", dma=True)
                wv = p.mark("qg")
                p.op("scalar",
                     lambda e, c0=c0, cn=cn, buf=buf: e.dma_start(
                         out=wbf[c0 * 128:(c0 + cn) * 128, :]
                         .rearrange("(t p) c -> p t c", p=128),
                         in_=buf),
                     wait=[("qg", wv)], inc="qw", dma=True)
            if rem:
                wv0 = p.mark("qw")
                p.op("gpsimd",
                     lambda e: e.dma_start(out=sb["wstage"][0:rem, 0, :],
                                           in_=src_rows_ap[full:nrows, :]),
                     wait=[("qw", wv0)], inc="qg", dma=True)
                wv = p.mark("qg")
                p.op("scalar",
                     lambda e: e.dma_start(out=wbf[full:nrows, :],
                                           in_=sb["wstage"][0:rem, 0, :]),
                     wait=[("qg", wv)], inc="qw", dma=True)
            if dst_ap is None:
                return
            sv = p.mark("qw")
            p.op("sync",
                 lambda e: e.dma_start(out=dst_ap, in_=wbf[0:nrows, :],
                                       transpose=True),
                 wait=[("qw", sv)] + list(xw), inc="qs", dma=True)

        def dir_params(pref):
            alog = dr["alog"] if pref == "fw" else dr["ablog"]
            dtbv = dr["dtb"] if pref == "fw" else dr["dtbb"]
            dvv = dr["dvec"] if pref == "fw" else dr["dbvec"]
            cwv = dr["cw"] if pref == "fw" else dr["cwb"]
            cbv = dr["cb"] if pref == "fw" else dr["cbb"]
            wv = [("ts", None), ("tv", None)]
            ld(sb["negA"][:], alog.unsqueeze(1), wait=wv, inc="qw")
            ld(sb["dtbias"][:], dtbv.unsqueeze(1), inc="qw")
            ld(sb["vrow16"][:], dvv.unsqueeze(0), inc="qw")
            ld(sb["cwX"][:], cwv[0:2048, :].rearrange("(t p) k -> p t k", p=128),
               inc="qw")
            p.op("scalar", lambda e: e.dma_start(
                out=sb["cbX"][:],
                in_=cbv[0:2048].rearrange("(t p) -> p t", p=128),
                allow_slow_non_contiguous=True), inc="qw", dma=True)
            ld(sb["cwB"][:], cwv[2048:2112, :], inc="qw")
            ld(sb["cbB"][:], cbv[2048:2112].unsqueeze(1), inc="qw")
            ld(sb["cwC"][:], cwv[2112:2176, :], inc="qw")
            ld(sb["cbC"][:], cbv[2112:2176].unsqueeze(1), inc="qw")
            ld(sb["maskst"][:], dr["c_maskfw" if pref == "fw" else "c_maskbw"],
               inc="qw")
            ld(sb["cwBC2"][0:64, 0, :], cwv[2048:2112, :], inc="qw")
            ld(sb["cwBC2"][0:64, 1, :], cwv[2112:2176, :], inc="qw")
            lv = p.mark("qw")
            p.op("scalar",
                 lambda e: e.activation(sb["negA"][:], sb["negA"][:], AF.Exp),
                 wait=[("qw", lv)], inc="ts")
            p.op("vector",
                 lambda e: e.tensor_scalar(sb["negA"][:], sb["negA"][:],
                                           -1.0, None, op0=ALU.mult),
                 wait=["ts"], inc="tv")
            mm(ps[:, PB_X:PB_X + 16], sb["ones1f"][:], sb["vrow16"][:],
               True, True, wait=[("qw", lv), "tv", "ts"])
            p.op("vector",
                 lambda e: e.tensor_copy(sb["dbc"][:], ps[:, PB_X:PB_X + 16]),
                 wait=["tp"], inc="tv")
            p.op("vector",
                 lambda e: e.tensor_tensor(
                     sb["ddiag"][:].rearrange("p (h q) -> p h q", h=NH),
                     sb["i128"][:].unsqueeze(1).broadcast_to(
                         [128, NH, 128]),
                     sb["dbc"][:].unsqueeze(2).broadcast_to(
                         [128, NH, 128]),
                     op=ALU.mult),
                 wait=["tv"], inc="tv")

        def ln_tile(s_, want_um):
            uap = sb["u2"][:]
            xnt = sb["xn2"][:, s_, :]
            p.op("vector", lambda e: e.bn_stats(sb["bns"][:, 0:6], uap[:, 0:512]),
                 wait=["qu", "tv"], inc="tv")
            p.op("vector", lambda e: e.bn_stats(sb["bns"][:, 6:12], uap[:, 512:1024]),
                 wait=["tv"], inc="tv")
            p.op("vector", lambda e: e.bn_aggr(sb["mv"][:, 0:2], sb["bns"][:]),
                 wait=["tv"], inc="tv")
            mvv = p.mark("tv")
            p.op("scalar",
                 lambda e: e.activation(sb["mv"][:, 2:3], sb["mv"][:, 1:2],
                                        AF.Ln, bias=sb["epsb"][:, 0:1]),
                 wait=[("tv", mvv), "ts"], inc="ts")
            p.op("scalar",
                 lambda e: e.activation(sb["mv"][:, 3:4], sb["mv"][:, 2:3],
                                        AF.Exp, scale=-0.5),
                 wait=["ts"], inc="ts")
            rsv = p.mark("ts")
            if want_um:
                p.op("vector",
                     lambda e: e.tensor_scalar(sb["um"][:], uap,
                                               sb["mv"][:, 0:1], None,
                                               op0=ALU.subtract),
                     wait=[("tv", mvv)], inc="tv")
                p.op("vector",
                     lambda e: e.tensor_scalar(xnt, sb["um"][:],
                                               sb["mv"][:, 3:4], None,
                                               op0=ALU.mult),
                     wait=[("ts", rsv), "tv", "qs"], inc="tv")
            else:
                p.op("vector",
                     lambda e: e.tensor_scalar(xnt, uap,
                                               sb["mv"][:, 0:1],
                                               sb["mv"][:, 3:4],
                                               op0=ALU.subtract, op1=ALU.mult),
                     wait=[("ts", rsv), ("tv", mvv), "qs"], inc="tv")

        def conv_dir(dirn, rdy, tmk=None, halo_fn=None):
            stX, stBC = sb["stgX"], sb["stgBC"]
            cvX, cvBC = sb["convX"], sb["convBC"]
            scr = sb["qv8"][:].bitcast(BF16)
            n_main = G - 3

            def sls(tap):
                if dirn == "fw":
                    return (slice(3, G), slice(tap, tap + n_main),
                            slice(0, 3), slice(tap, tap + 3))
                return (slice(0, G - 3), slice(3 - tap, 3 - tap + n_main),
                        slice(G - 3, G), slice(3 - tap, 3 - tap + 3))
            osl0, isl0, esl0, hsl0 = sls(0)

            hset = slice(0, 3) if dirn == "bw" else slice(3, 6)
            hsrc = slice(G - 3, G) if dirn == "bw" else slice(0, 3)

            def tap0_tile(i, w1):
                p.op("vector",
                     lambda e, i=i: e.tensor_scalar(
                         cvX[:, i, osl0], stX[:, i, isl0],
                         sb["cwX"][:, i, 0:1], sb["cbX"][:, i:i + 1],
                         op0=ALU.mult, op1=ALU.add),
                     wait=w1, inc="tv")
                p.op("vector",
                     lambda e, i=i: e.tensor_scalar(
                         cvX[:, i, esl0], sb["haloX"][:, i, hsl0],
                         sb["cwX"][:, i, 0:1], sb["cbX"][:, i:i + 1],
                         op0=ALU.mult, op1=ALU.add),
                     inc="tv")
                if halo_fn == "per_tile":
                    p.op("vector",
                         lambda e, i=i: e.tensor_copy(
                             sb["haloX"][:, i, hset], stX[:, i, hsrc]),
                         inc="tv")

            def taps_round(r4):
                t4 = slice(r4 * 4, r4 * 4 + 4)
                for tap in range(1, 4):
                    osl, isl, esl, hsl = sls(tap)
                    scr3 = scr[:, 0:4 * n_main].rearrange(
                        "p (i t) -> p i t", i=4)
                    p.op("vector",
                         lambda e, t4=t4, isl=isl, tap=tap, scr3=scr3:
                         e.tensor_tensor(
                             scr3, stX[:, t4, isl],
                             sb["cwX"][:, t4, tap:tap + 1].broadcast_to(
                                 [128, 4, n_main]),
                             op=ALU.mult),
                         wait=["tv"], inc="tv")
                    p.op("vector",
                         lambda e, t4=t4, osl=osl, scr3=scr3:
                         e.tensor_tensor(
                             cvX[:, t4, osl], cvX[:, t4, osl], scr3,
                             op=ALU.add),
                         wait=["tv"], inc="tv")

            for r4 in range(4):
                for i in range(r4 * 4, r4 * 4 + 4):
                    w1 = list(rdy) + ["tv", "qy"] if (r4 == 0 and i == 0) \
                        else ()
                    if tmk is not None:
                        w1 = ([("ts", tmk[0][i])] + (["tv", "qy"]
                              if i == 0 else []))
                    tap0_tile(i, w1)
                taps_round(r4)
            for bc, wv_, bv_ in ((0, sb["cwB"], sb["cbB"]),
                                 (1, sb["cwC"], sb["cbC"])):
                w1 = [("ts", tmk[1])] if (tmk is not None and bc == 0) else ()
                p.op("vector",
                     lambda e, bc=bc, wv_=wv_, bv_=bv_: e.tensor_scalar(
                         cvBC[0:64, bc, osl0], stBC[0:64, bc, isl0],
                         wv_[:, 0:1], bv_[:, 0:1],
                         op0=ALU.mult, op1=ALU.add),
                     wait=w1, inc="tv" if bc == 1 else None)
                p.op("vector",
                     lambda e, bc=bc, wv_=wv_, bv_=bv_: e.tensor_scalar(
                         cvBC[0:64, bc, esl0], sb["haloBC"][0:64, bc, hsl0],
                         wv_[:, 0:1], bv_[:, 0:1],
                         op0=ALU.mult, op1=ALU.add),
                     inc="tv" if bc == 1 else None)
                if halo_fn == "per_tile":
                    p.op("vector",
                         lambda e, bc=bc: e.tensor_copy(
                             sb["haloBC"][0:64, bc, hset],
                             stBC[0:64, bc, hsrc]),
                         inc="tv")
            p.op("vector", lambda e: e.nop(), wait=["tv"])
            if halo_fn is not None and halo_fn != "per_tile":
                halo_fn()
            for tap in range(1, 4):
                osl, isl, esl, hsl = sls(tap)
                scr2 = scr[:, 0:2 * n_main].rearrange("p (i t) -> p i t", i=2)
                p.op("vector",
                     lambda e, isl=isl, tap=tap, scr2=scr2: e.tensor_tensor(
                         scr2[0:64, :, :], stBC[0:64, :, isl],
                         sb["cwBC2"][0:64, :, tap:tap + 1].broadcast_to(
                             [64, 2, n_main]),
                         op=ALU.mult),
                     wait=["tv"], inc="tv")
                p.op("vector",
                     lambda e, osl=osl, scr2=scr2: e.tensor_tensor(
                         cvBC[0:64, :, osl], cvBC[0:64, :, osl],
                         scr2[0:64, :, :], op=ALU.add),
                     wait=["tv"], inc="tv")
            for tap in range(1, 4):
                osl, isl, esl, hsl = sls(tap)
                scr3 = scr[:, 0:NH * 3].rearrange("p (i t) -> p i t", i=NH)
                p.op("vector",
                     lambda e, hsl=hsl, tap=tap, scr3=scr3: e.tensor_tensor(
                         scr3, sb["haloX"][:, :, hsl],
                         sb["cwX"][:, :, tap:tap + 1].broadcast_to(
                             [128, NH, 3]),
                         op=ALU.mult),
                     wait=["tv"], inc="tv")
                p.op("vector",
                     lambda e, esl=esl, scr3=scr3: e.tensor_tensor(
                         cvX[:, :, esl], cvX[:, :, esl], scr3, op=ALU.add),
                     wait=["tv"], inc="tv")
                scr2 = scr[:, 512:512 + 2 * 3].rearrange(
                    "p (i t) -> p i t", i=2)
                p.op("vector",
                     lambda e, hsl=hsl, tap=tap, scr2=scr2: e.tensor_tensor(
                         scr2[0:64, :, :], sb["haloBC"][0:64, :, hsl],
                         sb["cwBC2"][0:64, :, tap:tap + 1].broadcast_to(
                             [64, 2, 3]),
                         op=ALU.mult),
                     wait=["tv"], inc="tv")
                p.op("vector",
                     lambda e, esl=esl, scr2=scr2: e.tensor_tensor(
                         cvBC[0:64, :, esl], cvBC[0:64, :, esl],
                         scr2[0:64, :, :], op=ALU.add),
                     wait=["tv"], inc="tv")
            cvv = p.mark("tv")
            p.op("scalar",
                 lambda e: e.activation(
                     cvX[:].rearrange("p i t -> p (i t)"),
                     cvX[:].rearrange("p i t -> p (i t)"), AF.Silu),
                 wait=[("tv", cvv), "tg", "ts"], inc="ts")
            p.op("scalar",
                 lambda e: e.activation(
                     cvBC[0:64, :, :].rearrange("p i t -> p (i t)"),
                     cvBC[0:64, :, :].rearrange("p i t -> p (i t)"),
                     AF.Silu),
                 wait=["ts"], inc="ts")

        def dt_pipeline(dirn, rdy, prev):
            v = sb
            p.op("vector",
                 lambda e: e.tensor_scalar(v["dt_t"][:], v["dtraw"][:],
                                           v["dtbias"][:, 0:1], None,
                                           op0=ALU.add),
                 wait=rdy + ["tv"], inc="tv")
            t0 = p.mark("tv")
            p.op("scalar",
                 lambda e: e.activation(v["sp1"][:], v["dt_t"][:], AF.Abs),
                 wait=[("tv", t0), "ts"], inc="ts")
            p.op("scalar",
                 lambda e: e.activation(v["sp1"][:], v["sp1"][:], AF.Exp,
                                        scale=-1.0),
                 wait=["ts"], inc="ts")
            p.op("scalar",
                 lambda e: e.activation(v["sp1"][:], v["sp1"][:], AF.Ln,
                                        bias=sb["oneb"][0:16, 0:1]),
                 wait=["ts"], inc="ts")
            p.op("scalar",
                 lambda e: e.activation(v["sp2"][:], v["dt_t"][:], AF.Relu),
                 wait=["ts"], inc="ts")
            spv = p.mark("ts")
            p.op("vector",
                 lambda e: e.tensor_tensor(v["dt_t"][:], v["sp1"][:],
                                           v["sp2"][:], op=ALU.add),
                 wait=[("ts", spv)], inc="tv")
            dtv = p.mark("tv")
            p.op("scalar",
                 lambda e: e.activation(v["lndt"][:], v["dt_t"][:], AF.Ln),
                 wait=[("tv", dtv)], inc="ts")
            lnv = p.mark("ts")
            p.op("vector",
                 lambda e: e.tensor_scalar(v["a_hm"][:], v["dt_t"][:],
                                           v["negA"][:, 0:1], None,
                                           op0=ALU.mult),
                 wait=[("tv", dtv)], inc="tv")
            av = p.mark("tv")
            for c in range(CPG):
                sl = slice(c * Q, (c + 1) * Q)
                p.op("vector",
                     lambda e, sl=sl: e.tensor_tensor_scan(
                         v["t16a"][:, sl], v["ones16"][:, 0:Q],
                         v["a_hm"][:, sl], 0.0, op0=ALU.mult, op1=ALU.add),
                     wait=[("tv", av)] if c == 0 else ["tv"], inc="tv")
            pv = p.mark("tv")
            if dirn == "fw":
                p.op("vector",
                     lambda e: e.tensor_copy(v["s_hm"][:], v["t16a"][:]),
                     wait=[("tv", pv)], inc="tv")
            else:
                for c in range(CPG):
                    sl = slice(c * Q, (c + 1) * Q)
                    last = (c + 1) * Q - 1
                    p.op("vector",
                         lambda e, sl=sl, last=last: e.tensor_scalar(
                             v["t16b"][:, sl], v["t16a"][:, sl],
                             -1.0, v["t16a"][:, last:last + 1],
                             op0=ALU.mult, op1=ALU.add),
                         wait=[("tv", pv)] if c == 0 else ["tv"], inc="tv")
                p.op("vector",
                     lambda e: e.tensor_tensor(v["s_hm"][:], v["t16b"][:],
                                               v["a_hm"][:], op=ALU.add),
                     wait=["tv"], inc="tv")
            sv = p.mark("tv")
            p.op("vector",
                 lambda e: e.tensor_tensor(v["lndt"][:], v["lndt"][:],
                                           v["s_hm"][:], op=ALU.subtract),
                 wait=[("tv", sv), ("ts", lnv)], inc="tv")
            p.op("vector",
                 lambda e: e.tensor_copy(v["spl6"][:, 0, :], v["s_hm"][:]),
                 wait=[("tv", sv), "tv"], inc="tv")
            p.op("vector",
                 lambda e: e.tensor_tensor(v["t16a"][:], v["s_hm"][:],
                                           v["spl6"][:, 0, :],
                                           op=ALU.subtract),
                 wait=["tv"], inc="tv")
            p.op("vector",
                 lambda e: e.tensor_copy(v["spl6"][:, 1, :], v["t16a"][:]),
                 wait=["tv"], inc="tv")
            p.op("vector",
                 lambda e: e.tensor_tensor(v["t16b"][:], v["t16a"][:],
                                           v["spl6"][:, 1, :],
                                           op=ALU.subtract),
                 wait=["tv"], inc="tv")
            p.op("vector",
                 lambda e: e.tensor_copy(v["spl6"][:, 2, :], v["t16b"][:]),
                 wait=["tv"], inc="tv")
            p.op("vector",
                 lambda e: e.tensor_copy(v["spl6"][:, 3, :], v["lndt"][:]),
                 wait=["tv"], inc="tv")
            p.op("vector",
                 lambda e: e.tensor_tensor(v["t16a"][:], v["lndt"][:],
                                           v["spl6"][:, 3, :],
                                           op=ALU.subtract),
                 wait=["tv"], inc="tv")
            p.op("vector",
                 lambda e: e.tensor_copy(v["spl6"][:, 4, :], v["t16a"][:]),
                 wait=["tv"], inc="tv")
            p.op("vector",
                 lambda e: e.tensor_tensor(v["t16b"][:], v["t16a"][:],
                                           v["spl6"][:, 4, :],
                                           op=ALU.subtract),
                 wait=["tv"], inc="tv")
            p.op("vector",
                 lambda e: e.tensor_copy(v["spl6"][:, 5, :], v["t16b"][:]),
                 wait=["tv"], inc="tv")
            spl = p.mark("tv")
            p.op("scalar",
                 lambda e: e.dma_start(out=dr["s_scr"], in_=v["spl6"][:]),
                 wait=[("tv", spl)], inc="qd", dma=True)
            sv2 = p.mark("qd")
            p.op("scalar",
                 lambda e: e.dma_start(
                     out=sb["sflat"][0:3, :, :],
                     in_=dr["s_scr"][:, 0:3, 0:128].rearrange(
                         "h j t -> j h t")),
                 wait=[("qd", sv2), ("tp", prev["tp"])], inc="qd", dma=True)
            p.op("scalar",
                 lambda e: e.dma_start(
                     out=sb["sflat"][32:35, :, :],
                     in_=dr["s_scr"][:, 0:3, 128:256].rearrange(
                         "h j t -> j h t")),
                 inc="qd", dma=True)
            for j_ in range(3):
                p.op("scalar",
                     lambda e, j_=j_: e.dma_start(
                         out=sb["b3c"][16 * j_:16 * (j_ + 1), :],
                         in_=dr["s_scr"][:, 3 + j_, :]),
                     inc="qd", dma=True)

        def scan_chunk(dirn, c, hTa, hTb, q_rot, tvm):
            anchor = (Q - 1) if dirn == "fw" else 0
            csl = slice(c * Q, (c + 1) * Q)
            v = sb
            urb = v["ur"][:].bitcast(BF16)
            ostb = v["ostg"][:].bitcast(BF16)
            mtq2 = urb[:, 0:1024]
            gq2 = urb[0:64, 1024:2048]
            bwq2 = ostb[:, 0:512]
            diagq2 = ostb[0:64, 512:1024]
            mm(ps[:, 3072:3072 + 128], v["convBC"][0:64, 0, csl],
               v["convBC"][0:64, 1, csl], True, True,
               wait=["ts", ("tv", tvm), "qd", "qy"])
            p.op("scalar",
                 lambda e: e.copy(v["cbt"][:], ps[:, 3072:3072 + 128]),
                 wait=["tp"], inc="ts")
            cbv = p.mark("ts")
            ph = []
            for hq in range(2):
                h0 = hq * 8
                pd = 1024 + 1024 * (q_rot[0] % 2)
                q_rot[0] += 1
                dsl = slice(pd, pd + 1024)
                cb0 = 32 * c
                mtq_h = v["mtq"][:] if hq == 0 else mtq2
                gq_h = v["gq"][:] if hq == 0 else gq2
                bwq_h = v["bwq"][:] if hq == 0 else bwq2
                diagq_h = v["diagq"][:] if hq == 0 else diagq2
                mm(ps[0:64, pd:pd + 512], v["ones3"][cb0:cb0 + 3, 0:64],
                   v["sflat"][cb0:cb0 + 3, h0:h0 + 4, :], True, True,
                   wait=["qd", "ts", ("tv", tvm)])
                mm(ps[0:64, pd + 512:pd + 1024],
                   v["ones3"][cb0:cb0 + 3, 0:64],
                   v["sflat"][cb0:cb0 + 3, h0 + 4:h0 + 8, :], True, True)
                t1v = p.mark("tp")
                p.op("scalar",
                     lambda e, gq_h=gq_h, pd=pd: e.activation(
                         gq_h, ps[0:64, pd:pd + 1024], AF.Exp),
                     wait=[("tp", t1v), "ts", ("tv", tvm)], inc="ts")
                gv = p.mark("ts")
                for nn in range(2):
                    nsl = slice(pd + nn * 512, pd + nn * 512 + 512)
                    mm(ps[:, nsl], v["ones3"][cb0:cb0 + 3, :],
                       v["sflat"][cb0:cb0 + 3, h0 + nn * 4:h0 + nn * 4 + 4, :],
                       True, False,
                       wait=[("ts", gv)] if nn == 0 else ())
                    mm(ps[:, nsl], v["b3c"][:, csl],
                       v["ind3"][:, h0 + nn * 4:h0 + nn * 4 + 4, :],
                       False, False)
                    mm(ps[:, nsl], v["i128"][:],
                       v["maskst"][:], False, True)
                t2v = p.mark("tp")
                p.op("scalar",
                     lambda e, mtq_h=mtq_h, dsl=dsl: e.activation(
                         mtq_h, ps[:, dsl], AF.Exp),
                     wait=[("tp", t2v), "tv"], inc="ts")
                mtv = p.mark("ts")
                p.op("vector",
                     lambda e, bwq_h=bwq_h, mtq_h=mtq_h: e.tensor_tensor(
                         bwq_h.rearrange("p (h n) -> p h n", h=8),
                         v["Btm"][:, c, :].unsqueeze(1).broadcast_to(
                             [128, 8, 64]),
                         mtq_h.rearrange("p (h t) -> p h t", h=8)[
                             :, :, anchor:anchor + 1].broadcast_to(
                                 [128, 8, 64]),
                         op=ALU.mult),
                     wait=[("ts", mtv), ("ts", cbv), "tv", "qo"],
                     inc="tv")
                p.op("vector",
                     lambda e, diagq_h=diagq_h, gq_h=gq_h: e.tensor_tensor(
                         diagq_h.rearrange("p (h n) -> p h n", h=8),
                         v["i64"][:].unsqueeze(1).broadcast_to([64, 8, 64]),
                         gq_h.rearrange("p (h t) -> p h t", h=8)[
                             :, :, anchor:anchor + 1].broadcast_to(
                                 [64, 8, 64]),
                         op=ALU.mult),
                     wait=["tv"], inc="tv")
                if hq == 0:
                    p.op("vector",
                         lambda e: e.tensor_tensor(
                             v["gcq"][:].rearrange("p (h t) -> p h t", h=8),
                             v["gq"][:].rearrange("p (h t) -> p h t", h=8),
                             v["convBC"][0:64, 1, csl].unsqueeze(1)
                             .broadcast_to([64, 8, 128]),
                             op=ALU.mult),
                         wait=["tv"], inc="tv")
                p.op("vector",
                     lambda e, mtq_h=mtq_h: e.tensor_tensor(
                         mtq_h.rearrange("p (h t) -> p h t", h=8),
                         mtq_h.rearrange("p (h t) -> p h t", h=8),
                         v["cbt"][:].unsqueeze(1).broadcast_to(
                             [128, 8, 128]),
                         op=ALU.mult),
                     wait=["tv"], inc="tv")
                p.op("vector",
                     lambda e, mtq_h=mtq_h, h0=h0: e.tensor_tensor(
                         mtq_h, mtq_h,
                         v["ddiag"][:, h0 * 128:(h0 + 8) * 128], op=ALU.add),
                     wait=["tv"], inc="tv")
                opv = p.mark("tv")
                ph.append((h0, pd, mtq_h, gq_h, bwq_h, diagq_h, opv))
            for hq in range(2):
                h0, pd, mtq_h, gq_h, bwq_h, diagq_h, opv = ph[hq]
                if hq == 1:
                    ymv0 = p.mark("tp")
                    p.op("vector",
                         lambda e, gq_h=gq_h: e.tensor_tensor(
                             v["gcq"][:].rearrange("p (h t) -> p h t", h=8),
                             gq_h.rearrange("p (h t) -> p h t", h=8),
                             v["convBC"][0:64, 1, csl].unsqueeze(1)
                             .broadcast_to([64, 8, 128]),
                             op=ALU.mult),
                         wait=[("tp", ymv0), "tv"], inc="tv")
                    opv = p.mark("tv")
                for half4 in range(2):
                    sev = p.mark("ts")
                    for hh4 in range(4):
                        hh = half4 * 4 + hh4
                        h = h0 + hh
                        ysl = slice(pd + hh * 128, pd + (hh + 1) * 128)
                        xop = v["Xtm"][:, h * CPG + c, :]
                        w0 = ()
                        if hh == 0:
                            w0 = [("tv", opv), "ts"]
                        elif hh4 == 0:
                            w0 = [("ts", sev)]
                        mm(ps[:, ysl], xop,
                           mtq_h[:, hh * 128:(hh + 1) * 128], True, False,
                           wait=w0)
                        mm(ps[:, ysl], hTa[:, h * 128:(h + 1) * 128],
                           v["gcq"][:, hh * 128:(hh + 1) * 128], False, True)
                        ssl = slice(3584 + hh4 * 128, 3584 + (hh4 + 1) * 128)
                        mm(ps[0:64, ssl], bwq_h[:, hh * 64:(hh + 1) * 64],
                           xop, True, False)
                        mm(ps[0:64, ssl],
                           diagq_h[:, hh * 64:(hh + 1) * 64],
                           hTa[:, h * 128:(h + 1) * 128], False, True)
                    smv = p.mark("tp")
                    p.op("scalar",
                         lambda e, h0=h0, half4=half4: e.copy(
                             hTb[:, (h0 + half4 * 4) * 128:
                                 (h0 + half4 * 4 + 4) * 128],
                             ps[0:64, 3584:3584 + 512]),
                         wait=[("tp", smv)], inc="ts")
                ymv = p.mark("tp")
                p.op("scalar",
                     lambda e, h0=h0, pd=pd, csl=csl: e.copy(
                         sb["ystg"][:, h0:h0 + 8, csl],
                         ps[:, pd:pd + 1024].rearrange(
                             "p (h t) -> p h t", h=8)),
                     wait=[("tp", ymv)], inc="ts")

        def inproj_mtile(wT, mofs, msz, evac_fn, xv, rot, extra_wait=()):
            idx = rot["n"] % 2
            pb = PB_MM[idx]
            rot["n"] += 1
            w0 = [("qs", xv), ("ts", rot["mk"][idx])] + list(extra_wait)
            for k in range(8):
                mm(ps[0:msz, pb:pb + G], wT[:, k, mofs:mofs + msz],
                   sb["xnT"][:, k, :], k == 0, k == 7,
                   wait=w0 if k == 0 else ())
            mv_ = p.mark("tp")
            evac_fn(ps[0:msz, pb:pb + G], mv_)
            rot["mk"][idx] = p.mark("ts")

        def pass_A():
            p.op("vector", lambda e: e.memset(sb["epsb"][:], EPS), inc="tv")
            p.op("vector", lambda e: e.memset(sb["oneb"][:], 1.0), inc="tv")
            load_consts()
            wTA = sb["warena"][:, 0:8 * 2192].rearrange(
                "p (k m) -> p k m", k=8)
            bbA = sb["warena"][:, 23552:40960].rearrange(
                "p (t c) -> p t c", t=17)
            wT_build(dr["ipw"][2048:4240, :], 2192, dr["wbfA"], wTA, bb=bbA)
            dir_params("bw")
            wTv = p.mark("qs")
            p.op("vector", lambda e: e.memset(sb["haloX"][:], 0.0),
                 wait=["tv"], inc="tv")
            p.op("vector", lambda e: e.memset(sb["haloBC"][:], 0.0), inc="tv")
            p.op("vector", lambda e: e.memset(sb["hT0"][:], 0.0), inc="tv")
            p.op("vector", lambda e: e.memset(sb["ones16"][:], 1.0), inc="tv")
            mk_st = None
            for i in range(8):
                ld(sb["um"][:], dr["fwt"][i * 128:(i + 1) * 128, :],
                   wait=["tv"], inc="qw")
                p.op("vector", lambda e: e.nop(), wait=["qw"])
                dwl = [("qw", mk_st)] if mk_st is not None else ()
                quantize_tile(sb["xd2"][:, 0, :], rst_is_one=True,
                              dst_wait=dwl)
                qv = p.mark("tv")
                p.op("scalar",
                     lambda e, i=i: e.dma_start(
                         out=dr["wbfF"][i * 128:(i + 1) * 128, :],
                         in_=sb["xd2"][:, 0, :]),
                     wait=[("tv", qv)], inc="qw", dma=True)
                mk_st = p.mark("qw")
            shared["qfv"] = p.mark("qw")
            q_rot = [0, 0]
            rot = {"n": 0, "mk": [0, 0]}

            def pro(t0):
                for s_ in range(2):
                    ld(sb["u2"][:],
                       dr["u"][t0 + s_ * 128:t0 + (s_ + 1) * 128, :],
                       wait=["tv", "qo"], inc="qu", q="sync")
                    ln_tile(s_, False)
                    xnv = p.mark("tv")
                    tpv = p.mark("tp")
                    p.op("sync",
                         lambda e, s_=s_, xnv=xnv, tpv=tpv: e.dma_start(
                             out=sb["xnT"][:, :, s_ * 128:(s_ + 1) * 128],
                             in_=sb["xn2"][:, s_, :].rearrange(
                                 "p (k f) -> p k f", k=8),
                             transpose=True),
                         wait=[("tv", xnv), ("tp", tpv), ("qs", wTv)],
                         inc="qs", dma=True)
                return p.mark("qs")

            prev = {"tv": p.mark("tv"), "ts": p.mark("ts"),
                    "tp": p.mark("tp")}
            xv = pro((NG - 1) * G)
            for gi in range(NG - 1, -1, -1):

                def ev(dst3):
                    def f(pap, mv_):
                        p.op("scalar", lambda e: e.copy(dst3, pap),
                             wait=[("tp", mv_)], inc="ts")
                    return f

                ew = [("tv", prev["tv"]), ("qx", None)]
                inproj_mtile(wTA, 2048, 64, ev(sb["stgBC"][0:64, 0, :]),
                             xv, rot, ew)
                inproj_mtile(wTA, 2112, 64, ev(sb["stgBC"][0:64, 1, :]),
                             xv, rot, ew)
                mkBC = p.mark("ts")
                inproj_mtile(wTA, 2176, 16, ev(sb["dtraw"][:]), xv, rot, ew)
                mkX = []
                for i in range(NH):
                    inproj_mtile(wTA, i * 128, 128, ev(sb["stgX"][:, i, :]),
                                 xv, rot, ew)
                    mkX.append(p.mark("ts"))
                stv = p.mark("ts")
                if gi == 0:
                    shared["tpA"] = p.mark("tp")
                conv_dir("bw", [("ts", stv)], tmk=(mkX, mkBC),
                         halo_fn="per_tile")
                p.op("gpsimd",
                     lambda e, gi=gi: e.dma_start(
                         out=dr["sp_x"][gi],
                         in_=sb["stgX"][:].rearrange("p i t -> p (i t)")),
                     wait=[("ts", stv)], inc="qx", dma=True)
                p.op("gpsimd",
                     lambda e, gi=gi: e.dma_start(
                         out=dr["sp_bc"][gi],
                         in_=sb["stgBC"][0:64, :, :].rearrange(
                             "p i t -> p (i t)")),
                     inc="qx", dma=True)
                p.op("gpsimd",
                     lambda e, gi=gi: e.dma_start(out=dr["sp_dt"][gi],
                                                  in_=sb["dtraw"][:]),
                     inc="qx", dma=True)
                cvv = p.mark("ts")
                p.op("sync",
                     lambda e: e.dma_start(
                         out=sb["Xtm"][:],
                         in_=sb["convX"][:].rearrange("p i t -> p (i t)")
                         .rearrange("p (m q) -> p m q", q=128),
                         transpose=True),
                     wait=[("ts", cvv), ("tp", prev["tp"])],
                     inc="qs", dma=True)
                p.op("sync",
                     lambda e: e.dma_start(
                         out=sb["Btm"][:],
                         in_=sb["convBC"][0:64, 0, :].rearrange(
                             "p (m q) -> p m q", q=128),
                         transpose=True),
                     inc="qs", dma=True)
                trv = p.mark("qs")
                dt_pipeline("bw", [("ts", stv)], prev)
                tvm = p.mark("tv")
                if gi > 0:
                    xv = pro((gi - 1) * G)
                for c in range(CPG - 1, -1, -1):
                    p.op("tensor", lambda e: e.nop(),
                         wait=[("qs", trv), ("tv", tvm), "ts"])
                    scan_chunk("bw", c, sb["hT0"][:], sb["hT0"][:], q_rot,
                               tvm)
                p.op("vector",
                     lambda e: e.tensor_copy(sb["haloX"][:, :, 3:6],
                                             sb["stgX"][:, :, 0:3]),
                     wait=["ts", "tv", "tg"], inc="tv")
                p.op("vector",
                     lambda e: e.tensor_copy(sb["haloBC"][0:64, :, 3:6],
                                             sb["stgBC"][0:64, :, 0:3]),
                     wait=["tv"], inc="tv")
                yv = p.mark("ts")
                p.op("gpsimd",
                     lambda e, gi=gi: e.dma_start(
                         out=dr["sp_y"][gi],
                         in_=sb["ystg"][:].rearrange("p i t -> p (i t)")),
                     wait=[("ts", yv), "tv"], inc="qy", dma=True)
                prev = {"tv": p.mark("tv"), "ts": p.mark("ts"),
                        "tp": p.mark("tp")}
            bbo = sb["warena"][:, 32768:40960].rearrange(
                "p (t c) -> p t c", t=4)
            for c0 in range(0, 8, 4):
                wv0 = p.mark("qw")
                p.op("gpsimd",
                     lambda e, c0=c0: e.dma_start(
                         out=bbo,
                         in_=dr["opw"][c0 * 128:(c0 + 4) * 128, :]
                         .rearrange("(t p) c -> p t c", p=128)),
                     wait=[("qw", wv0)], inc="qg", dma=True)
                wv = p.mark("qg")
                p.op("scalar",
                     lambda e, c0=c0: e.dma_start(
                         out=dr["wbfO"][c0 * 128:(c0 + 4) * 128, :]
                         .rearrange("(t p) c -> p t c", p=128),
                         in_=bbo),
                     wait=[("qg", wv)], inc="qw", dma=True)
            bbz = sb["warena"][:, 32768:40960].rearrange(
                "p (t c) -> p t c", t=8)
            wT_build(dr["ipw"][0:2048, :], 2048, dr["wbfZ"], None, bb=bbz)
            p.op("scalar", lambda e: e.dma_start(
                out=sb["nwtile"][:],
                in_=dr["normw"].rearrange("(k f) -> f k", f=128),
                allow_slow_non_contiguous=True), inc="qw", dma=True)
            shared["nv"] = p.mark("qw")
            wdTl = sb["warena"][:, 32768:40960].rearrange(
                "p (k m) -> p k m", k=8)
            p.op("sync",
                 lambda e: e.dma_start(out=wdTl, in_=dr["wbfF"],
                                       transpose=True),
                 wait=[("qw", None)], inc="qs", dma=True)

        def quantize_tile(dst_ap, rst_is_one=False, srcb=None, dst_wait=()):
            v = sb
            srcb = v["um"] if srcb is None else srcb
            rst = v["mv"][:, 3:4]
            for h2 in range(2):
                hsl = slice(h2 * 512, (h2 + 1) * 512)
                p.op("vector",
                     lambda e, hsl=hsl: e.tensor_reduce(
                         v["qsc"][:].unsqueeze(2),
                         srcb[:, hsl].rearrange("p (b f) -> p b f", b=16),
                         axis=mybir.AxisListType.X, op=ALU.max,
                         apply_absolute_value=True),
                     wait=["tv"], inc="tv")
                if rst_is_one:
                    p.op("vector",
                         lambda e: e.tensor_scalar(v["qsc"][:], v["qsc"][:],
                                                   1.0 / 7.5, 1e-30,
                                                   op0=ALU.mult, op1=ALU.max),
                         wait=["tv"], inc="tv")
                else:
                    p.op("vector",
                         lambda e: e.tensor_scalar(v["qsc"][:], v["qsc"][:],
                                                   rst, None, op0=ALU.mult),
                         wait=["tv"], inc="tv")
                    p.op("vector",
                         lambda e: e.tensor_scalar(v["qsc"][:], v["qsc"][:],
                                                   1.0 / 7.5, 1e-30,
                                                   op0=ALU.mult, op1=ALU.max),
                         wait=["tv"], inc="tv")
                p.op("vector",
                     lambda e: e.reciprocal(v["qrcp"][:], v["qsc"][:]),
                     wait=["tv"], inc="tv")
                if rst_is_one:
                    p.op("vector",
                         lambda e: e.tensor_scalar(v["qrcp"][:], v["qrcp"][:],
                                                   8.0, None, op0=ALU.mult),
                         wait=["tv"], inc="tv")
                else:
                    p.op("vector",
                         lambda e: e.tensor_scalar(v["qrcp"][:], v["qrcp"][:],
                                                   rst, 8.0,
                                                   op0=ALU.mult, op1=ALU.mult),
                         wait=["tv"], inc="tv")
                rv = p.mark("tv")
                p.op("vector",
                     lambda e, hsl=hsl: e.tensor_tensor(
                         v["qv8"][:].rearrange("p (b f) -> p b f", b=16),
                         srcb[:, hsl].rearrange("p (b f) -> p b f", b=16),
                         v["qrcp"][:].unsqueeze(2).broadcast_to(
                             [128, 16, 32]),
                         op=ALU.mult),
                     wait=[("tv", rv)], inc="tv")
                p.op("vector",
                     lambda e: e.scalar_tensor_tensor(v["qau"][:], v["qv8"][:],
                                                      -1.0, v["qv8"][:],
                                                      op0=ALU.mult,
                                                      op1=ALU.max),
                     wait=["tv"], inc="tv")
                p.op("vector",
                     lambda e: e.tensor_scalar(v["qm1"][:], v["qau"][:], 16.0,
                                               None, op0=ALU.is_ge),
                     wait=["tv"], inc="tv")
                p.op("vector",
                     lambda e: e.tensor_scalar(v["qm2"][:], v["qau"][:], 32.0,
                                               2.0, op0=ALU.is_ge,
                                               op1=ALU.mult),
                     wait=["tv"], inc="tv")
                p.op("vector",
                     lambda e: e.tensor_tensor(v["qm2"][:], v["qm2"][:],
                                               v["qm1"][:], op=ALU.add),
                     wait=["tv"], inc="tv")
                p.op("vector",
                     lambda e: e.tensor_scalar(v["qm1"][:], v["qm2"][:],
                                               1.0, 12582912.0,
                                               op0=ALU.add, op1=ALU.mult),
                     wait=["tv"], inc="tv")
                p.op("vector",
                     lambda e: e.tensor_tensor(v["qv8"][:], v["qv8"][:],
                                               v["qm1"][:], op=ALU.add),
                     wait=["tv"], inc="tv")
                p.op("vector",
                     lambda e: e.tensor_tensor(v["qv8"][:], v["qv8"][:],
                                               v["qm1"][:], op=ALU.subtract),
                     wait=["tv"], inc="tv")
                p.op("vector",
                     lambda e: e.tensor_scalar(v["qsc2"][:], v["qsc"][:],
                                               0.125, None, op0=ALU.mult),
                     wait=["tv"], inc="tv")
                p.op("vector",
                     lambda e, hsl=hsl: e.tensor_tensor(
                         dst_ap[:, hsl].rearrange("p (b f) -> p b f", b=16),
                         v["qv8"][:].rearrange("p (b f) -> p b f", b=16),
                         v["qsc2"][:].unsqueeze(2).broadcast_to(
                             [128, 16, 32]),
                         op=ALU.mult),
                     wait=(["tv", "qs"] + (list(dst_wait) if h2 == 0
                                           else [])), inc="tv")

        def pass_B():
            wTz = sb["warena"][:, 0:16384].rearrange("p (k m) -> p k m", k=8)
            WoT = sb["warena"][:, 16384:32768].rearrange(
                "p (k m) -> p k m", k=16)
            wdT = sb["warena"][:, 32768:40960].rearrange(
                "p (k m) -> p k m", k=8)
            wT_build(None, 2048, dr["wbfZ"], wTz,
                     xw=[("tp", shared["tpA"])], bounce=False)
            dir_params("fw")
            ov2 = p.mark("qw")
            p.op("sync",
                 lambda e: e.dma_start(out=WoT, in_=dr["wbfO"],
                                       transpose=True),
                 wait=[("qw", ov2), ("tp", shared["tpA"])],
                 inc="qs", dma=True)
            wov = p.mark("qs")
            nv = shared["nv"]
            for k in range(16):
                p.op("vector",
                     lambda e, k=k: e.tensor_scalar(
                         WoT[:, k, :], WoT[:, k, :],
                         sb["nwtile"][:, k:k + 1], None, op0=ALU.mult),
                     wait=[("qs", wov), ("qw", nv), "tv"] if k == 0
                     else ["tv"], inc="tv")
            ld(sb["um"][0:1, :], dr["fb"].unsqueeze(0),
               wait=["tv"], inc="qw")
            fbv = p.mark("qw")
            for half in range(2):
                dst = PB_X if half == 0 else PB_D[0]
                mm(ps[:, dst:dst + 512], sb["ones1f"][:],
                   sb["um"][0:1, half * 512:(half + 1) * 512],
                   True, True,
                   wait=[("qw", fbv), "tv", "ts"] if half == 0 else ())
            fmv = p.mark("tp")
            fb32 = sb["wstage"][:].rearrange("p t c -> p (t c)").bitcast(F32)
            p.op("vector",
                 lambda e: e.tensor_copy(fb32[:, 0:512],
                                         ps[:, PB_X:PB_X + 512]),
                 wait=[("tp", fmv)], inc="tv")
            p.op("vector",
                 lambda e: e.tensor_copy(fb32[:, 512:1024],
                                         ps[:, PB_D[0]:PB_D[0] + 512]),
                 wait=["tv"], inc="tv")
            wdv = p.mark("qs")
            p.op("vector", lambda e: e.memset(sb["haloX"][:], 0.0),
                 wait=["ts"], inc="tv")
            p.op("vector", lambda e: e.memset(sb["haloBC"][:], 0.0), inc="tv")
            p.op("vector", lambda e: e.memset(sb["hT0"][:], 0.0), inc="tv")
            p.op("vector", lambda e: e.memset(sb["ones16"][:], 1.0), inc="tv")
            q_rot = [0, 0]
            rot = {"n": 0, "mk": [0, 0]}
            yzn_ap = sb["Xtm"][:].rearrange("p m q -> p (m q)").rearrange(
                "p (i t) -> p i t", i=NH)

            def pro(t0):
                for s_ in range(2):
                    ld(sb["u2"][:],
                       dr["u"][t0 + s_ * 128:t0 + (s_ + 1) * 128, :],
                       wait=["tv", "qo"], inc="qu", q="sync")
                    ln_tile(s_, True)
                    xnv = p.mark("tv")
                    tpv = p.mark("tp")
                    p.op("sync",
                         lambda e, s_=s_, xnv=xnv, tpv=tpv: e.dma_start(
                             out=sb["xnT"][:, :, s_ * 128:(s_ + 1) * 128],
                             in_=sb["xn2"][:, s_, :].rearrange(
                                 "p (k f) -> p k f", k=8),
                             transpose=True),
                         wait=[("tv", xnv), ("tp", tpv), ("qs", wdv)],
                         inc="qs", dma=True)
                    quantize_tile(sb["xd2"][:, s_, :], rst_is_one=False)
                    qv = p.mark("tv")
                    p.op("sync",
                         lambda e, s_=s_, qv=qv: e.dma_start(
                             out=sb["xdT"][:, :, s_ * 128:(s_ + 1) * 128],
                             in_=sb["xd2"][:, s_, :].rearrange(
                                 "p (k f) -> p k f", k=8),
                             transpose=True),
                         wait=[("tv", qv)], inc="qs", dma=True)
                return p.mark("qs")

            prev = {"tv": p.mark("tv"), "ts": p.mark("ts"),
                    "tp": p.mark("tp")}
            xv = pro(0)
            for gi in range(NG):
                t0 = gi * G
                ld(sb["stgX"][:].rearrange("p i t -> p (i t)"),
                   dr["sp_x"][gi], wait=[("tv", prev["tv"])], inc="qx",
                   q="gpsimd")
                ld(sb["stgBC"][0:64, :, :].rearrange("p i t -> p (i t)"),
                   dr["sp_bc"][gi], inc="qx", q="gpsimd")
                ld(sb["dtraw"][:], dr["sp_dt"][gi],
                   wait=[("tv", prev["tv"])], inc="qx", q="gpsimd")
                ld(sb["ybw"][:].rearrange("p i t -> p (i t)"),
                   dr["sp_y"][gi], wait=[("tv", prev["tv"])], inc="qy",
                   q="gpsimd")
                spv = p.mark("qx")
                for i in range(NH):
                    idx = rot["n"] % 2
                    pb = PB_MM[idx]
                    rot["n"] += 1
                    w0 = [("qs", xv), ("ts", rot["mk"][idx]),
                          ("tv", prev["tv"])]
                    for k in range(8):
                        mm(ps[:, pb:pb + G], wTz[:, k, i * 128:(i + 1) * 128],
                           sb["xnT"][:, k, :], k == 0, k == 7,
                           wait=w0 if k == 0 else ())
                    zv = p.mark("tp")
                    p.op("scalar",
                         lambda e, i=i, pb=pb: e.activation(
                             sb["siluz"][:, i, :], ps[:, pb:pb + G], AF.Silu),
                         wait=[("tp", zv), "ts"], inc="ts")
                    rot["mk"][idx] = p.mark("ts")
                szv = p.mark("ts")
                ffbanks = [(PB_X, PB_D[0]), (PB_D[1], PB_S)]
                for s_ in range(2):
                    tsl = slice(s_ * 128, (s_ + 1) * 128)
                    for nh_ in range(2):
                        dst = ffbanks[s_][nh_]
                        for k in range(8):
                            w0 = ()
                            if k == 0 and nh_ == 0 and s_ == 0:
                                w0 = [("tv", prev["tv"]), ("ts", prev["ts"])]
                            mm(ps[:, dst:dst + 512], sb["xdT"][:, k, tsl],
                               wdT[:, k, nh_ * 512:(nh_ + 1) * 512],
                               k == 0, k == 7, wait=w0)
                ffv = p.mark("tp")
                p.op("vector",
                     lambda e: e.tensor_copy(sb["haloX"][:, :, 3:6],
                                             sb["stgX"][:, :, 0:3]),
                     wait=[("qx", spv), "tv", "tg"], inc="tv")
                p.op("vector",
                     lambda e: e.tensor_copy(sb["haloBC"][0:64, :, 3:6],
                                             sb["stgBC"][0:64, :, 0:3]),
                     wait=["tv"], inc="tv")
                conv_dir("fw", [("qx", spv)])
                cvv = p.mark("ts")
                p.op("sync",
                     lambda e: e.dma_start(
                         out=sb["Xtm"][:],
                         in_=sb["convX"][:].rearrange("p i t -> p (i t)")
                         .rearrange("p (m q) -> p m q", q=128),
                         transpose=True),
                     wait=[("ts", cvv), ("tp", prev["tp"])],
                     inc="qs", dma=True)
                p.op("sync",
                     lambda e: e.dma_start(
                         out=sb["Btm"][:],
                         in_=sb["convBC"][0:64, 0, :].rearrange(
                             "p (m q) -> p m q", q=128),
                         transpose=True),
                     inc="qs", dma=True)
                trv = p.mark("qs")
                p.op("vector",
                     lambda e: e.tensor_copy(sb["haloX"][:, :, 0:3],
                                             sb["stgX"][:, :, G - 3:G]),
                     wait=[("ts", cvv), "tv"], inc="tv")
                p.op("vector",
                     lambda e: e.tensor_copy(sb["haloBC"][0:64, :, 0:3],
                                             sb["stgBC"][0:64, :, G - 3:G]),
                     wait=["tv"], inc="tv")
                dt_pipeline("fw", [("qx", spv)], prev)
                for s_ in range(2):
                    for nh_ in range(2):
                        dst = ffbanks[s_][nh_]
                        p.op("vector",
                             lambda e, nh_=nh_, dst=dst, s_=s_:
                             e.tensor_tensor(
                                 sb["gff"][:, s_, nh_ * 512:(nh_ + 1) * 512],
                                 ps[:, dst:dst + 512],
                                 fb32[:, nh_ * 512:(nh_ + 1) * 512],
                                 op=ALU.add),
                             wait=[("tp", ffv), "tv"]
                             if (nh_ == 0 and s_ == 0) else ["tv"],
                             inc="tv")
                    gfv = p.mark("tv")
                    p.op("scalar",
                         lambda e, s_=s_: e.activation(sb["gff"][:, s_, :],
                                                       sb["gff"][:, s_, :],
                                                       AF.Silu),
                         wait=[("tv", gfv), "ts"], inc="ts")
                ggv = p.mark("ts")
                tvm = p.mark("tv")
                if gi < NG - 1:
                    xv = pro(t0 + G)
                for c in range(CPG):
                    p.op("tensor", lambda e: e.nop(),
                         wait=[("qs", trv), ("tv", tvm), "ts"])
                    scan_chunk("fw", c, sb["hT0"][:], sb["hT0"][:], q_rot,
                               tvm)
                yfv = p.mark("ts")
                opvs = [None, None]
                urvs = [None, None]

                def emit_gating(s_):
                    tsl = slice(s_ * 128, (s_ + 1) * 128)
                    w1 = [("ts", yfv), "tv"] if s_ == 0 else ["tv"]
                    p.op("vector",
                         lambda e, tsl=tsl: e.tensor_tensor(
                             sb["ystg"][:, :, tsl], sb["ystg"][:, :, tsl],
                             sb["siluz"][:, :, tsl], op=ALU.mult),
                         wait=w1, inc="tv")
                    p.op("vector",
                         lambda e, tsl=tsl: e.tensor_tensor(
                             sb["ybw"][:, :, tsl], sb["ybw"][:, :, tsl],
                             sb["siluz"][:, :, tsl], op=ALU.mult),
                         wait=["tv", "qy"], inc="tv")
                    gvv = p.mark("tv")
                    for br, buf in ((0, "ystg"), (1, "ybw")):
                        p.op("scalar",
                             lambda e, buf=buf, tsl=tsl: e.activation(
                                 yzn_ap[:, :, tsl], sb[buf][:, :, tsl],
                                 AF.Square),
                             wait=[("tv", gvv), "ts"], inc="ts")
                        sqv = p.mark("ts")
                        psl = slice(PB_X + br * G + s_ * 128,
                                    PB_X + br * G + s_ * 128 + 128)
                        for i in range(NH):
                            mm(ps[0:1, psl], sb["ones128b"][:],
                               yzn_ap[:, i, tsl], i == 0, i == NH - 1,
                               wait=[("ts", sqv)] if i == 0 else ())
                        rmv = p.mark("tp")
                        p.op("scalar",
                             lambda e, br=br, tsl=tsl, psl=psl: e.activation(
                                 sb["rrow"][0:1, br, tsl], ps[0:1, psl],
                                 AF.Ln, scale=1.0 / DIN,
                                 bias=sb["epsb"][0:1, 0:1]),
                             wait=[("tp", rmv), "ts"], inc="ts")
                        p.op("scalar",
                             lambda e, br=br, tsl=tsl: e.activation(
                                 sb["rrow"][0:1, br, tsl],
                                 sb["rrow"][0:1, br, tsl],
                                 AF.Exp, scale=-0.5),
                             wait=["ts"], inc="ts")
                    rsv = p.mark("ts")
                    for br in range(2):
                        psl = slice(PB_X + br * G + s_ * 128,
                                    PB_X + br * G + s_ * 128 + 128)
                        mm(ps[:, psl], sb["ones3"][0:1, :],
                           sb["rrow"][0:1, br, tsl], True, True,
                           wait=[("ts", rsv), "tv"] if br == 0 else ())
                    bcv = p.mark("tp")
                    p.op("vector",
                         lambda e, tsl=tsl: e.tensor_copy(
                             sb["rbc"][:, :, tsl],
                             ps[:, PB_X:PB_X + 512].rearrange(
                                 "p (b t) -> p b t", b=2)[:, :, tsl]),
                         wait=[("tp", bcv)], inc="tv")
                    rbv = p.mark("tv")
                    p.op("vector",
                         lambda e, tsl=tsl: e.tensor_tensor(
                             sb["ystg"][:, :, tsl], sb["ystg"][:, :, tsl],
                             sb["rbc"][:, 0:1, tsl].broadcast_to(
                                 [128, NH, 128]),
                             op=ALU.mult),
                         wait=[("tv", rbv)], inc="tv")
                    p.op("vector",
                         lambda e, tsl=tsl: e.tensor_tensor(
                             sb["ybw"][:, :, tsl], sb["ybw"][:, :, tsl],
                             sb["rbc"][:, 1:2, tsl].broadcast_to(
                                 [128, NH, 128]),
                             op=ALU.mult),
                         wait=["tv"], inc="tv")
                    p.op("vector",
                         lambda e, tsl=tsl: e.tensor_tensor(
                             yzn_ap[:, :, tsl], sb["ystg"][:, :, tsl],
                             sb["ybw"][:, :, tsl], op=ALU.add),
                         wait=["tv"], inc="tv")
                    return p.mark("tv")

                def emit_outproj(s_, ysv_s, extra=()):
                    tsl = slice(s_ * 128, (s_ + 1) * 128)
                    for nh_ in range(2):
                        pbo = PB_MM[nh_]
                        for k in range(16):
                            mm(ps[:, pbo:pbo + 512], yzn_ap[:, k, tsl],
                               WoT[:, k, nh_ * 512:(nh_ + 1) * 512],
                               k == 0, k == 15,
                               wait=[("tv", ysv_s), "ts"] + list(extra)
                               if (k == 0 and nh_ == 0) else ())
                    opvs[s_] = p.mark("tp")
                    ld(sb["ur"][:],
                       dr["u"][t0 + s_ * 128:t0 + (s_ + 1) * 128, :],
                       wait=["tv", "qo"], inc="qu", q="sync")
                    urvs[s_] = p.mark("qu")

                u2v = sb["u2"][:].rearrange("p c -> p c")

                def emit_res(s_):
                    rdst = [sb["ostg"][:], sb["u2"][:, 0:512]]
                    for nh_ in range(2):
                        pbo = PB_MM[nh_]
                        dstb = rdst[nh_]
                        p.op("vector",
                             lambda e, nh_=nh_, pbo=pbo, s_=s_, dstb=dstb:
                             e.tensor_tensor(
                                 dstb, ps[:, pbo:pbo + 512],
                                 sb["gff"][:, s_, nh_ * 512:(nh_ + 1) * 512],
                                 op=ALU.mult),
                             wait=[("tp", opvs[s_]), ("ts", ggv), "tv", "qo"]
                             if nh_ == 0 else ["tv", "qu"], inc="tv")
                    mulv = p.mark("tv")
                    for nh_ in range(2):
                        dstb = rdst[nh_]
                        p.op("vector",
                             lambda e, nh_=nh_, dstb=dstb: e.tensor_tensor(
                                 dstb, dstb,
                                 sb["ur"][:, nh_ * 512:(nh_ + 1) * 512],
                                 op=ALU.add),
                             wait=["tv", ("qu", urvs[s_])], inc="tv")
                        odv = p.mark("tv")
                        p.op("scalar",
                             lambda e, t0=t0, s_=s_, nh_=nh_, dstb=dstb:
                             e.dma_start(
                                 out=dr["out"][t0 + s_ * 128:
                                               t0 + (s_ + 1) * 128,
                                               nh_ * 512:(nh_ + 1) * 512],
                                 in_=dstb),
                             wait=[("tv", odv)], inc="qo", dma=True)
                    return mulv

                ysv0 = emit_gating(0)
                emit_outproj(0, ysv0)
                ysv1 = emit_gating(1)
                mulv0 = emit_res(0)
                emit_outproj(1, ysv1, extra=[("tv", mulv0)])
                emit_res(1)
                prev = {"tv": p.mark("tv"), "ts": p.mark("ts"),
                        "tp": p.mark("tp")}
            p.op("scalar", lambda e: e.nop(), wait=["qo", "qw", "qu", "qd"])
            p.op("sync", lambda e: e.nop(), wait=["qo", "qs"])
            p.op("vector", lambda e: e.nop(), wait=["qo"])
            p.op("gpsimd", lambda e: e.nop(), wait=["qx", "qy", "qg"])

        pass_A()
        pass_B()

    return prog


def build_nc(debug=0):
    prog = make_program(debug)
    nc = bass.Bass("TRN2", target_bir_lowering=False)
    prog(nc, None)
    with ExitStack() as st:
        handles = {n: st.enter_context(nc.semaphore(n)) for n in SEM_NAMES}
        p = P(handles)
        block = st.enter_context(nc.Block())

        @block.sync
        def _(eng):
            p.begin("sync", eng)
            prog(nc, p)

        @block.scalar
        def _(eng):
            p.begin("scalar", eng)
            prog(nc, p)

        @block.vector
        def _(eng):
            p.begin("vector", eng)
            prog(nc, p)

        @block.tensor
        def _(eng):
            p.begin("tensor", eng)
            prog(nc, p)

        @block.gpsimd
        def _(eng):
            p.begin("gpsimd", eng)
            prog(nc, p)
    return nc


def kernel(**inputs):
    consts = build_consts()
    nc = build_nc()
    keymap = {
        "u": "L0_attn_u", "ipw": "L0_attn_in_proj_w",
        "cw": "L0_attn_conv_w", "cb": "L0_attn_conv_b",
        "cwb": "L0_attn_conv_w_bw", "cbb": "L0_attn_conv_b_bw",
        "dtb": "L0_attn_dt_bias", "dtbb": "L0_attn_dt_bias_bw",
        "alog": "L0_attn_A_log", "ablog": "L0_attn_A_b_log",
        "dvec": "L0_attn_D", "dbvec": "L0_attn_D_b",
        "normw": "L0_attn_norm_w", "opw": "L0_attn_out_proj_w",
        "fwt": "L0_ffn_weight", "fb": "L0_ffn_bias",
    }
    base = {k: np.ascontiguousarray(np.asarray(inputs[vk], np.float32))
            for k, vk in keymap.items() if k != "u"}
    base.update(consts)
    u = np.ascontiguousarray(np.asarray(inputs["L0_attn_u"], np.float32))
    in_maps = []
    for b in range(8):
        m = dict(base)
        m["u"] = u[b]
        in_maps.append(m)
    res = run_bass_kernel_spmd(nc, in_maps, core_ids=list(range(8)))
    return np.stack([res.results[b]["out"] for b in range(8)])


# version 20
# speedup vs baseline: 1.1834x; 1.1834x over previous
import numpy as np
import ml_dtypes
from contextlib import ExitStack

import concourse.bass as bass
import concourse.mybir as mybir
from concourse.bass_utils import run_bass_kernel_spmd

F32 = mybir.dt.float32
BF16 = mybir.dt.bfloat16
I32 = mybir.dt.int32
AF = mybir.ActivationFunctionType
ALU = mybir.AluOpType

D = 1024
L = 4096
DIN = 2048
NH = 16
CDIM = 2176
DIP = 4240
G = 256
NG = L // G
Q = 128
CPG = G // Q
NEG = -1.0e30
EPS = 1e-5
BF = ml_dtypes.bfloat16


def build_consts():
    c = {}
    c["c_i128"] = np.eye(128, dtype=BF)
    c["c_i64"] = np.eye(64, dtype=BF)
    tt = np.arange(128)[None, :] - np.arange(128)[:, None]
    mfw = np.where(tt < 0, NEG, 0.0).astype(np.float32)
    mbw = np.where(tt > 0, NEG, 0.0).astype(np.float32)
    c["c_maskfw"] = np.repeat(mfw[:, None, :], 4, axis=1).astype(BF)
    c["c_maskbw"] = np.repeat(mbw[:, None, :], 4, axis=1).astype(BF)
    ind = np.zeros((48, NH, 128), dtype=BF)
    for base in (0, 16, 32):
        for h in range(NH):
            ind[base + h, h, :] = 1.0
    c["c_ind3"] = ind
    c["c_ones3"] = np.ones((35, 128), dtype=BF)
    c["c_ones128b"] = np.ones((128, 1), dtype=BF)
    c["c_ones1f"] = np.ones((1, 128), dtype=np.float32)
    return c


class P:
    def __init__(self, handles):
        self.h = handles
        self.cnt = {k: 0 for k in handles}
        self.eng_name = None
        self.eng = None

    def begin(self, name, eng):
        self.eng_name = name
        self.eng = eng
        for k in self.cnt:
            self.cnt[k] = 0

    def mark(self, sem):
        return self.cnt[sem]

    def op(self, eng_name, fn, wait=(), inc=None, dma=False):
        if self.eng_name == eng_name:
            for w in wait:
                if isinstance(w, str):
                    s, v = w, self.cnt[w]
                else:
                    s, v = w
                    if v is None:
                        v = self.cnt[s]
                self.eng.wait_ge(self.h[s], v)
            if dma and inc is not None:
                self.eng.wait_ge(self.h[inc], self.cnt[inc])
            instr = fn(self.eng)
            if inc is not None:
                instr.then_inc(self.h[inc], 16 if dma else 1)
        if inc is not None:
            self.cnt[inc] += 16 if dma else 1


SEM_NAMES = ["qs", "qc", "qg", "qw", "qu", "qx", "qy", "qd", "qo", "tv", "ts", "tp", "tg"]


def make_program(debug=0):
    state = {}

    def declare(nc):
        spk = "ExternalOutput" if debug else "Internal"
        dr = {}
        dr["u"] = nc.dram_tensor("u", [L, D], F32, kind="ExternalInput").ap()
        dr["ipw"] = nc.dram_tensor("ipw", [DIP, D], F32,
                                   kind="ExternalInput").ap()
        for nm, shp in [("cw", [CDIM, 4]), ("cb", [CDIM]),
                        ("cwb", [CDIM, 4]), ("cbb", [CDIM]),
                        ("dtb", [16]), ("dtbb", [16]),
                        ("alog", [16]), ("ablog", [16]),
                        ("dvec", [16]), ("dbvec", [16]),
                        ("normw", [DIN]), ("opw", [D, DIN]),
                        ("fwt", [D, D]), ("fb", [D])]:
            dr[nm] = nc.dram_tensor(nm, shp, F32, kind="ExternalInput").ap()
        for nm, shp, dt in [("c_i128", [128, 128], BF16),
                            ("c_i64", [64, 64], BF16),
                            ("c_maskfw", [128, 4, 128], BF16),
                            ("c_maskbw", [128, 4, 128], BF16),
                            ("c_ind3", [48, NH, 128], BF16),
                            ("c_ones3", [35, 128], BF16),
                            ("c_ones128b", [128, 1], BF16),
                            ("c_ones1f", [1, 128], F32)]:
            dr[nm] = nc.dram_tensor(nm, shp, dt, kind="ExternalInput").ap()
        dr["out"] = nc.dram_tensor("out", [L, D], F32, kind="ExternalOutput").ap()
        if debug:
            dr["dbg"] = nc.dram_tensor("dbg", [8, 128, 4096], BF16,
                                       kind="ExternalOutput").ap()
        dr["wbfA"] = nc.dram_tensor("wbfA", [2192, D], BF16, kind="Internal").ap()
        dr["wbfZ"] = nc.dram_tensor("wbfZ", [DIN, D], BF16, kind="Internal").ap()
        dr["wbfO"] = nc.dram_tensor("wbfO", [D, DIN], BF16, kind="Internal").ap()
        dr["wbfF"] = nc.dram_tensor("wbfF", [D, D], BF16, kind="Internal").ap()
        dr["sp_x"] = nc.dram_tensor("sp_x", [NG, 128, NH * G], BF16,
                                    kind=spk).ap()
        dr["sp_bc"] = nc.dram_tensor("sp_bc", [NG, 64, 2 * G], BF16,
                                     kind=spk).ap()
        dr["sp_dt"] = nc.dram_tensor("sp_dt", [NG, 16, G], F32,
                                     kind=spk).ap()
        dr["sp_y"] = nc.dram_tensor("sp_y", [NG, 128, NH * G], BF16,
                                    kind=spk).ap()
        dr["s_scr"] = nc.dram_tensor("s_scr", [16, 6, G], BF16,
                                     kind="Internal").ap()

        sbufs = {}

        def A(name, shape, dt):
            h = nc.sbuf_tensor(name, shape, dt).__enter__()
            sbufs[name] = h

        def PS(name, shape, dt):
            h = nc.psum_tensor(name, shape, dt).__enter__()
            sbufs[name] = h

        A("warena", [128, 40960], BF16)
        A("wstage", [128, 2, D], BF16)
        A("stgX", [128, NH, G], BF16)
        A("stgBC", [128, 2, G], BF16)
        A("haloX", [128, NH, 8], BF16)
        A("haloBC", [128, 2, 8], BF16)
        A("convX", [128, NH, G], BF16)
        A("convBC", [128, 2, G], BF16)
        A("u2", [128, D], F32)
        A("ur", [128, D], F32)
        A("um", [128, D], F32)
        A("mv", [128, 8], F32)
        A("bns", [128, 12], F32)
        A("xnT", [128, 8, G], BF16)
        A("xdT", [128, 8, G], BF16)
        A("xn2", [128, 2, D], BF16)
        A("xd2", [128, 2, D], BF16)
        A("Xtm", [128, NH * CPG, 128], BF16)
        A("Btm", [128, CPG, 64], BF16)
        A("dtraw", [16, G], F32)
        A("dt_t", [16, G], F32)
        A("sp1", [16, G], F32)
        A("sp2", [16, G], F32)
        A("a_hm", [16, G], F32)
        A("s_hm", [16, G], F32)
        A("lndt", [16, G], F32)
        A("ones16", [16, Q], F32)
        A("spl6", [16, 6, G], BF16)
        A("b3c", [48, G], BF16)
        A("sflat", [35, 16, 128], BF16)
        A("cbt", [128, 128], BF16)
        A("mtq", [128, 1024], BF16)
        A("gq", [64, 1024], BF16)
        A("gcq", [64, 1024], BF16)
        A("bwq", [128, 512], BF16)
        A("diagq", [64, 512], BF16)
        A("hT0", [64, NH * 128], BF16)
        A("ybw", [128, NH, G], BF16)
        A("siluz", [128, NH, G], BF16)
        A("gff", [128, 2, D], BF16)
        A("ostg", [128, 512], F32)
        A("rrow", [1, 2, G], BF16)
        A("rbc", [128, 2, G], BF16)
        A("qv8", [128, 512], F32)
        A("qau", [128, 512], BF16)
        A("qm1", [128, 512], BF16)
        A("qm2", [128, 512], BF16)
        A("qsc", [128, 16], F32)
        A("qsc2", [128, 16], F32)
        A("qrcp", [128, 16], F32)
        A("i128", [128, 128], BF16)
        A("i64", [64, 64], BF16)
        A("maskst", [128, 4, 128], BF16)
        A("ind3", [48, NH, 128], BF16)
        A("ones3", [35, 128], BF16)
        A("ones128b", [128, 1], BF16)
        A("ones1f", [1, 128], F32)
        A("negA", [16, 1], F32)
        A("dtbias", [16, 1], F32)
        A("ddiag", [128, NH * 128], BF16)
        A("dbc", [128, 16], F32)
        A("cwX", [128, NH, 4], F32)
        A("cbX", [128, NH], F32)
        A("cwB", [64, 4], F32)
        A("cwBC2", [64, 2, 4], F32)
        A("cbB", [64, 1], F32)
        A("cwC", [64, 4], F32)
        A("cbC", [64, 1], F32)
        A("vrow16", [1, 16], F32)
        A("nwtile", [128, 16], F32)
        A("epsb", [128, 1], F32)
        A("oneb", [128, 1], F32)
        sbufs["ystg"] = sbufs["convX"]
        sbufs["t16a"] = sbufs["sp1"]
        sbufs["t16b"] = sbufs["sp2"]
        sbufs["biasf"] = sbufs["lndt"]
        PS("ps", [128, 4096], F32)
        return dr, sbufs

    def prog(nc, p):
        shared = {}
        if "dr" not in state:
            state["dr"], state["sb"] = declare(nc)
            return
        dr = state["dr"]
        sb = state["sb"]
        ps = sb["ps"]

        PB_MM = [0, 512]
        PB_D = [1024, 1536]
        PB_Y = [2048, 2560]
        PB_S = 3072
        PB_X = 3584

        def mm(out_ap, lhsT, rhs, start, stop, wait=(), inc="tp"):
            p.op("tensor",
                 lambda e: e.matmul(out_ap, lhsT, rhs, start=start, stop=stop),
                 wait=wait, inc=inc)

        def ld(dst, src, wait=(), inc="qc", q="scalar"):
            p.op(q, lambda e: e.dma_start(out=dst, in_=src),
                 wait=wait, inc=inc, dma=True)

        def load_consts():
            ld(sb["i128"][:], dr["c_i128"], inc="qw")
            ld(sb["i64"][:], dr["c_i64"], inc="qw")
            ld(sb["ind3"][:], dr["c_ind3"], inc="qw")
            ld(sb["ones3"][:], dr["c_ones3"], inc="qw")
            ld(sb["ones128b"][:], dr["c_ones128b"], inc="qw")
            ld(sb["ones1f"][:], dr["c_ones1f"], inc="qw")

        def wT_build(src_rows_ap, nrows, wbf, dst_ap, xw=(), bounce=True,
                     bb=None):
            full = (nrows // 128) * 128
            if not bounce:
                sv = p.mark("qw")
                p.op("sync",
                     lambda e: e.dma_start(out=dst_ap, in_=wbf[0:nrows, :],
                                           transpose=True),
                     wait=[("qw", sv)] + list(xw), inc="qs", dma=True)
                return
            nt = nrows // 128
            rem = nrows - full
            bbt = 2 if bb is None else bb.shape[1]
            for c0 in range(0, nt, bbt):
                cn = min(bbt, nt - c0)
                buf = (sb["wstage"] if bb is None else bb)[:, 0:cn, :]
                wv0 = p.mark("qw")
                p.op("gpsimd",
                     lambda e, c0=c0, cn=cn, buf=buf: e.dma_start(
                         out=buf,
                         in_=src_rows_ap[c0 * 128:(c0 + cn) * 128, :]
                         .rearrange("(t p) c -> p t c", p=128)),
                     wait=[("qw", wv0)], inc="qg", dma=True)
                wv = p.mark("qg")
                p.op("scalar",
                     lambda e, c0=c0, cn=cn, buf=buf: e.dma_start(
                         out=wbf[c0 * 128:(c0 + cn) * 128, :]
                         .rearrange("(t p) c -> p t c", p=128),
                         in_=buf),
                     wait=[("qg", wv)], inc="qw", dma=True)
            if rem:
                wv0 = p.mark("qw")
                p.op("gpsimd",
                     lambda e: e.dma_start(out=sb["wstage"][0:rem, 0, :],
                                           in_=src_rows_ap[full:nrows, :]),
                     wait=[("qw", wv0)], inc="qg", dma=True)
                wv = p.mark("qg")
                p.op("scalar",
                     lambda e: e.dma_start(out=wbf[full:nrows, :],
                                           in_=sb["wstage"][0:rem, 0, :]),
                     wait=[("qg", wv)], inc="qw", dma=True)
            if dst_ap is None:
                return
            sv = p.mark("qw")
            p.op("sync",
                 lambda e: e.dma_start(out=dst_ap, in_=wbf[0:nrows, :],
                                       transpose=True),
                 wait=[("qw", sv)] + list(xw), inc="qs", dma=True)

        def dir_params(pref):
            alog = dr["alog"] if pref == "fw" else dr["ablog"]
            dtbv = dr["dtb"] if pref == "fw" else dr["dtbb"]
            dvv = dr["dvec"] if pref == "fw" else dr["dbvec"]
            cwv = dr["cw"] if pref == "fw" else dr["cwb"]
            cbv = dr["cb"] if pref == "fw" else dr["cbb"]
            wv = [("ts", None), ("tv", None)]
            ld(sb["negA"][:], alog.unsqueeze(1), wait=wv, inc="qw")
            ld(sb["dtbias"][:], dtbv.unsqueeze(1), inc="qw")
            ld(sb["vrow16"][:], dvv.unsqueeze(0), inc="qw")
            ld(sb["cwX"][:], cwv[0:2048, :].rearrange("(t p) k -> p t k", p=128),
               inc="qw")
            p.op("scalar", lambda e: e.dma_start(
                out=sb["cbX"][:],
                in_=cbv[0:2048].rearrange("(t p) -> p t", p=128),
                allow_slow_non_contiguous=True), inc="qw", dma=True)
            ld(sb["cwB"][:], cwv[2048:2112, :], inc="qw")
            ld(sb["cbB"][:], cbv[2048:2112].unsqueeze(1), inc="qw")
            ld(sb["cwC"][:], cwv[2112:2176, :], inc="qw")
            ld(sb["cbC"][:], cbv[2112:2176].unsqueeze(1), inc="qw")
            ld(sb["maskst"][:], dr["c_maskfw" if pref == "fw" else "c_maskbw"],
               inc="qw")
            ld(sb["cwBC2"][0:64, 0, :], cwv[2048:2112, :], inc="qw")
            ld(sb["cwBC2"][0:64, 1, :], cwv[2112:2176, :], inc="qw")
            lv = p.mark("qw")
            p.op("scalar",
                 lambda e: e.activation(sb["negA"][:], sb["negA"][:], AF.Exp),
                 wait=[("qw", lv)], inc="ts")
            p.op("vector",
                 lambda e: e.tensor_scalar(sb["negA"][:], sb["negA"][:],
                                           -1.0, None, op0=ALU.mult),
                 wait=["ts"], inc="tv")
            mm(ps[:, PB_X:PB_X + 16], sb["ones1f"][:], sb["vrow16"][:],
               True, True, wait=[("qw", lv), "tv", "ts"])
            p.op("vector",
                 lambda e: e.tensor_copy(sb["dbc"][:], ps[:, PB_X:PB_X + 16]),
                 wait=["tp"], inc="tv")
            p.op("vector",
                 lambda e: e.tensor_tensor(
                     sb["ddiag"][:].rearrange("p (h q) -> p h q", h=NH),
                     sb["i128"][:].unsqueeze(1).broadcast_to(
                         [128, NH, 128]),
                     sb["dbc"][:].unsqueeze(2).broadcast_to(
                         [128, NH, 128]),
                     op=ALU.mult),
                 wait=["tv"], inc="tv")

        def ln_tile(s_, want_um):
            uap = sb["u2"][:]
            xnt = sb["xn2"][:, s_, :]
            p.op("vector", lambda e: e.bn_stats(sb["bns"][:, 0:6], uap[:, 0:512]),
                 wait=["qu", "tv"], inc="tv")
            p.op("vector", lambda e: e.bn_stats(sb["bns"][:, 6:12], uap[:, 512:1024]),
                 wait=["tv"], inc="tv")
            p.op("vector", lambda e: e.bn_aggr(sb["mv"][:, 0:2], sb["bns"][:]),
                 wait=["tv"], inc="tv")
            mvv = p.mark("tv")
            p.op("scalar",
                 lambda e: e.activation(sb["mv"][:, 2:3], sb["mv"][:, 1:2],
                                        AF.Ln, bias=sb["epsb"][:, 0:1]),
                 wait=[("tv", mvv), "ts"], inc="ts")
            p.op("scalar",
                 lambda e: e.activation(sb["mv"][:, 3:4], sb["mv"][:, 2:3],
                                        AF.Exp, scale=-0.5),
                 wait=["ts"], inc="ts")
            rsv = p.mark("ts")
            if want_um:
                p.op("vector",
                     lambda e: e.tensor_scalar(sb["um"][:], uap,
                                               sb["mv"][:, 0:1], None,
                                               op0=ALU.subtract),
                     wait=[("tv", mvv)], inc="tv")
                p.op("vector",
                     lambda e: e.tensor_scalar(xnt, sb["um"][:],
                                               sb["mv"][:, 3:4], None,
                                               op0=ALU.mult),
                     wait=[("ts", rsv), "tv", "qs"], inc="tv")
            else:
                p.op("vector",
                     lambda e: e.tensor_scalar(xnt, uap,
                                               sb["mv"][:, 0:1],
                                               sb["mv"][:, 3:4],
                                               op0=ALU.subtract, op1=ALU.mult),
                     wait=[("ts", rsv), ("tv", mvv), "qs"], inc="tv")

        def conv_dir(dirn, rdy, tmk=None, halo_fn=None):
            stX, stBC = sb["stgX"], sb["stgBC"]
            cvX, cvBC = sb["convX"], sb["convBC"]
            scr = sb["qv8"][:].bitcast(BF16)
            n_main = G - 3

            def sls(tap):
                if dirn == "fw":
                    return (slice(3, G), slice(tap, tap + n_main),
                            slice(0, 3), slice(tap, tap + 3))
                return (slice(0, G - 3), slice(3 - tap, 3 - tap + n_main),
                        slice(G - 3, G), slice(3 - tap, 3 - tap + 3))
            osl0, isl0, esl0, hsl0 = sls(0)

            def tap0_tile(i, w1):
                p.op("vector",
                     lambda e, i=i: e.tensor_scalar(
                         cvX[:, i, osl0], stX[:, i, isl0],
                         sb["cwX"][:, i, 0:1], sb["cbX"][:, i:i + 1],
                         op0=ALU.mult, op1=ALU.add),
                     wait=w1, inc="tv")
                p.op("vector",
                     lambda e, i=i: e.tensor_scalar(
                         cvX[:, i, esl0], sb["haloX"][:, i, hsl0],
                         sb["cwX"][:, i, 0:1], sb["cbX"][:, i:i + 1],
                         op0=ALU.mult, op1=ALU.add),
                     inc="tv")

            def taps_round(r4):
                t4 = slice(r4 * 4, r4 * 4 + 4)
                for tap in range(1, 4):
                    osl, isl, esl, hsl = sls(tap)
                    scr3 = scr[:, 0:4 * n_main].rearrange(
                        "p (i t) -> p i t", i=4)
                    p.op("vector",
                         lambda e, t4=t4, isl=isl, tap=tap, scr3=scr3:
                         e.tensor_tensor(
                             scr3, stX[:, t4, isl],
                             sb["cwX"][:, t4, tap:tap + 1].broadcast_to(
                                 [128, 4, n_main]),
                             op=ALU.mult),
                         wait=["tv"], inc="tv")
                    p.op("vector",
                         lambda e, t4=t4, osl=osl, scr3=scr3:
                         e.tensor_tensor(
                             cvX[:, t4, osl], cvX[:, t4, osl], scr3,
                             op=ALU.add),
                         wait=["tv"], inc="tv")

            for r4 in range(4):
                for i in range(r4 * 4, r4 * 4 + 4):
                    w1 = list(rdy) + ["tv", "qy"] if (r4 == 0 and i == 0) \
                        else ()
                    if tmk is not None:
                        w1 = ([("ts", tmk[0][i])] + (["tv", "qy"]
                              if i == 0 else []))
                    tap0_tile(i, w1)
                taps_round(r4)
            for bc, wv_, bv_ in ((0, sb["cwB"], sb["cbB"]),
                                 (1, sb["cwC"], sb["cbC"])):
                w1 = [("ts", tmk[1])] if (tmk is not None and bc == 0) else ()
                p.op("vector",
                     lambda e, bc=bc, wv_=wv_, bv_=bv_: e.tensor_scalar(
                         cvBC[0:64, bc, osl0], stBC[0:64, bc, isl0],
                         wv_[:, 0:1], bv_[:, 0:1],
                         op0=ALU.mult, op1=ALU.add),
                     wait=w1, inc="tv" if bc == 1 else None)
                p.op("vector",
                     lambda e, bc=bc, wv_=wv_, bv_=bv_: e.tensor_scalar(
                         cvBC[0:64, bc, esl0], sb["haloBC"][0:64, bc, hsl0],
                         wv_[:, 0:1], bv_[:, 0:1],
                         op0=ALU.mult, op1=ALU.add),
                     inc="tv" if bc == 1 else None)
            p.op("vector", lambda e: e.nop(), wait=["tv"])
            if halo_fn is not None:
                halo_fn()
            for tap in range(1, 4):
                osl, isl, esl, hsl = sls(tap)
                scr2 = scr[:, 0:2 * n_main].rearrange("p (i t) -> p i t", i=2)
                p.op("vector",
                     lambda e, isl=isl, tap=tap, scr2=scr2: e.tensor_tensor(
                         scr2[0:64, :, :], stBC[0:64, :, isl],
                         sb["cwBC2"][0:64, :, tap:tap + 1].broadcast_to(
                             [64, 2, n_main]),
                         op=ALU.mult),
                     wait=["tv"], inc="tv")
                p.op("vector",
                     lambda e, osl=osl, scr2=scr2: e.tensor_tensor(
                         cvBC[0:64, :, osl], cvBC[0:64, :, osl],
                         scr2[0:64, :, :], op=ALU.add),
                     wait=["tv"], inc="tv")
            for tap in range(1, 4):
                osl, isl, esl, hsl = sls(tap)
                scr3 = scr[:, 0:NH * 3].rearrange("p (i t) -> p i t", i=NH)
                p.op("vector",
                     lambda e, hsl=hsl, tap=tap, scr3=scr3: e.tensor_tensor(
                         scr3, sb["haloX"][:, :, hsl],
                         sb["cwX"][:, :, tap:tap + 1].broadcast_to(
                             [128, NH, 3]),
                         op=ALU.mult),
                     wait=["tv"], inc="tv")
                p.op("vector",
                     lambda e, esl=esl, scr3=scr3: e.tensor_tensor(
                         cvX[:, :, esl], cvX[:, :, esl], scr3, op=ALU.add),
                     wait=["tv"], inc="tv")
                scr2 = scr[:, 512:512 + 2 * 3].rearrange(
                    "p (i t) -> p i t", i=2)
                p.op("vector",
                     lambda e, hsl=hsl, tap=tap, scr2=scr2: e.tensor_tensor(
                         scr2[0:64, :, :], sb["haloBC"][0:64, :, hsl],
                         sb["cwBC2"][0:64, :, tap:tap + 1].broadcast_to(
                             [64, 2, 3]),
                         op=ALU.mult),
                     wait=["tv"], inc="tv")
                p.op("vector",
                     lambda e, esl=esl, scr2=scr2: e.tensor_tensor(
                         cvBC[0:64, :, esl], cvBC[0:64, :, esl],
                         scr2[0:64, :, :], op=ALU.add),
                     wait=["tv"], inc="tv")
            cvv = p.mark("tv")
            p.op("scalar",
                 lambda e: e.activation(
                     cvX[:].rearrange("p i t -> p (i t)"),
                     cvX[:].rearrange("p i t -> p (i t)"), AF.Silu),
                 wait=[("tv", cvv), "tg", "ts"], inc="ts")
            p.op("scalar",
                 lambda e: e.activation(
                     cvBC[0:64, :, :].rearrange("p i t -> p (i t)"),
                     cvBC[0:64, :, :].rearrange("p i t -> p (i t)"),
                     AF.Silu),
                 wait=["ts"], inc="ts")

        def dt_pipeline(dirn, rdy, prev):
            v = sb
            p.op("vector",
                 lambda e: e.tensor_scalar(v["dt_t"][:], v["dtraw"][:],
                                           v["dtbias"][:, 0:1], None,
                                           op0=ALU.add),
                 wait=rdy + ["tv"], inc="tv")
            t0 = p.mark("tv")
            p.op("scalar",
                 lambda e: e.activation(v["sp1"][:], v["dt_t"][:], AF.Abs),
                 wait=[("tv", t0), "ts"], inc="ts")
            p.op("scalar",
                 lambda e: e.activation(v["sp1"][:], v["sp1"][:], AF.Exp,
                                        scale=-1.0),
                 wait=["ts"], inc="ts")
            p.op("scalar",
                 lambda e: e.activation(v["sp1"][:], v["sp1"][:], AF.Ln,
                                        bias=sb["oneb"][0:16, 0:1]),
                 wait=["ts"], inc="ts")
            p.op("scalar",
                 lambda e: e.activation(v["sp2"][:], v["dt_t"][:], AF.Relu),
                 wait=["ts"], inc="ts")
            spv = p.mark("ts")
            p.op("vector",
                 lambda e: e.tensor_tensor(v["dt_t"][:], v["sp1"][:],
                                           v["sp2"][:], op=ALU.add),
                 wait=[("ts", spv)], inc="tv")
            dtv = p.mark("tv")
            p.op("scalar",
                 lambda e: e.activation(v["lndt"][:], v["dt_t"][:], AF.Ln),
                 wait=[("tv", dtv)], inc="ts")
            lnv = p.mark("ts")
            p.op("vector",
                 lambda e: e.tensor_scalar(v["a_hm"][:], v["dt_t"][:],
                                           v["negA"][:, 0:1], None,
                                           op0=ALU.mult),
                 wait=[("tv", dtv)], inc="tv")
            av = p.mark("tv")
            for c in range(CPG):
                sl = slice(c * Q, (c + 1) * Q)
                p.op("vector",
                     lambda e, sl=sl: e.tensor_tensor_scan(
                         v["t16a"][:, sl], v["ones16"][:, 0:Q],
                         v["a_hm"][:, sl], 0.0, op0=ALU.mult, op1=ALU.add),
                     wait=[("tv", av)] if c == 0 else ["tv"], inc="tv")
            pv = p.mark("tv")
            if dirn == "fw":
                p.op("vector",
                     lambda e: e.tensor_copy(v["s_hm"][:], v["t16a"][:]),
                     wait=[("tv", pv)], inc="tv")
            else:
                for c in range(CPG):
                    sl = slice(c * Q, (c + 1) * Q)
                    last = (c + 1) * Q - 1
                    p.op("vector",
                         lambda e, sl=sl, last=last: e.tensor_scalar(
                             v["t16b"][:, sl], v["t16a"][:, sl],
                             -1.0, v["t16a"][:, last:last + 1],
                             op0=ALU.mult, op1=ALU.add),
                         wait=[("tv", pv)] if c == 0 else ["tv"], inc="tv")
                p.op("vector",
                     lambda e: e.tensor_tensor(v["s_hm"][:], v["t16b"][:],
                                               v["a_hm"][:], op=ALU.add),
                     wait=["tv"], inc="tv")
            sv = p.mark("tv")
            p.op("vector",
                 lambda e: e.tensor_tensor(v["lndt"][:], v["lndt"][:],
                                           v["s_hm"][:], op=ALU.subtract),
                 wait=[("tv", sv), ("ts", lnv)], inc="tv")
            p.op("vector",
                 lambda e: e.tensor_copy(v["spl6"][:, 0, :], v["s_hm"][:]),
                 wait=[("tv", sv), "tv"], inc="tv")
            p.op("vector",
                 lambda e: e.tensor_tensor(v["t16a"][:], v["s_hm"][:],
                                           v["spl6"][:, 0, :],
                                           op=ALU.subtract),
                 wait=["tv"], inc="tv")
            p.op("vector",
                 lambda e: e.tensor_copy(v["spl6"][:, 1, :], v["t16a"][:]),
                 wait=["tv"], inc="tv")
            p.op("vector",
                 lambda e: e.tensor_tensor(v["t16b"][:], v["t16a"][:],
                                           v["spl6"][:, 1, :],
                                           op=ALU.subtract),
                 wait=["tv"], inc="tv")
            p.op("vector",
                 lambda e: e.tensor_copy(v["spl6"][:, 2, :], v["t16b"][:]),
                 wait=["tv"], inc="tv")
            p.op("vector",
                 lambda e: e.tensor_copy(v["spl6"][:, 3, :], v["lndt"][:]),
                 wait=["tv"], inc="tv")
            p.op("vector",
                 lambda e: e.tensor_tensor(v["t16a"][:], v["lndt"][:],
                                           v["spl6"][:, 3, :],
                                           op=ALU.subtract),
                 wait=["tv"], inc="tv")
            p.op("vector",
                 lambda e: e.tensor_copy(v["spl6"][:, 4, :], v["t16a"][:]),
                 wait=["tv"], inc="tv")
            p.op("vector",
                 lambda e: e.tensor_tensor(v["t16b"][:], v["t16a"][:],
                                           v["spl6"][:, 4, :],
                                           op=ALU.subtract),
                 wait=["tv"], inc="tv")
            p.op("vector",
                 lambda e: e.tensor_copy(v["spl6"][:, 5, :], v["t16b"][:]),
                 wait=["tv"], inc="tv")
            spl = p.mark("tv")
            p.op("scalar",
                 lambda e: e.dma_start(out=dr["s_scr"], in_=v["spl6"][:]),
                 wait=[("tv", spl)], inc="qd", dma=True)
            sv2 = p.mark("qd")
            p.op("scalar",
                 lambda e: e.dma_start(
                     out=sb["sflat"][0:3, :, :],
                     in_=dr["s_scr"][:, 0:3, 0:128].rearrange(
                         "h j t -> j h t")),
                 wait=[("qd", sv2), ("tp", prev["tp"])], inc="qd", dma=True)
            p.op("scalar",
                 lambda e: e.dma_start(
                     out=sb["sflat"][32:35, :, :],
                     in_=dr["s_scr"][:, 0:3, 128:256].rearrange(
                         "h j t -> j h t")),
                 inc="qd", dma=True)
            for j_ in range(3):
                p.op("scalar",
                     lambda e, j_=j_: e.dma_start(
                         out=sb["b3c"][16 * j_:16 * (j_ + 1), :],
                         in_=dr["s_scr"][:, 3 + j_, :]),
                     inc="qd", dma=True)

        def scan_chunk(dirn, c, hTa, hTb, q_rot, tvm):
            anchor = (Q - 1) if dirn == "fw" else 0
            csl = slice(c * Q, (c + 1) * Q)
            v = sb
            urb = v["ur"][:].bitcast(BF16)
            ostb = v["ostg"][:].bitcast(BF16)
            mtq2 = urb[:, 0:1024]
            gq2 = urb[0:64, 1024:2048]
            bwq2 = ostb[:, 0:512]
            diagq2 = ostb[0:64, 512:1024]
            mm(ps[:, 3072:3072 + 128], v["convBC"][0:64, 0, csl],
               v["convBC"][0:64, 1, csl], True, True,
               wait=["ts", ("tv", tvm), "qd", "qy"])
            p.op("scalar",
                 lambda e: e.copy(v["cbt"][:], ps[:, 3072:3072 + 128]),
                 wait=["tp"], inc="ts")
            cbv = p.mark("ts")
            ph = []
            for hq in range(2):
                h0 = hq * 8
                pd = 1024 + 1024 * (q_rot[0] % 2)
                q_rot[0] += 1
                dsl = slice(pd, pd + 1024)
                cb0 = 32 * c
                mtq_h = v["mtq"][:] if hq == 0 else mtq2
                gq_h = v["gq"][:] if hq == 0 else gq2
                bwq_h = v["bwq"][:] if hq == 0 else bwq2
                diagq_h = v["diagq"][:] if hq == 0 else diagq2
                mm(ps[0:64, pd:pd + 512], v["ones3"][cb0:cb0 + 3, 0:64],
                   v["sflat"][cb0:cb0 + 3, h0:h0 + 4, :], True, True,
                   wait=["qd", "ts", ("tv", tvm)])
                mm(ps[0:64, pd + 512:pd + 1024],
                   v["ones3"][cb0:cb0 + 3, 0:64],
                   v["sflat"][cb0:cb0 + 3, h0 + 4:h0 + 8, :], True, True)
                t1v = p.mark("tp")
                p.op("scalar",
                     lambda e, gq_h=gq_h, pd=pd: e.activation(
                         gq_h, ps[0:64, pd:pd + 1024], AF.Exp),
                     wait=[("tp", t1v), "ts", ("tv", tvm)], inc="ts")
                gv = p.mark("ts")
                for nn in range(2):
                    nsl = slice(pd + nn * 512, pd + nn * 512 + 512)
                    mm(ps[:, nsl], v["ones3"][cb0:cb0 + 3, :],
                       v["sflat"][cb0:cb0 + 3, h0 + nn * 4:h0 + nn * 4 + 4, :],
                       True, False,
                       wait=[("ts", gv)] if nn == 0 else ())
                    mm(ps[:, nsl], v["b3c"][:, csl],
                       v["ind3"][:, h0 + nn * 4:h0 + nn * 4 + 4, :],
                       False, False)
                    mm(ps[:, nsl], v["i128"][:],
                       v["maskst"][:], False, True)
                t2v = p.mark("tp")
                p.op("scalar",
                     lambda e, mtq_h=mtq_h, dsl=dsl: e.activation(
                         mtq_h, ps[:, dsl], AF.Exp),
                     wait=[("tp", t2v), "tv"], inc="ts")
                mtv = p.mark("ts")
                p.op("vector",
                     lambda e, bwq_h=bwq_h, mtq_h=mtq_h: e.tensor_tensor(
                         bwq_h.rearrange("p (h n) -> p h n", h=8),
                         v["Btm"][:, c, :].unsqueeze(1).broadcast_to(
                             [128, 8, 64]),
                         mtq_h.rearrange("p (h t) -> p h t", h=8)[
                             :, :, anchor:anchor + 1].broadcast_to(
                                 [128, 8, 64]),
                         op=ALU.mult),
                     wait=[("ts", mtv), ("ts", cbv), "tv", "qo"],
                     inc="tv")
                p.op("vector",
                     lambda e, diagq_h=diagq_h, gq_h=gq_h: e.tensor_tensor(
                         diagq_h.rearrange("p (h n) -> p h n", h=8),
                         v["i64"][:].unsqueeze(1).broadcast_to([64, 8, 64]),
                         gq_h.rearrange("p (h t) -> p h t", h=8)[
                             :, :, anchor:anchor + 1].broadcast_to(
                                 [64, 8, 64]),
                         op=ALU.mult),
                     wait=["tv"], inc="tv")
                if hq == 0:
                    p.op("vector",
                         lambda e: e.tensor_tensor(
                             v["gcq"][:].rearrange("p (h t) -> p h t", h=8),
                             v["gq"][:].rearrange("p (h t) -> p h t", h=8),
                             v["convBC"][0:64, 1, csl].unsqueeze(1)
                             .broadcast_to([64, 8, 128]),
                             op=ALU.mult),
                         wait=["tv"], inc="tv")
                p.op("vector",
                     lambda e, mtq_h=mtq_h: e.tensor_tensor(
                         mtq_h.rearrange("p (h t) -> p h t", h=8),
                         mtq_h.rearrange("p (h t) -> p h t", h=8),
                         v["cbt"][:].unsqueeze(1).broadcast_to(
                             [128, 8, 128]),
                         op=ALU.mult),
                     wait=["tv"], inc="tv")
                p.op("vector",
                     lambda e, mtq_h=mtq_h, h0=h0: e.tensor_tensor(
                         mtq_h, mtq_h,
                         v["ddiag"][:, h0 * 128:(h0 + 8) * 128], op=ALU.add),
                     wait=["tv"], inc="tv")
                opv = p.mark("tv")
                ph.append((h0, pd, mtq_h, gq_h, bwq_h, diagq_h, opv))
            for hq in range(2):
                h0, pd, mtq_h, gq_h, bwq_h, diagq_h, opv = ph[hq]
                if hq == 1:
                    ymv0 = p.mark("tp")
                    p.op("vector",
                         lambda e, gq_h=gq_h: e.tensor_tensor(
                             v["gcq"][:].rearrange("p (h t) -> p h t", h=8),
                             gq_h.rearrange("p (h t) -> p h t", h=8),
                             v["convBC"][0:64, 1, csl].unsqueeze(1)
                             .broadcast_to([64, 8, 128]),
                             op=ALU.mult),
                         wait=[("tp", ymv0), "tv"], inc="tv")
                    opv = p.mark("tv")
                for half4 in range(2):
                    sev = p.mark("ts")
                    for hh4 in range(4):
                        hh = half4 * 4 + hh4
                        h = h0 + hh
                        ysl = slice(pd + hh * 128, pd + (hh + 1) * 128)
                        xop = v["Xtm"][:, h * CPG + c, :]
                        w0 = ()
                        if hh == 0:
                            w0 = [("tv", opv), "ts"]
                        elif hh4 == 0:
                            w0 = [("ts", sev)]
                        mm(ps[:, ysl], xop,
                           mtq_h[:, hh * 128:(hh + 1) * 128], True, False,
                           wait=w0)
                        mm(ps[:, ysl], hTa[:, h * 128:(h + 1) * 128],
                           v["gcq"][:, hh * 128:(hh + 1) * 128], False, True)
                        ssl = slice(3584 + hh4 * 128, 3584 + (hh4 + 1) * 128)
                        mm(ps[0:64, ssl], bwq_h[:, hh * 64:(hh + 1) * 64],
                           xop, True, False)
                        mm(ps[0:64, ssl],
                           diagq_h[:, hh * 64:(hh + 1) * 64],
                           hTa[:, h * 128:(h + 1) * 128], False, True)
                    smv = p.mark("tp")
                    p.op("scalar",
                         lambda e, h0=h0, half4=half4: e.copy(
                             hTb[:, (h0 + half4 * 4) * 128:
                                 (h0 + half4 * 4 + 4) * 128],
                             ps[0:64, 3584:3584 + 512]),
                         wait=[("tp", smv)], inc="ts")
                ymv = p.mark("tp")
                p.op("scalar",
                     lambda e, h0=h0, pd=pd, csl=csl: e.copy(
                         sb["ystg"][:, h0:h0 + 8, csl],
                         ps[:, pd:pd + 1024].rearrange(
                             "p (h t) -> p h t", h=8)),
                     wait=[("tp", ymv)], inc="ts")

        def inproj_mtile(wT, mofs, msz, evac_fn, xv, rot, extra_wait=()):
            idx = rot["n"] % 2
            pb = PB_MM[idx]
            rot["n"] += 1
            w0 = [("qs", xv), ("ts", rot["mk"][idx])] + list(extra_wait)
            for k in range(8):
                mm(ps[0:msz, pb:pb + G], wT[:, k, mofs:mofs + msz],
                   sb["xnT"][:, k, :], k == 0, k == 7,
                   wait=w0 if k == 0 else ())
            mv_ = p.mark("tp")
            evac_fn(ps[0:msz, pb:pb + G], mv_)
            rot["mk"][idx] = p.mark("ts")

        def pass_A():
            p.op("vector", lambda e: e.memset(sb["epsb"][:], EPS), inc="tv")
            p.op("vector", lambda e: e.memset(sb["oneb"][:], 1.0), inc="tv")
            load_consts()
            wTA = sb["warena"][:, 0:8 * 2192].rearrange(
                "p (k m) -> p k m", k=8)
            bbA = sb["warena"][:, 23552:40960].rearrange(
                "p (t c) -> p t c", t=17)
            wT_build(dr["ipw"][2048:4240, :], 2192, dr["wbfA"], wTA, bb=bbA)
            dir_params("bw")
            wTv = p.mark("qs")
            p.op("vector", lambda e: e.memset(sb["haloX"][:], 0.0),
                 wait=["tv"], inc="tv")
            p.op("vector", lambda e: e.memset(sb["haloBC"][:], 0.0), inc="tv")
            p.op("vector", lambda e: e.memset(sb["hT0"][:], 0.0), inc="tv")
            p.op("vector", lambda e: e.memset(sb["ones16"][:], 1.0), inc="tv")
            mk_st = None
            for i in range(8):
                ld(sb["um"][:], dr["fwt"][i * 128:(i + 1) * 128, :],
                   wait=["tv"], inc="qw")
                p.op("vector", lambda e: e.nop(), wait=["qw"])
                dwl = [("qw", mk_st)] if mk_st is not None else ()
                quantize_tile(sb["xd2"][:, 0, :], rst_is_one=True,
                              dst_wait=dwl)
                qv = p.mark("tv")
                p.op("scalar",
                     lambda e, i=i: e.dma_start(
                         out=dr["wbfF"][i * 128:(i + 1) * 128, :],
                         in_=sb["xd2"][:, 0, :]),
                     wait=[("tv", qv)], inc="qw", dma=True)
                mk_st = p.mark("qw")
            shared["qfv"] = p.mark("qw")
            q_rot = [0, 0]
            rot = {"n": 0, "mk": [0, 0]}

            def pro(t0):
                for s_ in range(2):
                    ld(sb["u2"][:],
                       dr["u"][t0 + s_ * 128:t0 + (s_ + 1) * 128, :],
                       wait=["tv", "qo"], inc="qu", q="gpsimd")
                    ln_tile(s_, False)
                    xnv = p.mark("tv")
                    tpv = p.mark("tp")
                    p.op("sync",
                         lambda e, s_=s_, xnv=xnv, tpv=tpv: e.dma_start(
                             out=sb["xnT"][:, :, s_ * 128:(s_ + 1) * 128],
                             in_=sb["xn2"][:, s_, :].rearrange(
                                 "p (k f) -> p k f", k=8),
                             transpose=True),
                         wait=[("tv", xnv), ("tp", tpv), ("qs", wTv)],
                         inc="qs", dma=True)
                return p.mark("qs")

            prev = {"tv": p.mark("tv"), "ts": p.mark("ts"),
                    "tp": p.mark("tp")}
            xv = pro((NG - 1) * G)
            for gi in range(NG - 1, -1, -1):
                if gi == 2:
                    bbo = sb["warena"][:, 32768:40960].rearrange(
                        "p (t c) -> p t c", t=4)
                    for c0 in range(0, 8, 4):
                        wv0 = p.mark("qw")
                        p.op("gpsimd",
                             lambda e, c0=c0: e.dma_start(
                                 out=bbo,
                                 in_=dr["opw"][c0 * 128:(c0 + 4) * 128, :]
                                 .rearrange("(t p) c -> p t c", p=128)),
                             wait=[("qw", wv0)], inc="qg", dma=True)
                        wv = p.mark("qg")
                        p.op("scalar",
                             lambda e, c0=c0: e.dma_start(
                                 out=dr["wbfO"][c0 * 128:(c0 + 4) * 128, :]
                                 .rearrange("(t p) c -> p t c", p=128),
                                 in_=bbo),
                             wait=[("qg", wv)], inc="qw", dma=True)
                    bbz = sb["warena"][:, 32768:40960].rearrange(
                        "p (t c) -> p t c", t=8)
                    wT_build(dr["ipw"][0:2048, :], 2048, dr["wbfZ"], None, bb=bbz)
                    p.op("scalar", lambda e: e.dma_start(
                        out=sb["nwtile"][:],
                        in_=dr["normw"].rearrange("(k f) -> f k", f=128),
                        allow_slow_non_contiguous=True), inc="qw", dma=True)
                    shared["nv"] = p.mark("qw")
                    wdTl = sb["warena"][:, 32768:40960].rearrange(
                        "p (k m) -> p k m", k=8)
                    p.op("sync",
                         lambda e: e.dma_start(out=wdTl, in_=dr["wbfF"],
                                               transpose=True),
                         wait=[("qw", None)], inc="qs", dma=True)


                def ev(dst3):
                    def f(pap, mv_):
                        p.op("scalar", lambda e: e.copy(dst3, pap),
                             wait=[("tp", mv_)], inc="ts")
                    return f

                ew = [("tv", prev["tv"]), ("qx", None)]
                inproj_mtile(wTA, 2048, 64, ev(sb["stgBC"][0:64, 0, :]),
                             xv, rot, ew)
                inproj_mtile(wTA, 2112, 64, ev(sb["stgBC"][0:64, 1, :]),
                             xv, rot, ew)
                mkBC = p.mark("ts")
                inproj_mtile(wTA, 2176, 16, ev(sb["dtraw"][:]), xv, rot, ew)
                mkX = []
                for i in range(NH):
                    inproj_mtile(wTA, i * 128, 128, ev(sb["stgX"][:, i, :]),
                                 xv, rot, ew)
                    mkX.append(p.mark("ts"))
                stv = p.mark("ts")
                if gi == 0:
                    shared["tpA"] = p.mark("tp")
                def halo_bw():
                    p.op("vector",
                         lambda e: e.tensor_copy(sb["haloX"][:, :, 0:3],
                                                 sb["stgX"][:, :, G - 3:G]),
                         wait=[("ts", stv), "tv"], inc="tv")
                    p.op("vector",
                         lambda e: e.tensor_copy(
                             sb["haloBC"][0:64, :, 0:3],
                             sb["stgBC"][0:64, :, G - 3:G]),
                         wait=["tv"], inc="tv")
                conv_dir("bw", [("ts", stv)], tmk=(mkX, mkBC),
                         halo_fn=halo_bw)
                p.op("gpsimd",
                     lambda e, gi=gi: e.dma_start(
                         out=dr["sp_x"][gi],
                         in_=sb["stgX"][:].rearrange("p i t -> p (i t)")),
                     wait=[("ts", stv)], inc="qx", dma=True)
                p.op("gpsimd",
                     lambda e, gi=gi: e.dma_start(
                         out=dr["sp_bc"][gi],
                         in_=sb["stgBC"][0:64, :, :].rearrange(
                             "p i t -> p (i t)")),
                     inc="qx", dma=True)
                p.op("gpsimd",
                     lambda e, gi=gi: e.dma_start(out=dr["sp_dt"][gi],
                                                  in_=sb["dtraw"][:]),
                     inc="qx", dma=True)
                cvv = p.mark("ts")
                p.op("sync",
                     lambda e: e.dma_start(
                         out=sb["Xtm"][:],
                         in_=sb["convX"][:].rearrange("p i t -> p (i t)")
                         .rearrange("p (m q) -> p m q", q=128),
                         transpose=True),
                     wait=[("ts", cvv), ("tp", prev["tp"])],
                     inc="qs", dma=True)
                p.op("sync",
                     lambda e: e.dma_start(
                         out=sb["Btm"][:],
                         in_=sb["convBC"][0:64, 0, :].rearrange(
                             "p (m q) -> p m q", q=128),
                         transpose=True),
                     inc="qs", dma=True)
                trv = p.mark("qs")
                dt_pipeline("bw", [("ts", stv)], prev)
                tvm = p.mark("tv")
                if gi > 0:
                    xv = pro((gi - 1) * G)
                for c in range(CPG - 1, -1, -1):
                    p.op("tensor", lambda e: e.nop(),
                         wait=[("qs", trv), ("tv", tvm), "ts"])
                    scan_chunk("bw", c, sb["hT0"][:], sb["hT0"][:], q_rot,
                               tvm)
                p.op("vector",
                     lambda e: e.tensor_copy(sb["haloX"][:, :, 3:6],
                                             sb["stgX"][:, :, 0:3]),
                     wait=["ts", "tv", "tg"], inc="tv")
                p.op("vector",
                     lambda e: e.tensor_copy(sb["haloBC"][0:64, :, 3:6],
                                             sb["stgBC"][0:64, :, 0:3]),
                     wait=["tv"], inc="tv")
                yv = p.mark("ts")
                p.op("gpsimd",
                     lambda e, gi=gi: e.dma_start(
                         out=dr["sp_y"][gi],
                         in_=sb["ystg"][:].rearrange("p i t -> p (i t)")),
                     wait=[("ts", yv), "tv"], inc="qy", dma=True)
                prev = {"tv": p.mark("tv"), "ts": p.mark("ts"),
                        "tp": p.mark("tp")}
        def quantize_tile(dst_ap, rst_is_one=False, srcb=None, dst_wait=()):
            v = sb
            srcb = v["um"] if srcb is None else srcb
            rst = v["mv"][:, 3:4]
            for h2 in range(2):
                hsl = slice(h2 * 512, (h2 + 1) * 512)
                p.op("vector",
                     lambda e, hsl=hsl: e.tensor_reduce(
                         v["qsc"][:].unsqueeze(2),
                         srcb[:, hsl].rearrange("p (b f) -> p b f", b=16),
                         axis=mybir.AxisListType.X, op=ALU.max,
                         apply_absolute_value=True),
                     wait=["tv"], inc="tv")
                if rst_is_one:
                    p.op("vector",
                         lambda e: e.tensor_scalar(v["qsc"][:], v["qsc"][:],
                                                   1.0 / 7.5, 1e-30,
                                                   op0=ALU.mult, op1=ALU.max),
                         wait=["tv"], inc="tv")
                else:
                    p.op("vector",
                         lambda e: e.tensor_scalar(v["qsc"][:], v["qsc"][:],
                                                   rst, None, op0=ALU.mult),
                         wait=["tv"], inc="tv")
                    p.op("vector",
                         lambda e: e.tensor_scalar(v["qsc"][:], v["qsc"][:],
                                                   1.0 / 7.5, 1e-30,
                                                   op0=ALU.mult, op1=ALU.max),
                         wait=["tv"], inc="tv")
                p.op("vector",
                     lambda e: e.reciprocal(v["qrcp"][:], v["qsc"][:]),
                     wait=["tv"], inc="tv")
                if rst_is_one:
                    p.op("vector",
                         lambda e: e.tensor_scalar(v["qrcp"][:], v["qrcp"][:],
                                                   8.0, None, op0=ALU.mult),
                         wait=["tv"], inc="tv")
                else:
                    p.op("vector",
                         lambda e: e.tensor_scalar(v["qrcp"][:], v["qrcp"][:],
                                                   rst, 8.0,
                                                   op0=ALU.mult, op1=ALU.mult),
                         wait=["tv"], inc="tv")
                rv = p.mark("tv")
                p.op("vector",
                     lambda e, hsl=hsl: e.tensor_tensor(
                         v["qv8"][:].rearrange("p (b f) -> p b f", b=16),
                         srcb[:, hsl].rearrange("p (b f) -> p b f", b=16),
                         v["qrcp"][:].unsqueeze(2).broadcast_to(
                             [128, 16, 32]),
                         op=ALU.mult),
                     wait=[("tv", rv)], inc="tv")
                p.op("vector",
                     lambda e: e.scalar_tensor_tensor(v["qau"][:], v["qv8"][:],
                                                      -1.0, v["qv8"][:],
                                                      op0=ALU.mult,
                                                      op1=ALU.max),
                     wait=["tv"], inc="tv")
                p.op("vector",
                     lambda e: e.tensor_scalar(v["qm1"][:], v["qau"][:], 16.0,
                                               None, op0=ALU.is_ge),
                     wait=["tv"], inc="tv")
                p.op("vector",
                     lambda e: e.tensor_scalar(v["qm2"][:], v["qau"][:], 32.0,
                                               2.0, op0=ALU.is_ge,
                                               op1=ALU.mult),
                     wait=["tv"], inc="tv")
                p.op("vector",
                     lambda e: e.tensor_tensor(v["qm2"][:], v["qm2"][:],
                                               v["qm1"][:], op=ALU.add),
                     wait=["tv"], inc="tv")
                p.op("vector",
                     lambda e: e.tensor_scalar(v["qm1"][:], v["qm2"][:],
                                               1.0, 12582912.0,
                                               op0=ALU.add, op1=ALU.mult),
                     wait=["tv"], inc="tv")
                p.op("vector",
                     lambda e: e.tensor_tensor(v["qv8"][:], v["qv8"][:],
                                               v["qm1"][:], op=ALU.add),
                     wait=["tv"], inc="tv")
                p.op("vector",
                     lambda e: e.tensor_tensor(v["qv8"][:], v["qv8"][:],
                                               v["qm1"][:], op=ALU.subtract),
                     wait=["tv"], inc="tv")
                p.op("vector",
                     lambda e: e.tensor_scalar(v["qsc2"][:], v["qsc"][:],
                                               0.125, None, op0=ALU.mult),
                     wait=["tv"], inc="tv")
                p.op("vector",
                     lambda e, hsl=hsl: e.tensor_tensor(
                         dst_ap[:, hsl].rearrange("p (b f) -> p b f", b=16),
                         v["qv8"][:].rearrange("p (b f) -> p b f", b=16),
                         v["qsc2"][:].unsqueeze(2).broadcast_to(
                             [128, 16, 32]),
                         op=ALU.mult),
                     wait=(["tv", "qs"] + (list(dst_wait) if h2 == 0
                                           else [])), inc="tv")

        def pass_B():
            wTz = sb["warena"][:, 0:16384].rearrange("p (k m) -> p k m", k=8)
            WoT = sb["warena"][:, 16384:32768].rearrange(
                "p (k m) -> p k m", k=16)
            wdT = sb["warena"][:, 32768:40960].rearrange(
                "p (k m) -> p k m", k=8)
            wT_build(None, 2048, dr["wbfZ"], wTz,
                     xw=[("tp", shared["tpA"])], bounce=False)
            dir_params("fw")
            ov2 = p.mark("qw")
            p.op("sync",
                 lambda e: e.dma_start(out=WoT, in_=dr["wbfO"],
                                       transpose=True),
                 wait=[("qw", ov2), ("tp", shared["tpA"])],
                 inc="qs", dma=True)
            wov = p.mark("qs")
            nv = shared["nv"]
            for k in range(16):
                p.op("vector",
                     lambda e, k=k: e.tensor_scalar(
                         WoT[:, k, :], WoT[:, k, :],
                         sb["nwtile"][:, k:k + 1], None, op0=ALU.mult),
                     wait=[("qs", wov), ("qw", nv), "tv"] if k == 0
                     else ["tv"], inc="tv")
            ld(sb["um"][0:1, :], dr["fb"].unsqueeze(0),
               wait=["tv"], inc="qw")
            fbv = p.mark("qw")
            for half in range(2):
                dst = PB_X if half == 0 else PB_D[0]
                mm(ps[:, dst:dst + 512], sb["ones1f"][:],
                   sb["um"][0:1, half * 512:(half + 1) * 512],
                   True, True,
                   wait=[("qw", fbv), "tv", "ts"] if half == 0 else ())
            fmv = p.mark("tp")
            fb32 = sb["wstage"][:].rearrange("p t c -> p (t c)").bitcast(F32)
            p.op("vector",
                 lambda e: e.tensor_copy(fb32[:, 0:512],
                                         ps[:, PB_X:PB_X + 512]),
                 wait=[("tp", fmv)], inc="tv")
            p.op("vector",
                 lambda e: e.tensor_copy(fb32[:, 512:1024],
                                         ps[:, PB_D[0]:PB_D[0] + 512]),
                 wait=["tv"], inc="tv")
            wdv = p.mark("qs")
            p.op("vector", lambda e: e.memset(sb["haloX"][:], 0.0),
                 wait=["ts"], inc="tv")
            p.op("vector", lambda e: e.memset(sb["haloBC"][:], 0.0), inc="tv")
            p.op("vector", lambda e: e.memset(sb["hT0"][:], 0.0), inc="tv")
            p.op("vector", lambda e: e.memset(sb["ones16"][:], 1.0), inc="tv")
            q_rot = [0, 0]
            rot = {"n": 0, "mk": [0, 0]}
            yzn_ap = sb["Xtm"][:].rearrange("p m q -> p (m q)").rearrange(
                "p (i t) -> p i t", i=NH)

            def pro(t0):
                for s_ in range(2):
                    ld(sb["u2"][:],
                       dr["u"][t0 + s_ * 128:t0 + (s_ + 1) * 128, :],
                       wait=["tv", "qo"], inc="qu", q="gpsimd")
                    ln_tile(s_, True)
                    xnv = p.mark("tv")
                    tpv = p.mark("tp")
                    p.op("sync",
                         lambda e, s_=s_, xnv=xnv, tpv=tpv: e.dma_start(
                             out=sb["xnT"][:, :, s_ * 128:(s_ + 1) * 128],
                             in_=sb["xn2"][:, s_, :].rearrange(
                                 "p (k f) -> p k f", k=8),
                             transpose=True),
                         wait=[("tv", xnv), ("tp", tpv), ("qs", wdv)],
                         inc="qs", dma=True)
                    quantize_tile(sb["xd2"][:, s_, :], rst_is_one=False)
                    qv = p.mark("tv")
                    p.op("sync",
                         lambda e, s_=s_, qv=qv: e.dma_start(
                             out=sb["xdT"][:, :, s_ * 128:(s_ + 1) * 128],
                             in_=sb["xd2"][:, s_, :].rearrange(
                                 "p (k f) -> p k f", k=8),
                             transpose=True),
                         wait=[("tv", qv)], inc="qs", dma=True)
                return p.mark("qs")

            prev = {"tv": p.mark("tv"), "ts": p.mark("ts"),
                    "tp": p.mark("tp")}
            xv = pro(0)
            for gi in range(NG):
                t0 = gi * G
                ld(sb["stgX"][:].rearrange("p i t -> p (i t)"),
                   dr["sp_x"][gi], wait=[("tv", prev["tv"])], inc="qx",
                   q="gpsimd")
                ld(sb["stgBC"][0:64, :, :].rearrange("p i t -> p (i t)"),
                   dr["sp_bc"][gi], inc="qx", q="gpsimd")
                ld(sb["dtraw"][:], dr["sp_dt"][gi],
                   wait=[("tv", prev["tv"])], inc="qx", q="gpsimd")
                ld(sb["ybw"][:].rearrange("p i t -> p (i t)"),
                   dr["sp_y"][gi], wait=[("tv", prev["tv"])], inc="qy",
                   q="gpsimd")
                spv = p.mark("qx")
                for i in range(NH):
                    idx = rot["n"] % 2
                    pb = PB_MM[idx]
                    rot["n"] += 1
                    w0 = [("qs", xv), ("ts", rot["mk"][idx]),
                          ("tv", prev["tv"])]
                    for k in range(8):
                        mm(ps[:, pb:pb + G], wTz[:, k, i * 128:(i + 1) * 128],
                           sb["xnT"][:, k, :], k == 0, k == 7,
                           wait=w0 if k == 0 else ())
                    zv = p.mark("tp")
                    p.op("scalar",
                         lambda e, i=i, pb=pb: e.activation(
                             sb["siluz"][:, i, :], ps[:, pb:pb + G], AF.Silu),
                         wait=[("tp", zv), "ts"], inc="ts")
                    rot["mk"][idx] = p.mark("ts")
                szv = p.mark("ts")
                ffbanks = [(PB_X, PB_D[0]), (PB_D[1], PB_S)]
                for s_ in range(2):
                    tsl = slice(s_ * 128, (s_ + 1) * 128)
                    for nh_ in range(2):
                        dst = ffbanks[s_][nh_]
                        for k in range(8):
                            w0 = ()
                            if k == 0 and nh_ == 0 and s_ == 0:
                                w0 = [("tv", prev["tv"]), ("ts", prev["ts"])]
                            mm(ps[:, dst:dst + 512], sb["xdT"][:, k, tsl],
                               wdT[:, k, nh_ * 512:(nh_ + 1) * 512],
                               k == 0, k == 7, wait=w0)
                ffv = p.mark("tp")
                p.op("vector",
                     lambda e: e.tensor_copy(sb["haloX"][:, :, 3:6],
                                             sb["stgX"][:, :, 0:3]),
                     wait=[("qx", spv), "tv", "tg"], inc="tv")
                p.op("vector",
                     lambda e: e.tensor_copy(sb["haloBC"][0:64, :, 3:6],
                                             sb["stgBC"][0:64, :, 0:3]),
                     wait=["tv"], inc="tv")
                conv_dir("fw", [("qx", spv)])
                cvv = p.mark("ts")
                p.op("sync",
                     lambda e: e.dma_start(
                         out=sb["Xtm"][:],
                         in_=sb["convX"][:].rearrange("p i t -> p (i t)")
                         .rearrange("p (m q) -> p m q", q=128),
                         transpose=True),
                     wait=[("ts", cvv), ("tp", prev["tp"])],
                     inc="qs", dma=True)
                p.op("sync",
                     lambda e: e.dma_start(
                         out=sb["Btm"][:],
                         in_=sb["convBC"][0:64, 0, :].rearrange(
                             "p (m q) -> p m q", q=128),
                         transpose=True),
                     inc="qs", dma=True)
                trv = p.mark("qs")
                p.op("vector",
                     lambda e: e.tensor_copy(sb["haloX"][:, :, 0:3],
                                             sb["stgX"][:, :, G - 3:G]),
                     wait=[("ts", cvv), "tv"], inc="tv")
                p.op("vector",
                     lambda e: e.tensor_copy(sb["haloBC"][0:64, :, 0:3],
                                             sb["stgBC"][0:64, :, G - 3:G]),
                     wait=["tv"], inc="tv")
                dt_pipeline("fw", [("qx", spv)], prev)
                for s_ in range(2):
                    for nh_ in range(2):
                        dst = ffbanks[s_][nh_]
                        p.op("vector",
                             lambda e, nh_=nh_, dst=dst, s_=s_:
                             e.tensor_tensor(
                                 sb["gff"][:, s_, nh_ * 512:(nh_ + 1) * 512],
                                 ps[:, dst:dst + 512],
                                 fb32[:, nh_ * 512:(nh_ + 1) * 512],
                                 op=ALU.add),
                             wait=[("tp", ffv), "tv"]
                             if (nh_ == 0 and s_ == 0) else ["tv"],
                             inc="tv")
                    gfv = p.mark("tv")
                    p.op("scalar",
                         lambda e, s_=s_: e.activation(sb["gff"][:, s_, :],
                                                       sb["gff"][:, s_, :],
                                                       AF.Silu),
                         wait=[("tv", gfv), "ts"], inc="ts")
                ggv = p.mark("ts")
                tvm = p.mark("tv")
                if gi < NG - 1:
                    xv = pro(t0 + G)
                for c in range(CPG):
                    p.op("tensor", lambda e: e.nop(),
                         wait=[("qs", trv), ("tv", tvm), "ts"])
                    scan_chunk("fw", c, sb["hT0"][:], sb["hT0"][:], q_rot,
                               tvm)
                yfv = p.mark("ts")
                opvs = [None, None]
                urvs = [None, None]

                def emit_gating(s_):
                    tsl = slice(s_ * 128, (s_ + 1) * 128)
                    w1 = [("ts", yfv), "tv"] if s_ == 0 else ["tv"]
                    p.op("vector",
                         lambda e, tsl=tsl: e.tensor_tensor(
                             sb["ystg"][:, :, tsl], sb["ystg"][:, :, tsl],
                             sb["siluz"][:, :, tsl], op=ALU.mult),
                         wait=w1, inc="tv")
                    p.op("vector",
                         lambda e, tsl=tsl: e.tensor_tensor(
                             sb["ybw"][:, :, tsl], sb["ybw"][:, :, tsl],
                             sb["siluz"][:, :, tsl], op=ALU.mult),
                         wait=["tv", "qy"], inc="tv")
                    gvv = p.mark("tv")
                    for br, buf in ((0, "ystg"), (1, "ybw")):
                        p.op("scalar",
                             lambda e, buf=buf, tsl=tsl: e.activation(
                                 yzn_ap[:, :, tsl], sb[buf][:, :, tsl],
                                 AF.Square),
                             wait=[("tv", gvv), "ts"], inc="ts")
                        sqv = p.mark("ts")
                        psl = slice(PB_X + br * G + s_ * 128,
                                    PB_X + br * G + s_ * 128 + 128)
                        for i in range(NH):
                            mm(ps[0:1, psl], sb["ones128b"][:],
                               yzn_ap[:, i, tsl], i == 0, i == NH - 1,
                               wait=[("ts", sqv)] if i == 0 else ())
                        rmv = p.mark("tp")
                        p.op("scalar",
                             lambda e, br=br, tsl=tsl, psl=psl: e.activation(
                                 sb["rrow"][0:1, br, tsl], ps[0:1, psl],
                                 AF.Ln, scale=1.0 / DIN,
                                 bias=sb["epsb"][0:1, 0:1]),
                             wait=[("tp", rmv), "ts"], inc="ts")
                        p.op("scalar",
                             lambda e, br=br, tsl=tsl: e.activation(
                                 sb["rrow"][0:1, br, tsl],
                                 sb["rrow"][0:1, br, tsl],
                                 AF.Exp, scale=-0.5),
                             wait=["ts"], inc="ts")
                    rsv = p.mark("ts")
                    for br in range(2):
                        psl = slice(PB_X + br * G + s_ * 128,
                                    PB_X + br * G + s_ * 128 + 128)
                        mm(ps[:, psl], sb["ones3"][0:1, :],
                           sb["rrow"][0:1, br, tsl], True, True,
                           wait=[("ts", rsv), "tv"] if br == 0 else ())
                    bcv = p.mark("tp")
                    p.op("vector",
                         lambda e, tsl=tsl: e.tensor_copy(
                             sb["rbc"][:, :, tsl],
                             ps[:, PB_X:PB_X + 512].rearrange(
                                 "p (b t) -> p b t", b=2)[:, :, tsl]),
                         wait=[("tp", bcv)], inc="tv")
                    rbv = p.mark("tv")
                    p.op("vector",
                         lambda e, tsl=tsl: e.tensor_tensor(
                             sb["ystg"][:, :, tsl], sb["ystg"][:, :, tsl],
                             sb["rbc"][:, 0:1, tsl].broadcast_to(
                                 [128, NH, 128]),
                             op=ALU.mult),
                         wait=[("tv", rbv)], inc="tv")
                    p.op("vector",
                         lambda e, tsl=tsl: e.tensor_tensor(
                             sb["ybw"][:, :, tsl], sb["ybw"][:, :, tsl],
                             sb["rbc"][:, 1:2, tsl].broadcast_to(
                                 [128, NH, 128]),
                             op=ALU.mult),
                         wait=["tv"], inc="tv")
                    p.op("vector",
                         lambda e, tsl=tsl: e.tensor_tensor(
                             yzn_ap[:, :, tsl], sb["ystg"][:, :, tsl],
                             sb["ybw"][:, :, tsl], op=ALU.add),
                         wait=["tv"], inc="tv")
                    return p.mark("tv")

                def emit_outproj(s_, ysv_s, extra=()):
                    tsl = slice(s_ * 128, (s_ + 1) * 128)
                    for nh_ in range(2):
                        pbo = PB_MM[nh_]
                        for k in range(16):
                            mm(ps[:, pbo:pbo + 512], yzn_ap[:, k, tsl],
                               WoT[:, k, nh_ * 512:(nh_ + 1) * 512],
                               k == 0, k == 15,
                               wait=[("tv", ysv_s), "ts"] + list(extra)
                               if (k == 0 and nh_ == 0) else ())
                    opvs[s_] = p.mark("tp")
                    ld(sb["ur"][:],
                       dr["u"][t0 + s_ * 128:t0 + (s_ + 1) * 128, :],
                       wait=["tv", "qo"], inc="qu", q="gpsimd")
                    urvs[s_] = p.mark("qu")

                u2v = sb["u2"][:].rearrange("p c -> p c")

                def emit_res(s_):
                    rdst = [sb["ostg"][:], sb["u2"][:, 0:512]]
                    for nh_ in range(2):
                        pbo = PB_MM[nh_]
                        dstb = rdst[nh_]
                        p.op("vector",
                             lambda e, nh_=nh_, pbo=pbo, s_=s_, dstb=dstb:
                             e.tensor_tensor(
                                 dstb, ps[:, pbo:pbo + 512],
                                 sb["gff"][:, s_, nh_ * 512:(nh_ + 1) * 512],
                                 op=ALU.mult),
                             wait=[("tp", opvs[s_]), ("ts", ggv), "tv", "qo"]
                             if nh_ == 0 else ["tv", "qu"], inc="tv")
                    mulv = p.mark("tv")
                    for nh_ in range(2):
                        dstb = rdst[nh_]
                        p.op("vector",
                             lambda e, nh_=nh_, dstb=dstb: e.tensor_tensor(
                                 dstb, dstb,
                                 sb["ur"][:, nh_ * 512:(nh_ + 1) * 512],
                                 op=ALU.add),
                             wait=["tv", ("qu", urvs[s_])], inc="tv")
                        odv = p.mark("tv")
                        p.op("scalar",
                             lambda e, t0=t0, s_=s_, nh_=nh_, dstb=dstb:
                             e.dma_start(
                                 out=dr["out"][t0 + s_ * 128:
                                               t0 + (s_ + 1) * 128,
                                               nh_ * 512:(nh_ + 1) * 512],
                                 in_=dstb),
                             wait=[("tv", odv)], inc="qo", dma=True)
                    return mulv

                ysv0 = emit_gating(0)
                emit_outproj(0, ysv0)
                ysv1 = emit_gating(1)
                mulv0 = emit_res(0)
                emit_outproj(1, ysv1, extra=[("tv", mulv0)])
                emit_res(1)
                prev = {"tv": p.mark("tv"), "ts": p.mark("ts"),
                        "tp": p.mark("tp")}
            p.op("scalar", lambda e: e.nop(), wait=["qo", "qw", "qu", "qd"])
            p.op("sync", lambda e: e.nop(), wait=["qo", "qs"])
            p.op("vector", lambda e: e.nop(), wait=["qo"])
            p.op("gpsimd", lambda e: e.nop(), wait=["qx", "qy", "qg"])

        pass_A()
        pass_B()

    return prog


def build_nc(debug=0):
    prog = make_program(debug)
    nc = bass.Bass("TRN2", target_bir_lowering=False)
    prog(nc, None)
    with ExitStack() as st:
        handles = {n: st.enter_context(nc.semaphore(n)) for n in SEM_NAMES}
        p = P(handles)
        block = st.enter_context(nc.Block())

        @block.sync
        def _(eng):
            p.begin("sync", eng)
            prog(nc, p)

        @block.scalar
        def _(eng):
            p.begin("scalar", eng)
            prog(nc, p)

        @block.vector
        def _(eng):
            p.begin("vector", eng)
            prog(nc, p)

        @block.tensor
        def _(eng):
            p.begin("tensor", eng)
            prog(nc, p)

        @block.gpsimd
        def _(eng):
            p.begin("gpsimd", eng)
            prog(nc, p)
    return nc


def kernel(**inputs):
    consts = build_consts()
    nc = build_nc()
    keymap = {
        "u": "L0_attn_u", "ipw": "L0_attn_in_proj_w",
        "cw": "L0_attn_conv_w", "cb": "L0_attn_conv_b",
        "cwb": "L0_attn_conv_w_bw", "cbb": "L0_attn_conv_b_bw",
        "dtb": "L0_attn_dt_bias", "dtbb": "L0_attn_dt_bias_bw",
        "alog": "L0_attn_A_log", "ablog": "L0_attn_A_b_log",
        "dvec": "L0_attn_D", "dbvec": "L0_attn_D_b",
        "normw": "L0_attn_norm_w", "opw": "L0_attn_out_proj_w",
        "fwt": "L0_ffn_weight", "fb": "L0_ffn_bias",
    }
    base = {k: np.ascontiguousarray(np.asarray(inputs[vk], np.float32))
            for k, vk in keymap.items() if k != "u"}
    base.update(consts)
    u = np.ascontiguousarray(np.asarray(inputs["L0_attn_u"], np.float32))
    in_maps = []
    for b in range(8):
        m = dict(base)
        m["u"] = u[b]
        in_maps.append(m)
    res = run_bass_kernel_spmd(nc, in_maps, core_ids=list(range(8)))
    return np.stack([res.results[b]["out"] for b in range(8)])


# version 21
# speedup vs baseline: 1.2505x; 1.0567x over previous
import numpy as np
import ml_dtypes
from contextlib import ExitStack

import concourse.bass as bass
import concourse.mybir as mybir
from concourse.bass_utils import run_bass_kernel_spmd

F32 = mybir.dt.float32
BF16 = mybir.dt.bfloat16
I32 = mybir.dt.int32
AF = mybir.ActivationFunctionType
ALU = mybir.AluOpType

D = 1024
L = 4096
DIN = 2048
NH = 16
CDIM = 2176
DIP = 4240
G = 256
NG = L // G
Q = 128
CPG = G // Q
NEG = -1.0e30
EPS = 1e-5
BF = ml_dtypes.bfloat16


def build_consts():
    c = {}
    c["c_i128"] = np.eye(128, dtype=BF)
    c["c_i64"] = np.eye(64, dtype=BF)
    tt = np.arange(128)[None, :] - np.arange(128)[:, None]
    mfw = np.where(tt < 0, NEG, 0.0).astype(np.float32)
    mbw = np.where(tt > 0, NEG, 0.0).astype(np.float32)
    c["c_maskfw"] = np.repeat(mfw[:, None, :], 4, axis=1).astype(BF)
    c["c_maskbw"] = np.repeat(mbw[:, None, :], 4, axis=1).astype(BF)
    ind = np.zeros((48, NH, 128), dtype=BF)
    for base in (0, 16, 32):
        for h in range(NH):
            ind[base + h, h, :] = 1.0
    c["c_ind3"] = ind
    c["c_ones3"] = np.ones((35, 128), dtype=BF)
    c["c_ones128b"] = np.ones((128, 1), dtype=BF)
    c["c_ones1f"] = np.ones((1, 128), dtype=np.float32)
    return c


class P:
    def __init__(self, handles):
        self.h = handles
        self.cnt = {k: 0 for k in handles}
        self.eng_name = None
        self.eng = None

    def begin(self, name, eng):
        self.eng_name = name
        self.eng = eng
        for k in self.cnt:
            self.cnt[k] = 0

    def mark(self, sem):
        return self.cnt[sem]

    def op(self, eng_name, fn, wait=(), inc=None, dma=False):
        if self.eng_name == eng_name:
            for w in wait:
                if isinstance(w, str):
                    s, v = w, self.cnt[w]
                else:
                    s, v = w
                    if v is None:
                        v = self.cnt[s]
                self.eng.wait_ge(self.h[s], v)
            if dma and inc is not None:
                self.eng.wait_ge(self.h[inc], self.cnt[inc])
            instr = fn(self.eng)
            if inc is not None:
                instr.then_inc(self.h[inc], 16 if dma else 1)
        if inc is not None:
            self.cnt[inc] += 16 if dma else 1


SEM_NAMES = ["qs", "qc", "qg", "qw", "qu", "qx", "qy", "qd", "qo", "tv", "ts", "tp", "tg"]


def make_program(debug=0):
    state = {}

    def declare(nc):
        spk = "ExternalOutput" if debug else "Internal"
        dr = {}
        dr["u"] = nc.dram_tensor("u", [L, D], F32, kind="ExternalInput").ap()
        dr["ipw"] = nc.dram_tensor("ipw", [DIP, D], F32,
                                   kind="ExternalInput").ap()
        for nm, shp in [("cw", [CDIM, 4]), ("cb", [CDIM]),
                        ("cwb", [CDIM, 4]), ("cbb", [CDIM]),
                        ("dtb", [16]), ("dtbb", [16]),
                        ("alog", [16]), ("ablog", [16]),
                        ("dvec", [16]), ("dbvec", [16]),
                        ("normw", [DIN]), ("opw", [D, DIN]),
                        ("fwt", [D, D]), ("fb", [D])]:
            dr[nm] = nc.dram_tensor(nm, shp, F32, kind="ExternalInput").ap()
        for nm, shp, dt in [("c_i128", [128, 128], BF16),
                            ("c_i64", [64, 64], BF16),
                            ("c_maskfw", [128, 4, 128], BF16),
                            ("c_maskbw", [128, 4, 128], BF16),
                            ("c_ind3", [48, NH, 128], BF16),
                            ("c_ones3", [35, 128], BF16),
                            ("c_ones128b", [128, 1], BF16),
                            ("c_ones1f", [1, 128], F32)]:
            dr[nm] = nc.dram_tensor(nm, shp, dt, kind="ExternalInput").ap()
        dr["out"] = nc.dram_tensor("out", [L, D], F32, kind="ExternalOutput").ap()
        if debug:
            dr["dbg"] = nc.dram_tensor("dbg", [8, 128, 4096], BF16,
                                       kind="ExternalOutput").ap()
        dr["wbfA"] = nc.dram_tensor("wbfA", [2192, D], BF16, kind="Internal").ap()
        dr["wbfZ"] = nc.dram_tensor("wbfZ", [DIN, D], BF16, kind="Internal").ap()
        dr["wbfO"] = nc.dram_tensor("wbfO", [D, DIN], BF16, kind="Internal").ap()
        dr["wbfF"] = nc.dram_tensor("wbfF", [D, D], BF16, kind="Internal").ap()
        dr["sp_x"] = nc.dram_tensor("sp_x", [NG, 128, NH * G], BF16,
                                    kind=spk).ap()
        dr["sp_bc"] = nc.dram_tensor("sp_bc", [NG, 64, 2 * G], BF16,
                                     kind=spk).ap()
        dr["sp_dt"] = nc.dram_tensor("sp_dt", [NG, 16, G], F32,
                                     kind=spk).ap()
        dr["sp_y"] = nc.dram_tensor("sp_y", [NG, 128, NH * G], BF16,
                                    kind=spk).ap()
        dr["s_scr"] = nc.dram_tensor("s_scr", [16, 6, G], BF16,
                                     kind="Internal").ap()

        sbufs = {}

        def A(name, shape, dt):
            h = nc.sbuf_tensor(name, shape, dt).__enter__()
            sbufs[name] = h

        def PS(name, shape, dt):
            h = nc.psum_tensor(name, shape, dt).__enter__()
            sbufs[name] = h

        A("warena", [128, 40960], BF16)
        A("wstage", [128, 2, D], BF16)
        A("stgX", [128, NH, G], BF16)
        A("stgBC", [128, 2, G], BF16)
        A("haloX", [128, NH, 8], BF16)
        A("haloBC", [128, 2, 8], BF16)
        A("convX", [128, NH, G], BF16)
        A("convBC", [128, 2, G], BF16)
        A("u2", [128, D], F32)
        A("ur", [128, D], F32)
        A("um", [128, D], F32)
        A("mv", [128, 8], F32)
        A("bns", [128, 12], F32)
        A("xnT", [128, 8, G], BF16)
        A("xdT", [128, 8, G], BF16)
        A("xn2", [128, 2, D], BF16)
        A("xd2", [128, 2, D], BF16)
        A("Xtm", [128, NH * CPG, 128], BF16)
        A("Btm", [128, CPG, 64], BF16)
        A("dtraw", [16, G], F32)
        A("dt_t", [16, G], F32)
        A("sp1", [16, G], F32)
        A("sp2", [16, G], F32)
        A("a_hm", [16, G], F32)
        A("s_hm", [16, G], F32)
        A("lndt", [16, G], F32)
        A("ones16", [16, Q], F32)
        A("spl6", [16, 6, G], BF16)
        A("b3c", [48, G], BF16)
        A("sflat", [35, 16, 128], BF16)
        A("cbt", [128, 128], BF16)
        A("mtq", [128, 1024], BF16)
        A("gq", [64, 1024], BF16)
        A("gcq", [64, 1024], BF16)
        A("bwq", [128, 512], BF16)
        A("diagq", [64, 512], BF16)
        A("hT0", [64, NH * 128], BF16)
        A("ybw", [128, NH, G], BF16)
        A("siluz", [128, NH, G], BF16)
        A("gff", [128, 2, D], BF16)
        A("ostg", [128, 512], F32)
        A("rrow", [1, 2, G], BF16)
        A("rbc", [128, 2, G], BF16)
        A("qv8", [128, 512], F32)
        A("qau", [128, 512], BF16)
        A("qm1", [128, 512], BF16)
        A("qm2", [128, 512], BF16)
        A("qsc", [128, 16], F32)
        A("qsc2", [128, 16], F32)
        A("qrcp", [128, 16], F32)
        A("i128", [128, 128], BF16)
        A("i64", [64, 64], BF16)
        A("maskst", [128, 4, 128], BF16)
        A("ind3", [48, NH, 128], BF16)
        A("ones3", [35, 128], BF16)
        A("ones128b", [128, 1], BF16)
        A("ones1f", [1, 128], F32)
        A("negA", [16, 1], F32)
        A("dtbias", [16, 1], F32)
        A("ddiag", [128, NH * 128], BF16)
        A("dbc", [128, 16], F32)
        A("cwX", [128, NH, 4], F32)
        A("cbX", [128, NH], F32)
        A("cwB", [64, 4], F32)
        A("cwBC2", [64, 2, 4], F32)
        A("cbB", [64, 1], F32)
        A("cwC", [64, 4], F32)
        A("cbC", [64, 1], F32)
        A("vrow16", [1, 16], F32)
        A("nwtile", [128, 16], F32)
        A("epsb", [128, 1], F32)
        A("oneb", [128, 1], F32)
        sbufs["ystg"] = sbufs["convX"]
        sbufs["t16a"] = sbufs["sp1"]
        sbufs["t16b"] = sbufs["sp2"]
        sbufs["biasf"] = sbufs["lndt"]
        PS("ps", [128, 4096], F32)
        return dr, sbufs

    def prog(nc, p):
        shared = {}
        if "dr" not in state:
            state["dr"], state["sb"] = declare(nc)
            return
        dr = state["dr"]
        sb = state["sb"]
        ps = sb["ps"]

        PB_MM = [0, 512]
        PB_D = [1024, 1536]
        PB_Y = [2048, 2560]
        PB_S = 3072
        PB_X = 3584

        def mm(out_ap, lhsT, rhs, start, stop, wait=(), inc="tp"):
            p.op("tensor",
                 lambda e: e.matmul(out_ap, lhsT, rhs, start=start, stop=stop),
                 wait=wait, inc=inc)

        def ld(dst, src, wait=(), inc="qc", q="scalar"):
            p.op(q, lambda e: e.dma_start(out=dst, in_=src),
                 wait=wait, inc=inc, dma=True)

        def load_consts():
            ld(sb["i128"][:], dr["c_i128"], inc="qw")
            ld(sb["i64"][:], dr["c_i64"], inc="qw")
            ld(sb["ind3"][:], dr["c_ind3"], inc="qw")
            ld(sb["ones3"][:], dr["c_ones3"], inc="qw")
            ld(sb["ones128b"][:], dr["c_ones128b"], inc="qw")
            ld(sb["ones1f"][:], dr["c_ones1f"], inc="qw")

        def wT_build(src_rows_ap, nrows, wbf, dst_ap, xw=(), bounce=True,
                     bb=None):
            full = (nrows // 128) * 128
            if not bounce:
                sv = p.mark("qw")
                p.op("sync",
                     lambda e: e.dma_start(out=dst_ap, in_=wbf[0:nrows, :],
                                           transpose=True),
                     wait=[("qw", sv)] + list(xw), inc="qs", dma=True)
                return
            nt = nrows // 128
            rem = nrows - full
            bbt = 2 if bb is None else bb.shape[1]
            for c0 in range(0, nt, bbt):
                cn = min(bbt, nt - c0)
                buf = (sb["wstage"] if bb is None else bb)[:, 0:cn, :]
                wv0 = p.mark("qw")
                p.op("gpsimd",
                     lambda e, c0=c0, cn=cn, buf=buf: e.dma_start(
                         out=buf,
                         in_=src_rows_ap[c0 * 128:(c0 + cn) * 128, :]
                         .rearrange("(t p) c -> p t c", p=128)),
                     wait=[("qw", wv0)], inc="qg", dma=True)
                wv = p.mark("qg")
                p.op("scalar",
                     lambda e, c0=c0, cn=cn, buf=buf: e.dma_start(
                         out=wbf[c0 * 128:(c0 + cn) * 128, :]
                         .rearrange("(t p) c -> p t c", p=128),
                         in_=buf),
                     wait=[("qg", wv)], inc="qw", dma=True)
            if rem:
                wv0 = p.mark("qw")
                p.op("gpsimd",
                     lambda e: e.dma_start(out=sb["wstage"][0:rem, 0, :],
                                           in_=src_rows_ap[full:nrows, :]),
                     wait=[("qw", wv0)], inc="qg", dma=True)
                wv = p.mark("qg")
                p.op("scalar",
                     lambda e: e.dma_start(out=wbf[full:nrows, :],
                                           in_=sb["wstage"][0:rem, 0, :]),
                     wait=[("qg", wv)], inc="qw", dma=True)
            if dst_ap is None:
                return
            sv = p.mark("qw")
            p.op("sync",
                 lambda e: e.dma_start(out=dst_ap, in_=wbf[0:nrows, :],
                                       transpose=True),
                 wait=[("qw", sv)] + list(xw), inc="qs", dma=True)

        def dir_params(pref):
            alog = dr["alog"] if pref == "fw" else dr["ablog"]
            dtbv = dr["dtb"] if pref == "fw" else dr["dtbb"]
            dvv = dr["dvec"] if pref == "fw" else dr["dbvec"]
            cwv = dr["cw"] if pref == "fw" else dr["cwb"]
            cbv = dr["cb"] if pref == "fw" else dr["cbb"]
            wv = [("ts", None), ("tv", None)]
            ld(sb["negA"][:], alog.unsqueeze(1), wait=wv, inc="qw")
            ld(sb["dtbias"][:], dtbv.unsqueeze(1), inc="qw")
            ld(sb["vrow16"][:], dvv.unsqueeze(0), inc="qw")
            ld(sb["cwX"][:], cwv[0:2048, :].rearrange("(t p) k -> p t k", p=128),
               inc="qw")
            p.op("scalar", lambda e: e.dma_start(
                out=sb["cbX"][:],
                in_=cbv[0:2048].rearrange("(t p) -> p t", p=128),
                allow_slow_non_contiguous=True), inc="qw", dma=True)
            ld(sb["cwB"][:], cwv[2048:2112, :], inc="qw")
            ld(sb["cbB"][:], cbv[2048:2112].unsqueeze(1), inc="qw")
            ld(sb["cwC"][:], cwv[2112:2176, :], inc="qw")
            ld(sb["cbC"][:], cbv[2112:2176].unsqueeze(1), inc="qw")
            ld(sb["maskst"][:], dr["c_maskfw" if pref == "fw" else "c_maskbw"],
               inc="qw")
            ld(sb["cwBC2"][0:64, 0, :], cwv[2048:2112, :], inc="qw")
            ld(sb["cwBC2"][0:64, 1, :], cwv[2112:2176, :], inc="qw")
            lv = p.mark("qw")
            p.op("scalar",
                 lambda e: e.activation(sb["negA"][:], sb["negA"][:], AF.Exp),
                 wait=[("qw", lv)], inc="ts")
            p.op("vector",
                 lambda e: e.tensor_scalar(sb["negA"][:], sb["negA"][:],
                                           -1.0, None, op0=ALU.mult),
                 wait=["ts"], inc="tv")
            mm(ps[:, PB_X:PB_X + 16], sb["ones1f"][:], sb["vrow16"][:],
               True, True, wait=[("qw", lv), "tv", "ts"])
            p.op("vector",
                 lambda e: e.tensor_copy(sb["dbc"][:], ps[:, PB_X:PB_X + 16]),
                 wait=["tp"], inc="tv")
            p.op("vector",
                 lambda e: e.tensor_tensor(
                     sb["ddiag"][:].rearrange("p (h q) -> p h q", h=NH),
                     sb["i128"][:].unsqueeze(1).broadcast_to(
                         [128, NH, 128]),
                     sb["dbc"][:].unsqueeze(2).broadcast_to(
                         [128, NH, 128]),
                     op=ALU.mult),
                 wait=["tv"], inc="tv")

        def ln_tile(s_, want_um):
            uap = sb["u2"][:]
            xnt = sb["xn2"][:, s_, :]
            p.op("vector", lambda e: e.bn_stats(sb["bns"][:, 0:6], uap[:, 0:512]),
                 wait=["qu", "tv"], inc="tv")
            p.op("vector", lambda e: e.bn_stats(sb["bns"][:, 6:12], uap[:, 512:1024]),
                 wait=["tv"], inc="tv")
            p.op("vector", lambda e: e.bn_aggr(sb["mv"][:, 0:2], sb["bns"][:]),
                 wait=["tv"], inc="tv")
            mvv = p.mark("tv")
            p.op("scalar",
                 lambda e: e.activation(sb["mv"][:, 2:3], sb["mv"][:, 1:2],
                                        AF.Ln, bias=sb["epsb"][:, 0:1]),
                 wait=[("tv", mvv), "ts"], inc="ts")
            p.op("scalar",
                 lambda e: e.activation(sb["mv"][:, 3:4], sb["mv"][:, 2:3],
                                        AF.Exp, scale=-0.5),
                 wait=["ts"], inc="ts")
            rsv = p.mark("ts")
            if want_um:
                p.op("vector",
                     lambda e: e.tensor_scalar(sb["um"][:], uap,
                                               sb["mv"][:, 0:1], None,
                                               op0=ALU.subtract),
                     wait=[("tv", mvv)], inc="tv")
                p.op("vector",
                     lambda e: e.tensor_scalar(xnt, sb["um"][:],
                                               sb["mv"][:, 3:4], None,
                                               op0=ALU.mult),
                     wait=[("ts", rsv), "tv", "qs"], inc="tv")
            else:
                p.op("vector",
                     lambda e: e.tensor_scalar(xnt, uap,
                                               sb["mv"][:, 0:1],
                                               sb["mv"][:, 3:4],
                                               op0=ALU.subtract, op1=ALU.mult),
                     wait=[("ts", rsv), ("tv", mvv), "qs"], inc="tv")

        def conv_dir(dirn, rdy, tmk=None, halo_fn=None):
            stX, stBC = sb["stgX"], sb["stgBC"]
            cvX, cvBC = sb["convX"], sb["convBC"]
            scr = sb["qv8"][:].bitcast(BF16)
            n_main = G - 3

            def sls(tap):
                if dirn == "fw":
                    return (slice(3, G), slice(tap, tap + n_main),
                            slice(0, 3), slice(tap, tap + 3))
                return (slice(0, G - 3), slice(3 - tap, 3 - tap + n_main),
                        slice(G - 3, G), slice(3 - tap, 3 - tap + 3))
            osl0, isl0, esl0, hsl0 = sls(0)

            def tap0_tile(i, w1):
                p.op("vector",
                     lambda e, i=i: e.tensor_scalar(
                         cvX[:, i, osl0], stX[:, i, isl0],
                         sb["cwX"][:, i, 0:1], sb["cbX"][:, i:i + 1],
                         op0=ALU.mult, op1=ALU.add),
                     wait=w1, inc="tv")
                p.op("vector",
                     lambda e, i=i: e.tensor_scalar(
                         cvX[:, i, esl0], sb["haloX"][:, i, hsl0],
                         sb["cwX"][:, i, 0:1], sb["cbX"][:, i:i + 1],
                         op0=ALU.mult, op1=ALU.add),
                     inc="tv")

            def taps_round(r4):
                t4 = slice(r4 * 4, r4 * 4 + 4)
                for tap in range(1, 4):
                    osl, isl, esl, hsl = sls(tap)
                    scr3 = scr[:, 0:4 * n_main].rearrange(
                        "p (i t) -> p i t", i=4)
                    p.op("vector",
                         lambda e, t4=t4, isl=isl, tap=tap, scr3=scr3:
                         e.tensor_tensor(
                             scr3, stX[:, t4, isl],
                             sb["cwX"][:, t4, tap:tap + 1].broadcast_to(
                                 [128, 4, n_main]),
                             op=ALU.mult),
                         wait=["tv"], inc="tv")
                    p.op("vector",
                         lambda e, t4=t4, osl=osl, scr3=scr3:
                         e.tensor_tensor(
                             cvX[:, t4, osl], cvX[:, t4, osl], scr3,
                             op=ALU.add),
                         wait=["tv"], inc="tv")

            for r4 in range(4):
                for i in range(r4 * 4, r4 * 4 + 4):
                    w1 = list(rdy) + ["tv", "qy"] if (r4 == 0 and i == 0) \
                        else ()
                    if tmk is not None:
                        w1 = ([("ts", tmk[0][i])] + (["tv", "qy"]
                              if i == 0 else []))
                    tap0_tile(i, w1)
                taps_round(r4)
            for bc, wv_, bv_ in ((0, sb["cwB"], sb["cbB"]),
                                 (1, sb["cwC"], sb["cbC"])):
                w1 = [("ts", tmk[1])] if (tmk is not None and bc == 0) else ()
                p.op("vector",
                     lambda e, bc=bc, wv_=wv_, bv_=bv_: e.tensor_scalar(
                         cvBC[0:64, bc, osl0], stBC[0:64, bc, isl0],
                         wv_[:, 0:1], bv_[:, 0:1],
                         op0=ALU.mult, op1=ALU.add),
                     wait=w1, inc="tv" if bc == 1 else None)
                p.op("vector",
                     lambda e, bc=bc, wv_=wv_, bv_=bv_: e.tensor_scalar(
                         cvBC[0:64, bc, esl0], sb["haloBC"][0:64, bc, hsl0],
                         wv_[:, 0:1], bv_[:, 0:1],
                         op0=ALU.mult, op1=ALU.add),
                     inc="tv" if bc == 1 else None)
            p.op("vector", lambda e: e.nop(), wait=["tv"])
            if halo_fn is not None:
                halo_fn()
            for tap in range(1, 4):
                osl, isl, esl, hsl = sls(tap)
                scr2 = scr[:, 0:2 * n_main].rearrange("p (i t) -> p i t", i=2)
                p.op("vector",
                     lambda e, isl=isl, tap=tap, scr2=scr2: e.tensor_tensor(
                         scr2[0:64, :, :], stBC[0:64, :, isl],
                         sb["cwBC2"][0:64, :, tap:tap + 1].broadcast_to(
                             [64, 2, n_main]),
                         op=ALU.mult),
                     wait=["tv"], inc="tv")
                p.op("vector",
                     lambda e, osl=osl, scr2=scr2: e.tensor_tensor(
                         cvBC[0:64, :, osl], cvBC[0:64, :, osl],
                         scr2[0:64, :, :], op=ALU.add),
                     wait=["tv"], inc="tv")
            for tap in range(1, 4):
                osl, isl, esl, hsl = sls(tap)
                scr3 = scr[:, 0:NH * 3].rearrange("p (i t) -> p i t", i=NH)
                p.op("vector",
                     lambda e, hsl=hsl, tap=tap, scr3=scr3: e.tensor_tensor(
                         scr3, sb["haloX"][:, :, hsl],
                         sb["cwX"][:, :, tap:tap + 1].broadcast_to(
                             [128, NH, 3]),
                         op=ALU.mult),
                     wait=["tv"], inc="tv")
                p.op("vector",
                     lambda e, esl=esl, scr3=scr3: e.tensor_tensor(
                         cvX[:, :, esl], cvX[:, :, esl], scr3, op=ALU.add),
                     wait=["tv"], inc="tv")
                scr2 = scr[:, 512:512 + 2 * 3].rearrange(
                    "p (i t) -> p i t", i=2)
                p.op("vector",
                     lambda e, hsl=hsl, tap=tap, scr2=scr2: e.tensor_tensor(
                         scr2[0:64, :, :], sb["haloBC"][0:64, :, hsl],
                         sb["cwBC2"][0:64, :, tap:tap + 1].broadcast_to(
                             [64, 2, 3]),
                         op=ALU.mult),
                     wait=["tv"], inc="tv")
                p.op("vector",
                     lambda e, esl=esl, scr2=scr2: e.tensor_tensor(
                         cvBC[0:64, :, esl], cvBC[0:64, :, esl],
                         scr2[0:64, :, :], op=ALU.add),
                     wait=["tv"], inc="tv")
            cvv = p.mark("tv")
            p.op("scalar",
                 lambda e: e.activation(
                     cvX[:].rearrange("p i t -> p (i t)"),
                     cvX[:].rearrange("p i t -> p (i t)"), AF.Silu),
                 wait=[("tv", cvv), "tg", "ts"], inc="ts")
            p.op("scalar",
                 lambda e: e.activation(
                     cvBC[0:64, :, :].rearrange("p i t -> p (i t)"),
                     cvBC[0:64, :, :].rearrange("p i t -> p (i t)"),
                     AF.Silu),
                 wait=["ts"], inc="ts")

        def dt_pipeline(dirn, rdy, prev):
            v = sb
            p.op("vector",
                 lambda e: e.tensor_scalar(v["dt_t"][:], v["dtraw"][:],
                                           v["dtbias"][:, 0:1], None,
                                           op0=ALU.add),
                 wait=rdy + ["tv"], inc="tv")
            t0 = p.mark("tv")
            p.op("scalar",
                 lambda e: e.activation(v["sp1"][:], v["dt_t"][:], AF.Abs),
                 wait=[("tv", t0), "ts"], inc="ts")
            p.op("scalar",
                 lambda e: e.activation(v["sp1"][:], v["sp1"][:], AF.Exp,
                                        scale=-1.0),
                 wait=["ts"], inc="ts")
            p.op("scalar",
                 lambda e: e.activation(v["sp1"][:], v["sp1"][:], AF.Ln,
                                        bias=sb["oneb"][0:16, 0:1]),
                 wait=["ts"], inc="ts")
            p.op("scalar",
                 lambda e: e.activation(v["sp2"][:], v["dt_t"][:], AF.Relu),
                 wait=["ts"], inc="ts")
            spv = p.mark("ts")
            p.op("vector",
                 lambda e: e.tensor_tensor(v["dt_t"][:], v["sp1"][:],
                                           v["sp2"][:], op=ALU.add),
                 wait=[("ts", spv)], inc="tv")
            dtv = p.mark("tv")
            p.op("scalar",
                 lambda e: e.activation(v["lndt"][:], v["dt_t"][:], AF.Ln),
                 wait=[("tv", dtv)], inc="ts")
            lnv = p.mark("ts")
            p.op("vector",
                 lambda e: e.tensor_scalar(v["a_hm"][:], v["dt_t"][:],
                                           v["negA"][:, 0:1], None,
                                           op0=ALU.mult),
                 wait=[("tv", dtv)], inc="tv")
            av = p.mark("tv")
            for c in range(CPG):
                sl = slice(c * Q, (c + 1) * Q)
                p.op("vector",
                     lambda e, sl=sl: e.tensor_tensor_scan(
                         v["t16a"][:, sl], v["ones16"][:, 0:Q],
                         v["a_hm"][:, sl], 0.0, op0=ALU.mult, op1=ALU.add),
                     wait=[("tv", av)] if c == 0 else ["tv"], inc="tv")
            pv = p.mark("tv")
            if dirn == "fw":
                p.op("vector",
                     lambda e: e.tensor_copy(v["s_hm"][:], v["t16a"][:]),
                     wait=[("tv", pv)], inc="tv")
            else:
                for c in range(CPG):
                    sl = slice(c * Q, (c + 1) * Q)
                    last = (c + 1) * Q - 1
                    p.op("vector",
                         lambda e, sl=sl, last=last: e.tensor_scalar(
                             v["t16b"][:, sl], v["t16a"][:, sl],
                             -1.0, v["t16a"][:, last:last + 1],
                             op0=ALU.mult, op1=ALU.add),
                         wait=[("tv", pv)] if c == 0 else ["tv"], inc="tv")
                p.op("vector",
                     lambda e: e.tensor_tensor(v["s_hm"][:], v["t16b"][:],
                                               v["a_hm"][:], op=ALU.add),
                     wait=["tv"], inc="tv")
            sv = p.mark("tv")
            p.op("vector",
                 lambda e: e.tensor_tensor(v["lndt"][:], v["lndt"][:],
                                           v["s_hm"][:], op=ALU.subtract),
                 wait=[("tv", sv), ("ts", lnv)], inc="tv")
            p.op("vector",
                 lambda e: e.tensor_copy(v["spl6"][:, 0, :], v["s_hm"][:]),
                 wait=[("tv", sv), "tv"], inc="tv")
            p.op("vector",
                 lambda e: e.tensor_tensor(v["t16a"][:], v["s_hm"][:],
                                           v["spl6"][:, 0, :],
                                           op=ALU.subtract),
                 wait=["tv"], inc="tv")
            p.op("vector",
                 lambda e: e.tensor_copy(v["spl6"][:, 1, :], v["t16a"][:]),
                 wait=["tv"], inc="tv")
            p.op("vector",
                 lambda e: e.tensor_tensor(v["t16b"][:], v["t16a"][:],
                                           v["spl6"][:, 1, :],
                                           op=ALU.subtract),
                 wait=["tv"], inc="tv")
            p.op("vector",
                 lambda e: e.tensor_copy(v["spl6"][:, 2, :], v["t16b"][:]),
                 wait=["tv"], inc="tv")
            p.op("vector",
                 lambda e: e.tensor_copy(v["spl6"][:, 3, :], v["lndt"][:]),
                 wait=["tv"], inc="tv")
            p.op("vector",
                 lambda e: e.tensor_tensor(v["t16a"][:], v["lndt"][:],
                                           v["spl6"][:, 3, :],
                                           op=ALU.subtract),
                 wait=["tv"], inc="tv")
            p.op("vector",
                 lambda e: e.tensor_copy(v["spl6"][:, 4, :], v["t16a"][:]),
                 wait=["tv"], inc="tv")
            p.op("vector",
                 lambda e: e.tensor_tensor(v["t16b"][:], v["t16a"][:],
                                           v["spl6"][:, 4, :],
                                           op=ALU.subtract),
                 wait=["tv"], inc="tv")
            p.op("vector",
                 lambda e: e.tensor_copy(v["spl6"][:, 5, :], v["t16b"][:]),
                 wait=["tv"], inc="tv")
            spl = p.mark("tv")
            p.op("scalar",
                 lambda e: e.dma_start(out=dr["s_scr"], in_=v["spl6"][:]),
                 wait=[("tv", spl)], inc="qd", dma=True)
            sv2 = p.mark("qd")
            p.op("scalar",
                 lambda e: e.dma_start(
                     out=sb["sflat"][0:3, :, :],
                     in_=dr["s_scr"][:, 0:3, 0:128].rearrange(
                         "h j t -> j h t")),
                 wait=[("qd", sv2), ("tp", prev["tp"])], inc="qd", dma=True)
            p.op("scalar",
                 lambda e: e.dma_start(
                     out=sb["sflat"][32:35, :, :],
                     in_=dr["s_scr"][:, 0:3, 128:256].rearrange(
                         "h j t -> j h t")),
                 inc="qd", dma=True)
            for j_ in range(3):
                p.op("scalar",
                     lambda e, j_=j_: e.dma_start(
                         out=sb["b3c"][16 * j_:16 * (j_ + 1), :],
                         in_=dr["s_scr"][:, 3 + j_, :]),
                     inc="qd", dma=True)

        def scan_chunk(dirn, c, hTa, hTb, q_rot, tvm,
                       fill_fn=None):
            anchor = (Q - 1) if dirn == "fw" else 0
            csl = slice(c * Q, (c + 1) * Q)
            v = sb
            urb = v["ur"][:].bitcast(BF16)
            ostb = v["ostg"][:].bitcast(BF16)
            mtq2 = urb[:, 0:1024]
            gq2 = urb[0:64, 1024:2048]
            bwq2 = ostb[:, 0:512]
            diagq2 = ostb[0:64, 512:1024]
            mm(ps[:, 3072:3072 + 128], v["convBC"][0:64, 0, csl],
               v["convBC"][0:64, 1, csl], True, True,
               wait=["ts", ("tv", tvm), "qd", "qy"])
            p.op("scalar",
                 lambda e: e.copy(v["cbt"][:], ps[:, 3072:3072 + 128]),
                 wait=["tp"], inc="ts")
            cbv = p.mark("ts")
            ph = []
            for hq in range(2):
                h0 = hq * 8
                pd = 1024 + 1024 * (q_rot[0] % 2)
                q_rot[0] += 1
                dsl = slice(pd, pd + 1024)
                cb0 = 32 * c
                mtq_h = v["mtq"][:] if hq == 0 else mtq2
                gq_h = v["gq"][:] if hq == 0 else gq2
                bwq_h = v["bwq"][:] if hq == 0 else bwq2
                diagq_h = v["diagq"][:] if hq == 0 else diagq2
                mm(ps[0:64, pd:pd + 512], v["ones3"][cb0:cb0 + 3, 0:64],
                   v["sflat"][cb0:cb0 + 3, h0:h0 + 4, :], True, True,
                   wait=["qd", "ts", ("tv", tvm)])
                mm(ps[0:64, pd + 512:pd + 1024],
                   v["ones3"][cb0:cb0 + 3, 0:64],
                   v["sflat"][cb0:cb0 + 3, h0 + 4:h0 + 8, :], True, True)
                t1v = p.mark("tp")
                p.op("scalar",
                     lambda e, gq_h=gq_h, pd=pd: e.activation(
                         gq_h, ps[0:64, pd:pd + 1024], AF.Exp),
                     wait=[("tp", t1v), "ts", ("tv", tvm)], inc="ts")
                gv = p.mark("ts")
                for nn in range(2):
                    nsl = slice(pd + nn * 512, pd + nn * 512 + 512)
                    mm(ps[:, nsl], v["ones3"][cb0:cb0 + 3, :],
                       v["sflat"][cb0:cb0 + 3, h0 + nn * 4:h0 + nn * 4 + 4, :],
                       True, False,
                       wait=[("ts", gv)] if nn == 0 else ())
                    mm(ps[:, nsl], v["b3c"][:, csl],
                       v["ind3"][:, h0 + nn * 4:h0 + nn * 4 + 4, :],
                       False, False)
                    mm(ps[:, nsl], v["i128"][:],
                       v["maskst"][:], False, True)
                t2v = p.mark("tp")
                p.op("scalar",
                     lambda e, mtq_h=mtq_h, dsl=dsl: e.activation(
                         mtq_h, ps[:, dsl], AF.Exp),
                     wait=[("tp", t2v), "tv"], inc="ts")
                mtv = p.mark("ts")
                if fill_fn is not None:
                    fill_fn()
                p.op("vector",
                     lambda e, bwq_h=bwq_h, mtq_h=mtq_h: e.tensor_tensor(
                         bwq_h.rearrange("p (h n) -> p h n", h=8),
                         v["Btm"][:, c, :].unsqueeze(1).broadcast_to(
                             [128, 8, 64]),
                         mtq_h.rearrange("p (h t) -> p h t", h=8)[
                             :, :, anchor:anchor + 1].broadcast_to(
                                 [128, 8, 64]),
                         op=ALU.mult),
                     wait=[("ts", mtv), ("ts", cbv), "tv", "qo"],
                     inc="tv")
                p.op("vector",
                     lambda e, diagq_h=diagq_h, gq_h=gq_h: e.tensor_tensor(
                         diagq_h.rearrange("p (h n) -> p h n", h=8),
                         v["i64"][:].unsqueeze(1).broadcast_to([64, 8, 64]),
                         gq_h.rearrange("p (h t) -> p h t", h=8)[
                             :, :, anchor:anchor + 1].broadcast_to(
                                 [64, 8, 64]),
                         op=ALU.mult),
                     wait=["tv"], inc="tv")
                if hq == 0:
                    p.op("vector",
                         lambda e: e.tensor_tensor(
                             v["gcq"][:].rearrange("p (h t) -> p h t", h=8),
                             v["gq"][:].rearrange("p (h t) -> p h t", h=8),
                             v["convBC"][0:64, 1, csl].unsqueeze(1)
                             .broadcast_to([64, 8, 128]),
                             op=ALU.mult),
                         wait=["tv"], inc="tv")
                p.op("vector",
                     lambda e, mtq_h=mtq_h: e.tensor_tensor(
                         mtq_h.rearrange("p (h t) -> p h t", h=8),
                         mtq_h.rearrange("p (h t) -> p h t", h=8),
                         v["cbt"][:].unsqueeze(1).broadcast_to(
                             [128, 8, 128]),
                         op=ALU.mult),
                     wait=["tv"], inc="tv")
                p.op("vector",
                     lambda e, mtq_h=mtq_h, h0=h0: e.tensor_tensor(
                         mtq_h, mtq_h,
                         v["ddiag"][:, h0 * 128:(h0 + 8) * 128], op=ALU.add),
                     wait=["tv"], inc="tv")
                opv = p.mark("tv")
                ph.append((h0, pd, mtq_h, gq_h, bwq_h, diagq_h, opv))
            for hq in range(2):
                h0, pd, mtq_h, gq_h, bwq_h, diagq_h, opv = ph[hq]
                if hq == 1:
                    ymv0 = p.mark("tp")
                    p.op("vector",
                         lambda e, gq_h=gq_h: e.tensor_tensor(
                             v["gcq"][:].rearrange("p (h t) -> p h t", h=8),
                             gq_h.rearrange("p (h t) -> p h t", h=8),
                             v["convBC"][0:64, 1, csl].unsqueeze(1)
                             .broadcast_to([64, 8, 128]),
                             op=ALU.mult),
                         wait=[("tp", ymv0), "tv"], inc="tv")
                    opv = p.mark("tv")
                for half4 in range(2):
                    sev = p.mark("ts")
                    for hh4 in range(4):
                        hh = half4 * 4 + hh4
                        h = h0 + hh
                        ysl = slice(pd + hh * 128, pd + (hh + 1) * 128)
                        xop = v["Xtm"][:, h * CPG + c, :]
                        w0 = ()
                        if hh == 0:
                            w0 = [("tv", opv), "ts"]
                        elif hh4 == 0:
                            w0 = [("ts", sev)]
                        mm(ps[:, ysl], xop,
                           mtq_h[:, hh * 128:(hh + 1) * 128], True, False,
                           wait=w0)
                        mm(ps[:, ysl], hTa[:, h * 128:(h + 1) * 128],
                           v["gcq"][:, hh * 128:(hh + 1) * 128], False, True)
                        ssl = slice(3584 + hh4 * 128, 3584 + (hh4 + 1) * 128)
                        mm(ps[0:64, ssl], bwq_h[:, hh * 64:(hh + 1) * 64],
                           xop, True, False)
                        mm(ps[0:64, ssl],
                           diagq_h[:, hh * 64:(hh + 1) * 64],
                           hTa[:, h * 128:(h + 1) * 128], False, True)
                    smv = p.mark("tp")
                    p.op("scalar",
                         lambda e, h0=h0, half4=half4: e.copy(
                             hTb[:, (h0 + half4 * 4) * 128:
                                 (h0 + half4 * 4 + 4) * 128],
                             ps[0:64, 3584:3584 + 512]),
                         wait=[("tp", smv)], inc="ts")
                ymv = p.mark("tp")
                p.op("scalar",
                     lambda e, h0=h0, pd=pd, csl=csl: e.copy(
                         sb["ystg"][:, h0:h0 + 8, csl],
                         ps[:, pd:pd + 1024].rearrange(
                             "p (h t) -> p h t", h=8)),
                     wait=[("tp", ymv)], inc="ts")

        def inproj_mtile(wT, mofs, msz, evac_fn, xv, rot, extra_wait=()):
            idx = rot["n"] % 2
            pb = PB_MM[idx]
            rot["n"] += 1
            w0 = [("qs", xv), ("ts", rot["mk"][idx])] + list(extra_wait)
            for k in range(8):
                mm(ps[0:msz, pb:pb + G], wT[:, k, mofs:mofs + msz],
                   sb["xnT"][:, k, :], k == 0, k == 7,
                   wait=w0 if k == 0 else ())
            mv_ = p.mark("tp")
            evac_fn(ps[0:msz, pb:pb + G], mv_)
            rot["mk"][idx] = p.mark("ts")

        def pass_A():
            p.op("vector", lambda e: e.memset(sb["epsb"][:], EPS), inc="tv")
            p.op("vector", lambda e: e.memset(sb["oneb"][:], 1.0), inc="tv")
            load_consts()
            wTA = sb["warena"][:, 0:8 * 2192].rearrange(
                "p (k m) -> p k m", k=8)
            bbA = sb["warena"][:, 23552:40960].rearrange(
                "p (t c) -> p t c", t=17)
            wT_build(dr["ipw"][2048:4240, :], 2192, dr["wbfA"], wTA, bb=bbA)
            dir_params("bw")
            wTv = p.mark("qs")
            p.op("vector", lambda e: e.memset(sb["haloX"][:], 0.0),
                 wait=["tv"], inc="tv")
            p.op("vector", lambda e: e.memset(sb["haloBC"][:], 0.0), inc="tv")
            p.op("vector", lambda e: e.memset(sb["hT0"][:], 0.0), inc="tv")
            p.op("vector", lambda e: e.memset(sb["ones16"][:], 1.0), inc="tv")
            mk_st = None
            for i in range(8):
                ld(sb["um"][:], dr["fwt"][i * 128:(i + 1) * 128, :],
                   wait=["tv"], inc="qw")
                p.op("vector", lambda e: e.nop(), wait=["qw"])
                dwl = [("qw", mk_st)] if mk_st is not None else ()
                quantize_tile(sb["xd2"][:, 0, :], rst_is_one=True,
                              dst_wait=dwl)
                qv = p.mark("tv")
                p.op("scalar",
                     lambda e, i=i: e.dma_start(
                         out=dr["wbfF"][i * 128:(i + 1) * 128, :],
                         in_=sb["xd2"][:, 0, :]),
                     wait=[("tv", qv)], inc="qw", dma=True)
                mk_st = p.mark("qw")
            shared["qfv"] = p.mark("qw")
            q_rot = [0, 0]
            rot = {"n": 0, "mk": [0, 0]}

            def pro_pieces(t0):
                pieces = []
                for s_ in range(2):
                    def pa(s_=s_):
                        ld(sb["u2"][:],
                           dr["u"][t0 + s_ * 128:t0 + (s_ + 1) * 128, :],
                           wait=["tv", "qo"], inc="qu", q="gpsimd")
                        ln_tile(s_, False)

                    def pb(s_=s_):
                        xnv = p.mark("tv")
                        tpv = p.mark("tp")
                        p.op("sync",
                             lambda e, s_=s_, xnv=xnv, tpv=tpv: e.dma_start(
                                 out=sb["xnT"][:, :,
                                              s_ * 128:(s_ + 1) * 128],
                                 in_=sb["xn2"][:, s_, :].rearrange(
                                     "p (k f) -> p k f", k=8),
                                 transpose=True),
                             wait=[("tv", xnv), ("tp", tpv), ("qs", wTv)],
                             inc="qs", dma=True)
                        shared["xvA"] = p.mark("qs")
                    pieces.append(pa)
                    pieces.append(pb)
                return pieces

            def pro(t0):
                for f in pro_pieces(t0):
                    f()
                return shared["xvA"]

            prev = {"tv": p.mark("tv"), "ts": p.mark("ts"),
                    "tp": p.mark("tp")}
            xv = pro((NG - 1) * G)
            for gi in range(NG - 1, -1, -1):
                if gi == 2:
                    bbo = sb["warena"][:, 32768:40960].rearrange(
                        "p (t c) -> p t c", t=4)
                    for c0 in range(0, 8, 4):
                        wv0 = p.mark("qw")
                        p.op("gpsimd",
                             lambda e, c0=c0: e.dma_start(
                                 out=bbo,
                                 in_=dr["opw"][c0 * 128:(c0 + 4) * 128, :]
                                 .rearrange("(t p) c -> p t c", p=128)),
                             wait=[("qw", wv0)], inc="qg", dma=True)
                        wv = p.mark("qg")
                        p.op("scalar",
                             lambda e, c0=c0: e.dma_start(
                                 out=dr["wbfO"][c0 * 128:(c0 + 4) * 128, :]
                                 .rearrange("(t p) c -> p t c", p=128),
                                 in_=bbo),
                             wait=[("qg", wv)], inc="qw", dma=True)
                    bbz = sb["warena"][:, 32768:40960].rearrange(
                        "p (t c) -> p t c", t=8)
                    wT_build(dr["ipw"][0:2048, :], 2048, dr["wbfZ"], None, bb=bbz)
                    p.op("scalar", lambda e: e.dma_start(
                        out=sb["nwtile"][:],
                        in_=dr["normw"].rearrange("(k f) -> f k", f=128),
                        allow_slow_non_contiguous=True), inc="qw", dma=True)
                    shared["nv"] = p.mark("qw")
                    wdTl = sb["warena"][:, 32768:40960].rearrange(
                        "p (k m) -> p k m", k=8)
                    p.op("sync",
                         lambda e: e.dma_start(out=wdTl, in_=dr["wbfF"],
                                               transpose=True),
                         wait=[("qw", None)], inc="qs", dma=True)


                def ev(dst3):
                    def f(pap, mv_):
                        p.op("scalar", lambda e: e.copy(dst3, pap),
                             wait=[("tp", mv_)], inc="ts")
                    return f

                ew = [("tv", prev["tv"]), ("qx", None)]
                inproj_mtile(wTA, 2048, 64, ev(sb["stgBC"][0:64, 0, :]),
                             xv, rot, ew)
                inproj_mtile(wTA, 2112, 64, ev(sb["stgBC"][0:64, 1, :]),
                             xv, rot, ew)
                mkBC = p.mark("ts")
                inproj_mtile(wTA, 2176, 16, ev(sb["dtraw"][:]), xv, rot, ew)
                mkX = []
                for i in range(NH):
                    inproj_mtile(wTA, i * 128, 128, ev(sb["stgX"][:, i, :]),
                                 xv, rot, ew)
                    mkX.append(p.mark("ts"))
                stv = p.mark("ts")
                if gi == 0:
                    shared["tpA"] = p.mark("tp")
                def halo_bw():
                    p.op("vector",
                         lambda e: e.tensor_copy(sb["haloX"][:, :, 0:3],
                                                 sb["stgX"][:, :, G - 3:G]),
                         wait=[("ts", stv), "tv"], inc="tv")
                    p.op("vector",
                         lambda e: e.tensor_copy(
                             sb["haloBC"][0:64, :, 0:3],
                             sb["stgBC"][0:64, :, G - 3:G]),
                         wait=["tv"], inc="tv")
                conv_dir("bw", [("ts", stv)], tmk=(mkX, mkBC),
                         halo_fn=halo_bw)
                p.op("gpsimd",
                     lambda e, gi=gi: e.dma_start(
                         out=dr["sp_x"][gi],
                         in_=sb["stgX"][:].rearrange("p i t -> p (i t)")),
                     wait=[("ts", stv)], inc="qx", dma=True)
                p.op("gpsimd",
                     lambda e, gi=gi: e.dma_start(
                         out=dr["sp_bc"][gi],
                         in_=sb["stgBC"][0:64, :, :].rearrange(
                             "p i t -> p (i t)")),
                     inc="qx", dma=True)
                p.op("gpsimd",
                     lambda e, gi=gi: e.dma_start(out=dr["sp_dt"][gi],
                                                  in_=sb["dtraw"][:]),
                     inc="qx", dma=True)
                cvv = p.mark("ts")
                p.op("sync",
                     lambda e: e.dma_start(
                         out=sb["Xtm"][:],
                         in_=sb["convX"][:].rearrange("p i t -> p (i t)")
                         .rearrange("p (m q) -> p m q", q=128),
                         transpose=True),
                     wait=[("ts", cvv), ("tp", prev["tp"])],
                     inc="qs", dma=True)
                p.op("sync",
                     lambda e: e.dma_start(
                         out=sb["Btm"][:],
                         in_=sb["convBC"][0:64, 0, :].rearrange(
                             "p (m q) -> p m q", q=128),
                         transpose=True),
                     inc="qs", dma=True)
                trv = p.mark("qs")
                dt_pipeline("bw", [("ts", stv)], prev)
                tvm = p.mark("tv")
                fills = pro_pieces((gi - 1) * G) if gi > 0 else []

                def ff(fills=fills):
                    if fills:
                        fills.pop(0)()
                for c in range(CPG - 1, -1, -1):
                    p.op("tensor", lambda e: e.nop(),
                         wait=[("qs", trv), ("tv", tvm), "ts"])
                    scan_chunk("bw", c, sb["hT0"][:], sb["hT0"][:], q_rot,
                               tvm, fill_fn=ff)
                while fills:
                    fills.pop(0)()
                if gi > 0:
                    xv = shared["xvA"]
                p.op("vector",
                     lambda e: e.tensor_copy(sb["haloX"][:, :, 3:6],
                                             sb["stgX"][:, :, 0:3]),
                     wait=["ts", "tv", "tg"], inc="tv")
                p.op("vector",
                     lambda e: e.tensor_copy(sb["haloBC"][0:64, :, 3:6],
                                             sb["stgBC"][0:64, :, 0:3]),
                     wait=["tv"], inc="tv")
                yv = p.mark("ts")
                p.op("gpsimd",
                     lambda e, gi=gi: e.dma_start(
                         out=dr["sp_y"][gi],
                         in_=sb["ystg"][:].rearrange("p i t -> p (i t)")),
                     wait=[("ts", yv), "tv"], inc="qy", dma=True)
                prev = {"tv": p.mark("tv"), "ts": p.mark("ts"),
                        "tp": p.mark("tp")}
        def quantize_tile(dst_ap, rst_is_one=False, srcb=None, dst_wait=()):
            v = sb
            srcb = v["um"] if srcb is None else srcb
            rst = v["mv"][:, 3:4]
            for h2 in range(2):
                hsl = slice(h2 * 512, (h2 + 1) * 512)
                p.op("vector",
                     lambda e, hsl=hsl: e.tensor_reduce(
                         v["qsc"][:].unsqueeze(2),
                         srcb[:, hsl].rearrange("p (b f) -> p b f", b=16),
                         axis=mybir.AxisListType.X, op=ALU.max,
                         apply_absolute_value=True),
                     wait=["tv"], inc="tv")
                if rst_is_one:
                    p.op("vector",
                         lambda e: e.tensor_scalar(v["qsc"][:], v["qsc"][:],
                                                   1.0 / 7.5, 1e-30,
                                                   op0=ALU.mult, op1=ALU.max),
                         wait=["tv"], inc="tv")
                else:
                    p.op("vector",
                         lambda e: e.tensor_scalar(v["qsc"][:], v["qsc"][:],
                                                   rst, None, op0=ALU.mult),
                         wait=["tv"], inc="tv")
                    p.op("vector",
                         lambda e: e.tensor_scalar(v["qsc"][:], v["qsc"][:],
                                                   1.0 / 7.5, 1e-30,
                                                   op0=ALU.mult, op1=ALU.max),
                         wait=["tv"], inc="tv")
                p.op("vector",
                     lambda e: e.reciprocal(v["qrcp"][:], v["qsc"][:]),
                     wait=["tv"], inc="tv")
                if rst_is_one:
                    p.op("vector",
                         lambda e: e.tensor_scalar(v["qrcp"][:], v["qrcp"][:],
                                                   8.0, None, op0=ALU.mult),
                         wait=["tv"], inc="tv")
                else:
                    p.op("vector",
                         lambda e: e.tensor_scalar(v["qrcp"][:], v["qrcp"][:],
                                                   rst, 8.0,
                                                   op0=ALU.mult, op1=ALU.mult),
                         wait=["tv"], inc="tv")
                rv = p.mark("tv")
                p.op("vector",
                     lambda e, hsl=hsl: e.tensor_tensor(
                         v["qv8"][:].rearrange("p (b f) -> p b f", b=16),
                         srcb[:, hsl].rearrange("p (b f) -> p b f", b=16),
                         v["qrcp"][:].unsqueeze(2).broadcast_to(
                             [128, 16, 32]),
                         op=ALU.mult),
                     wait=[("tv", rv)], inc="tv")
                p.op("vector",
                     lambda e: e.scalar_tensor_tensor(v["qau"][:], v["qv8"][:],
                                                      -1.0, v["qv8"][:],
                                                      op0=ALU.mult,
                                                      op1=ALU.max),
                     wait=["tv"], inc="tv")
                p.op("vector",
                     lambda e: e.tensor_scalar(v["qm1"][:], v["qau"][:], 16.0,
                                               None, op0=ALU.is_ge),
                     wait=["tv"], inc="tv")
                p.op("vector",
                     lambda e: e.tensor_scalar(v["qm2"][:], v["qau"][:], 32.0,
                                               2.0, op0=ALU.is_ge,
                                               op1=ALU.mult),
                     wait=["tv"], inc="tv")
                p.op("vector",
                     lambda e: e.tensor_tensor(v["qm2"][:], v["qm2"][:],
                                               v["qm1"][:], op=ALU.add),
                     wait=["tv"], inc="tv")
                p.op("vector",
                     lambda e: e.tensor_scalar(v["qm1"][:], v["qm2"][:],
                                               1.0, 12582912.0,
                                               op0=ALU.add, op1=ALU.mult),
                     wait=["tv"], inc="tv")
                p.op("vector",
                     lambda e: e.tensor_tensor(v["qv8"][:], v["qv8"][:],
                                               v["qm1"][:], op=ALU.add),
                     wait=["tv"], inc="tv")
                p.op("vector",
                     lambda e: e.tensor_tensor(v["qv8"][:], v["qv8"][:],
                                               v["qm1"][:], op=ALU.subtract),
                     wait=["tv"], inc="tv")
                p.op("vector",
                     lambda e: e.tensor_scalar(v["qsc2"][:], v["qsc"][:],
                                               0.125, None, op0=ALU.mult),
                     wait=["tv"], inc="tv")
                p.op("vector",
                     lambda e, hsl=hsl: e.tensor_tensor(
                         dst_ap[:, hsl].rearrange("p (b f) -> p b f", b=16),
                         v["qv8"][:].rearrange("p (b f) -> p b f", b=16),
                         v["qsc2"][:].unsqueeze(2).broadcast_to(
                             [128, 16, 32]),
                         op=ALU.mult),
                     wait=(["tv", "qs"] + (list(dst_wait) if h2 == 0
                                           else [])), inc="tv")

        def pass_B():
            wTz = sb["warena"][:, 0:16384].rearrange("p (k m) -> p k m", k=8)
            WoT = sb["warena"][:, 16384:32768].rearrange(
                "p (k m) -> p k m", k=16)
            wdT = sb["warena"][:, 32768:40960].rearrange(
                "p (k m) -> p k m", k=8)
            wT_build(None, 2048, dr["wbfZ"], wTz,
                     xw=[("tp", shared["tpA"])], bounce=False)
            dir_params("fw")
            ov2 = p.mark("qw")
            p.op("sync",
                 lambda e: e.dma_start(out=WoT, in_=dr["wbfO"],
                                       transpose=True),
                 wait=[("qw", ov2), ("tp", shared["tpA"])],
                 inc="qs", dma=True)
            wov = p.mark("qs")
            nv = shared["nv"]
            for k in range(16):
                p.op("vector",
                     lambda e, k=k: e.tensor_scalar(
                         WoT[:, k, :], WoT[:, k, :],
                         sb["nwtile"][:, k:k + 1], None, op0=ALU.mult),
                     wait=[("qs", wov), ("qw", nv), "tv"] if k == 0
                     else ["tv"], inc="tv")
            ld(sb["um"][0:1, :], dr["fb"].unsqueeze(0),
               wait=["tv"], inc="qw")
            fbv = p.mark("qw")
            for half in range(2):
                dst = PB_X if half == 0 else PB_D[0]
                mm(ps[:, dst:dst + 512], sb["ones1f"][:],
                   sb["um"][0:1, half * 512:(half + 1) * 512],
                   True, True,
                   wait=[("qw", fbv), "tv", "ts"] if half == 0 else ())
            fmv = p.mark("tp")
            fb32 = sb["wstage"][:].rearrange("p t c -> p (t c)").bitcast(F32)
            p.op("vector",
                 lambda e: e.tensor_copy(fb32[:, 0:512],
                                         ps[:, PB_X:PB_X + 512]),
                 wait=[("tp", fmv)], inc="tv")
            p.op("vector",
                 lambda e: e.tensor_copy(fb32[:, 512:1024],
                                         ps[:, PB_D[0]:PB_D[0] + 512]),
                 wait=["tv"], inc="tv")
            wdv = p.mark("qs")
            p.op("vector", lambda e: e.memset(sb["haloX"][:], 0.0),
                 wait=["ts"], inc="tv")
            p.op("vector", lambda e: e.memset(sb["haloBC"][:], 0.0), inc="tv")
            p.op("vector", lambda e: e.memset(sb["hT0"][:], 0.0), inc="tv")
            p.op("vector", lambda e: e.memset(sb["ones16"][:], 1.0), inc="tv")
            q_rot = [0, 0]
            rot = {"n": 0, "mk": [0, 0]}
            yzn_ap = sb["Xtm"][:].rearrange("p m q -> p (m q)").rearrange(
                "p (i t) -> p i t", i=NH)

            def pro_pieces(t0):
                pieces = []
                for s_ in range(2):
                    def pa(s_=s_):
                        ld(sb["u2"][:],
                           dr["u"][t0 + s_ * 128:t0 + (s_ + 1) * 128, :],
                           wait=["tv", "qo"], inc="qu", q="gpsimd")
                        ln_tile(s_, True)

                    def pb(s_=s_):
                        xnv = p.mark("tv")
                        tpv = p.mark("tp")
                        p.op("sync",
                             lambda e, s_=s_, xnv=xnv, tpv=tpv: e.dma_start(
                                 out=sb["xnT"][:, :,
                                              s_ * 128:(s_ + 1) * 128],
                                 in_=sb["xn2"][:, s_, :].rearrange(
                                     "p (k f) -> p k f", k=8),
                                 transpose=True),
                             wait=[("tv", xnv), ("tp", tpv), ("qs", wdv)],
                             inc="qs", dma=True)
                        quantize_tile(sb["xd2"][:, s_, :], rst_is_one=False)
                        qv = p.mark("tv")
                        p.op("sync",
                             lambda e, s_=s_, qv=qv: e.dma_start(
                                 out=sb["xdT"][:, :,
                                              s_ * 128:(s_ + 1) * 128],
                                 in_=sb["xd2"][:, s_, :].rearrange(
                                     "p (k f) -> p k f", k=8),
                                 transpose=True),
                             wait=[("tv", qv)], inc="qs", dma=True)
                        shared["xvB"] = p.mark("qs")
                    pieces.append(pa)
                    pieces.append(pb)
                return pieces

            def pro(t0):
                for f in pro_pieces(t0):
                    f()
                return shared["xvB"]

            prev = {"tv": p.mark("tv"), "ts": p.mark("ts"),
                    "tp": p.mark("tp")}
            xv = pro(0)
            for gi in range(NG):
                t0 = gi * G
                ld(sb["stgX"][:].rearrange("p i t -> p (i t)"),
                   dr["sp_x"][gi], wait=[("tv", prev["tv"])], inc="qx",
                   q="gpsimd")
                ld(sb["stgBC"][0:64, :, :].rearrange("p i t -> p (i t)"),
                   dr["sp_bc"][gi], inc="qx", q="gpsimd")
                ld(sb["dtraw"][:], dr["sp_dt"][gi],
                   wait=[("tv", prev["tv"])], inc="qx", q="gpsimd")
                ld(sb["ybw"][:].rearrange("p i t -> p (i t)"),
                   dr["sp_y"][gi], wait=[("tv", prev["tv"])], inc="qy",
                   q="gpsimd")
                spv = p.mark("qx")
                for i in range(NH):
                    idx = rot["n"] % 2
                    pb = PB_MM[idx]
                    rot["n"] += 1
                    w0 = [("qs", xv), ("ts", rot["mk"][idx]),
                          ("tv", prev["tv"])]
                    for k in range(8):
                        mm(ps[:, pb:pb + G], wTz[:, k, i * 128:(i + 1) * 128],
                           sb["xnT"][:, k, :], k == 0, k == 7,
                           wait=w0 if k == 0 else ())
                    zv = p.mark("tp")
                    p.op("scalar",
                         lambda e, i=i, pb=pb: e.activation(
                             sb["siluz"][:, i, :], ps[:, pb:pb + G], AF.Silu),
                         wait=[("tp", zv), "ts"], inc="ts")
                    rot["mk"][idx] = p.mark("ts")
                szv = p.mark("ts")
                ffbanks = [(PB_X, PB_D[0]), (PB_D[1], PB_S)]
                for s_ in range(2):
                    tsl = slice(s_ * 128, (s_ + 1) * 128)
                    for nh_ in range(2):
                        dst = ffbanks[s_][nh_]
                        for k in range(8):
                            w0 = ()
                            if k == 0 and nh_ == 0 and s_ == 0:
                                w0 = [("tv", prev["tv"]), ("ts", prev["ts"])]
                            mm(ps[:, dst:dst + 512], sb["xdT"][:, k, tsl],
                               wdT[:, k, nh_ * 512:(nh_ + 1) * 512],
                               k == 0, k == 7, wait=w0)
                ffv = p.mark("tp")
                p.op("vector",
                     lambda e: e.tensor_copy(sb["haloX"][:, :, 3:6],
                                             sb["stgX"][:, :, 0:3]),
                     wait=[("qx", spv), "tv", "tg"], inc="tv")
                p.op("vector",
                     lambda e: e.tensor_copy(sb["haloBC"][0:64, :, 3:6],
                                             sb["stgBC"][0:64, :, 0:3]),
                     wait=["tv"], inc="tv")
                conv_dir("fw", [("qx", spv)])
                cvv = p.mark("ts")
                p.op("sync",
                     lambda e: e.dma_start(
                         out=sb["Xtm"][:],
                         in_=sb["convX"][:].rearrange("p i t -> p (i t)")
                         .rearrange("p (m q) -> p m q", q=128),
                         transpose=True),
                     wait=[("ts", cvv), ("tp", prev["tp"])],
                     inc="qs", dma=True)
                p.op("sync",
                     lambda e: e.dma_start(
                         out=sb["Btm"][:],
                         in_=sb["convBC"][0:64, 0, :].rearrange(
                             "p (m q) -> p m q", q=128),
                         transpose=True),
                     inc="qs", dma=True)
                trv = p.mark("qs")
                p.op("vector",
                     lambda e: e.tensor_copy(sb["haloX"][:, :, 0:3],
                                             sb["stgX"][:, :, G - 3:G]),
                     wait=[("ts", cvv), "tv"], inc="tv")
                p.op("vector",
                     lambda e: e.tensor_copy(sb["haloBC"][0:64, :, 0:3],
                                             sb["stgBC"][0:64, :, G - 3:G]),
                     wait=["tv"], inc="tv")
                dt_pipeline("fw", [("qx", spv)], prev)
                for s_ in range(2):
                    for nh_ in range(2):
                        dst = ffbanks[s_][nh_]
                        p.op("vector",
                             lambda e, nh_=nh_, dst=dst, s_=s_:
                             e.tensor_tensor(
                                 sb["gff"][:, s_, nh_ * 512:(nh_ + 1) * 512],
                                 ps[:, dst:dst + 512],
                                 fb32[:, nh_ * 512:(nh_ + 1) * 512],
                                 op=ALU.add),
                             wait=[("tp", ffv), "tv"]
                             if (nh_ == 0 and s_ == 0) else ["tv"],
                             inc="tv")
                    gfv = p.mark("tv")
                    p.op("scalar",
                         lambda e, s_=s_: e.activation(sb["gff"][:, s_, :],
                                                       sb["gff"][:, s_, :],
                                                       AF.Silu),
                         wait=[("tv", gfv), "ts"], inc="ts")
                ggv = p.mark("ts")
                tvm = p.mark("tv")
                fills = pro_pieces(t0 + G) if gi < NG - 1 else []

                def ff(fills=fills):
                    if fills:
                        fills.pop(0)()
                for c in range(CPG):
                    p.op("tensor", lambda e: e.nop(),
                         wait=[("qs", trv), ("tv", tvm), "ts"])
                    scan_chunk("fw", c, sb["hT0"][:], sb["hT0"][:], q_rot,
                               tvm, fill_fn=ff)
                while fills:
                    fills.pop(0)()
                if gi < NG - 1:
                    xv = shared["xvB"]
                yfv = p.mark("ts")
                opvs = [None, None]
                urvs = [None, None]

                def emit_gating(s_):
                    tsl = slice(s_ * 128, (s_ + 1) * 128)
                    w1 = [("ts", yfv), "tv"] if s_ == 0 else ["tv"]
                    p.op("vector",
                         lambda e, tsl=tsl: e.tensor_tensor(
                             sb["ystg"][:, :, tsl], sb["ystg"][:, :, tsl],
                             sb["siluz"][:, :, tsl], op=ALU.mult),
                         wait=w1, inc="tv")
                    p.op("vector",
                         lambda e, tsl=tsl: e.tensor_tensor(
                             sb["ybw"][:, :, tsl], sb["ybw"][:, :, tsl],
                             sb["siluz"][:, :, tsl], op=ALU.mult),
                         wait=["tv", "qy"], inc="tv")
                    gvv = p.mark("tv")
                    for br, buf in ((0, "ystg"), (1, "ybw")):
                        p.op("scalar",
                             lambda e, buf=buf, tsl=tsl: e.activation(
                                 yzn_ap[:, :, tsl], sb[buf][:, :, tsl],
                                 AF.Square),
                             wait=[("tv", gvv), "ts"], inc="ts")
                        sqv = p.mark("ts")
                        psl = slice(PB_X + br * G + s_ * 128,
                                    PB_X + br * G + s_ * 128 + 128)
                        for i in range(NH):
                            mm(ps[0:1, psl], sb["ones128b"][:],
                               yzn_ap[:, i, tsl], i == 0, i == NH - 1,
                               wait=[("ts", sqv)] if i == 0 else ())
                        rmv = p.mark("tp")
                        p.op("scalar",
                             lambda e, br=br, tsl=tsl, psl=psl: e.activation(
                                 sb["rrow"][0:1, br, tsl], ps[0:1, psl],
                                 AF.Ln, scale=1.0 / DIN,
                                 bias=sb["epsb"][0:1, 0:1]),
                             wait=[("tp", rmv), "ts"], inc="ts")
                        p.op("scalar",
                             lambda e, br=br, tsl=tsl: e.activation(
                                 sb["rrow"][0:1, br, tsl],
                                 sb["rrow"][0:1, br, tsl],
                                 AF.Exp, scale=-0.5),
                             wait=["ts"], inc="ts")
                    rsv = p.mark("ts")
                    for br in range(2):
                        psl = slice(PB_X + br * G + s_ * 128,
                                    PB_X + br * G + s_ * 128 + 128)
                        mm(ps[:, psl], sb["ones3"][0:1, :],
                           sb["rrow"][0:1, br, tsl], True, True,
                           wait=[("ts", rsv), "tv"] if br == 0 else ())
                    bcv = p.mark("tp")
                    p.op("vector",
                         lambda e, tsl=tsl: e.tensor_copy(
                             sb["rbc"][:, :, tsl],
                             ps[:, PB_X:PB_X + 512].rearrange(
                                 "p (b t) -> p b t", b=2)[:, :, tsl]),
                         wait=[("tp", bcv)], inc="tv")
                    rbv = p.mark("tv")
                    p.op("vector",
                         lambda e, tsl=tsl: e.tensor_tensor(
                             sb["ystg"][:, :, tsl], sb["ystg"][:, :, tsl],
                             sb["rbc"][:, 0:1, tsl].broadcast_to(
                                 [128, NH, 128]),
                             op=ALU.mult),
                         wait=[("tv", rbv)], inc="tv")
                    p.op("vector",
                         lambda e, tsl=tsl: e.tensor_tensor(
                             sb["ybw"][:, :, tsl], sb["ybw"][:, :, tsl],
                             sb["rbc"][:, 1:2, tsl].broadcast_to(
                                 [128, NH, 128]),
                             op=ALU.mult),
                         wait=["tv"], inc="tv")
                    p.op("vector",
                         lambda e, tsl=tsl: e.tensor_tensor(
                             yzn_ap[:, :, tsl], sb["ystg"][:, :, tsl],
                             sb["ybw"][:, :, tsl], op=ALU.add),
                         wait=["tv"], inc="tv")
                    return p.mark("tv")

                def emit_outproj(s_, ysv_s, extra=()):
                    tsl = slice(s_ * 128, (s_ + 1) * 128)
                    for nh_ in range(2):
                        pbo = PB_MM[nh_]
                        for k in range(16):
                            mm(ps[:, pbo:pbo + 512], yzn_ap[:, k, tsl],
                               WoT[:, k, nh_ * 512:(nh_ + 1) * 512],
                               k == 0, k == 15,
                               wait=[("tv", ysv_s), "ts"] + list(extra)
                               if (k == 0 and nh_ == 0) else ())
                    opvs[s_] = p.mark("tp")
                    ld(sb["ur"][:],
                       dr["u"][t0 + s_ * 128:t0 + (s_ + 1) * 128, :],
                       wait=["tv", "qo"], inc="qu", q="gpsimd")
                    urvs[s_] = p.mark("qu")

                u2v = sb["u2"][:].rearrange("p c -> p c")

                def emit_res(s_):
                    rdst = [sb["ostg"][:], sb["u2"][:, 0:512]]
                    for nh_ in range(2):
                        pbo = PB_MM[nh_]
                        dstb = rdst[nh_]
                        p.op("vector",
                             lambda e, nh_=nh_, pbo=pbo, s_=s_, dstb=dstb:
                             e.tensor_tensor(
                                 dstb, ps[:, pbo:pbo + 512],
                                 sb["gff"][:, s_, nh_ * 512:(nh_ + 1) * 512],
                                 op=ALU.mult),
                             wait=[("tp", opvs[s_]), ("ts", ggv), "tv", "qo"]
                             if nh_ == 0 else ["tv", "qu"], inc="tv")
                    mulv = p.mark("tv")
                    for nh_ in range(2):
                        dstb = rdst[nh_]
                        p.op("vector",
                             lambda e, nh_=nh_, dstb=dstb: e.tensor_tensor(
                                 dstb, dstb,
                                 sb["ur"][:, nh_ * 512:(nh_ + 1) * 512],
                                 op=ALU.add),
                             wait=["tv", ("qu", urvs[s_])], inc="tv")
                        odv = p.mark("tv")
                        p.op("scalar",
                             lambda e, t0=t0, s_=s_, nh_=nh_, dstb=dstb:
                             e.dma_start(
                                 out=dr["out"][t0 + s_ * 128:
                                               t0 + (s_ + 1) * 128,
                                               nh_ * 512:(nh_ + 1) * 512],
                                 in_=dstb),
                             wait=[("tv", odv)], inc="qo", dma=True)
                    return mulv

                ysv0 = emit_gating(0)
                emit_outproj(0, ysv0)
                ysv1 = emit_gating(1)
                mulv0 = emit_res(0)
                emit_outproj(1, ysv1, extra=[("tv", mulv0)])
                emit_res(1)
                prev = {"tv": p.mark("tv"), "ts": p.mark("ts"),
                        "tp": p.mark("tp")}
            p.op("scalar", lambda e: e.nop(), wait=["qo", "qw", "qu", "qd"])
            p.op("sync", lambda e: e.nop(), wait=["qo", "qs"])
            p.op("vector", lambda e: e.nop(), wait=["qo"])
            p.op("gpsimd", lambda e: e.nop(), wait=["qx", "qy", "qg"])

        pass_A()
        pass_B()

    return prog


def build_nc(debug=0):
    prog = make_program(debug)
    nc = bass.Bass("TRN2", target_bir_lowering=False)
    prog(nc, None)
    with ExitStack() as st:
        handles = {n: st.enter_context(nc.semaphore(n)) for n in SEM_NAMES}
        p = P(handles)
        block = st.enter_context(nc.Block())

        @block.sync
        def _(eng):
            p.begin("sync", eng)
            prog(nc, p)

        @block.scalar
        def _(eng):
            p.begin("scalar", eng)
            prog(nc, p)

        @block.vector
        def _(eng):
            p.begin("vector", eng)
            prog(nc, p)

        @block.tensor
        def _(eng):
            p.begin("tensor", eng)
            prog(nc, p)

        @block.gpsimd
        def _(eng):
            p.begin("gpsimd", eng)
            prog(nc, p)
    return nc


def kernel(**inputs):
    consts = build_consts()
    nc = build_nc()
    keymap = {
        "u": "L0_attn_u", "ipw": "L0_attn_in_proj_w",
        "cw": "L0_attn_conv_w", "cb": "L0_attn_conv_b",
        "cwb": "L0_attn_conv_w_bw", "cbb": "L0_attn_conv_b_bw",
        "dtb": "L0_attn_dt_bias", "dtbb": "L0_attn_dt_bias_bw",
        "alog": "L0_attn_A_log", "ablog": "L0_attn_A_b_log",
        "dvec": "L0_attn_D", "dbvec": "L0_attn_D_b",
        "normw": "L0_attn_norm_w", "opw": "L0_attn_out_proj_w",
        "fwt": "L0_ffn_weight", "fb": "L0_ffn_bias",
    }
    base = {k: np.ascontiguousarray(np.asarray(inputs[vk], np.float32))
            for k, vk in keymap.items() if k != "u"}
    base.update(consts)
    u = np.ascontiguousarray(np.asarray(inputs["L0_attn_u"], np.float32))
    in_maps = []
    for b in range(8):
        m = dict(base)
        m["u"] = u[b]
        in_maps.append(m)
    res = run_bass_kernel_spmd(nc, in_maps, core_ids=list(range(8)))
    return np.stack([res.results[b]["out"] for b in range(8)])
